# version 05
# baseline (speedup 1.0000x reference)
import contextlib
import numpy as np
import ml_dtypes
import concourse.bass as bass
import concourse.mybir as mybir
from concourse.bass_utils import run_bass_kernel_spmd

F32 = mybir.dt.float32
BF16 = mybir.dt.bfloat16
AF = mybir.ActivationFunctionType
ALU = mybir.AluOpType
AX = mybir.AxisListType

NCORES = 8
B_TOT = 32
S = 2048
D = 1024
NSEQ = B_TOT // NCORES
T = NSEQ * S
NT = T // 128
TPS = S // 128
KC = D // 128
FF = 4096
NEXP = 8


class Sem:

    def __init__(self, nc, name):
        self._cm = nc.semaphore(name)
        self.h = self._cm.__enter__()
        self.n = 0

    def inc(self, instr, k=1):
        instr.then_inc(self.h, k)
        self.n += k
        return instr


class Ctx:
    def __init__(self, nc):
        self.nc = nc
        self.sd = Sem(nc, "sd")
        self.so = Sem(nc, "so")
        self.st = Sem(nc, "st")
        self.sv = Sem(nc, "sv")
        self.sa = Sem(nc, "sa")
        self.pa = Sem(nc, "pa")
        self.pb = Sem(nc, "pb")
        self.sw1 = Sem(nc, "sw1")
        self.sw2 = Sem(nc, "sw2")

    def alloc_consts(self, es):
        nc = self.nc
        self.ident = es(nc.sbuf_tensor("c_ident", [128, 128], BF16))
        self.ident32 = es(nc.sbuf_tensor("c_ident32", [128, 128], F32))
        self.ones_row = es(nc.sbuf_tensor("c_ones", [1, 128], BF16))
        self.ones32 = es(nc.sbuf_tensor("c_ones32", [1, 128], F32))
        self.ltri = es(nc.sbuf_tensor("c_ltri", [128, 128], BF16))
        with nc.Block() as blk:
            @blk.gpsimd
            def _(g):
                g.memset(self.ident[:], 0.0)
                g.affine_select(out=self.ident[:], in_=self.ident[:],
                                compare_op=ALU.not_equal, fill=1.0, base=0,
                                pattern=[[-1, 128]], channel_multiplier=1)
                g.memset(self.ident32[:], 0.0)
                g.affine_select(out=self.ident32[:], in_=self.ident32[:],
                                compare_op=ALU.not_equal, fill=1.0, base=0,
                                pattern=[[-1, 128]], channel_multiplier=1)
                g.memset(self.ones_row[:], 1.0)
                g.memset(self.ones32[:], 1.0)
                g.memset(self.ltri[:], 1.0)
                g.affine_select(out=self.ltri[:], in_=self.ltri[:],
                                compare_op=ALU.is_ge, fill=0.0, base=0,
                                pattern=[[1, 128]], channel_multiplier=-1)


def emit_ln_tile(ctx, x_tile_ap, out_bf, scr, eps=1e-5, out_f32=None,
                 W=1024, blk=None):
    nc = ctx.nc
    sv, sa = ctx.sv, ctx.sa
    mean, t1, rstd, sq = scr["mean"], scr["t1"], scr["rstd"], scr["sq"]
    inv = 1.0 / W
    sv0, sa0 = sv.n, sa.n
    import contextlib as _c
    own = blk is None
    cmgr = nc.Block() if own else _c.nullcontext(blk)
    with cmgr as blk_:
        @blk_.vector
        def _(v):
            sv.inc(v.reduce_sum(mean[:], x_tile_ap, axis=AX.X))
            v.wait_ge(sv.h, sv0 + 1)
            sv.inc(v.tensor_scalar_mul(mean[:], mean[:], -inv))
            v.wait_ge(sa.h, sa0 + 1)
            sv.inc(v.tensor_scalar(rstd[:], t1[:], inv, eps, ALU.mult, ALU.add))
            v.wait_ge(sa.h, sa0 + 2)
            sv.inc(v.reciprocal(rstd[:], t1[:]))
            v.wait_ge(sv.h, sv0 + 4)
            if out_bf is not None:
                sv.inc(v.tensor_scalar(out_bf, x_tile_ap, mean[:], rstd[:],
                                       ALU.add, ALU.mult))
            if out_f32 is not None:
                sv.inc(v.tensor_scalar(out_f32, x_tile_ap, mean[:], rstd[:],
                                       ALU.add, ALU.mult))

        @blk_.scalar
        def _(sc):
            sc.wait_ge(sv.h, sv0 + 2)
            sa.inc(sc.activation(sq[:, 0:W], x_tile_ap, AF.Square,
                                 bias=mean[:], accum_out=t1[:]))
            sc.wait_ge(sv.h, sv0 + 3)
            sa.inc(sc.activation(t1[:], rstd[:], AF.Sqrt))


def emit_transpose_tile(ctx, src_bf, dst_slices, ptr, blk=None,
                        src_ready_sv=None):
    nc = ctx.nc
    st, sv = ctx.st, ctx.sv
    n = len(dst_slices)
    st0, sv0 = st.n, sv.n
    import contextlib as _c
    own = blk is None
    cmgr = nc.Block() if own else _c.nullcontext(blk)
    with cmgr as blk_:
        @blk_.tensor
        def _(t):
            if src_ready_sv is not None:
                t.wait_ge(sv.h, src_ready_sv)
            for k in range(n):
                if k >= 2:
                    t.wait_ge(sv.h, sv0 + k - 1)
                slot = ptr[k % 2][:, 0:128]
                st.inc(t.transpose(slot, src_bf[:, k * 128:(k + 1) * 128],
                                   ctx.ident[:]))

        @blk_.vector
        def _(v):
            for k in range(n):
                v.wait_ge(st.h, st0 + k + 1)
                slot = ptr[k % 2][:, 0:128]
                sv.inc(v.tensor_copy(dst_slices[k], slot))



def emit_s1(ctx, es, resid_in, resid_out, W):
    nc = ctx.nc
    sd, so, st, sv, sa = ctx.sd, ctx.so, ctx.st, ctx.sv, ctx.sa

    x_t = resid_in.rearrange("(n p) d -> n p d", p=128)
    xo_t = resid_out.rearrange("(n p) d -> n p d", p=128)

    wuv = es(nc.sbuf_tensor("s1_wuv", [128, KC * 4224], BF16))
    wout = es(nc.sbuf_tensor("s1_wout", [128, 16 * 1024], BF16))
    buv_row = es(nc.sbuf_tensor("s1_buvr", [1, 4224], BF16))
    buv_col = es(nc.sbuf_tensor("s1_buvc", [128, 33], F32))
    bout_row = es(nc.sbuf_tensor("s1_boutr", [1, 1024], BF16))
    gb = es(nc.sbuf_tensor("s1_gb", [128, 4], F32))
    gbq = es(nc.sbuf_tensor("s1_gbq", [128, 2], F32))
    aTt = es(nc.sbuf_tensor("s1_aTt", [128, KC * 128], BF16))
    aTg = es(nc.sbuf_tensor("s1_aTg", [128, KC * 512], BF16))
    qT = es(nc.sbuf_tensor("s1_qT", [128, S], BF16))
    kT = es(nc.sbuf_tensor("s1_kT", [128, S], BF16))
    uT = es(nc.sbuf_tensor("s1_uT", [128, 16 * 512], BF16))
    sqT = es(nc.sbuf_tensor("s1_sqT", [128, 16 * 512], BF16))
    vband = es(nc.sbuf_tensor("s1_vband", [128, 16 * 512], BF16))
    xt = es(nc.sbuf_tensor("s1_xt", [128, 1024], F32))
    a_bf = es(nc.sbuf_tensor("s1_abf", [128, 1024], BF16))
    vtile = es(nc.sbuf_tensor("s1_vtile", [128, 2048], BF16))
    base_bf = es(nc.sbuf_tensor("s1_basebf", [128, 128], BF16))
    tmp0 = es(nc.sbuf_tensor("s1_tmp0", [128, 512], F32))
    tmp1 = es(nc.sbuf_tensor("s1_tmp1", [128, 512], F32))
    xnew = es(nc.sbuf_tensor("s1_xnew", [128, 1024], F32))
    scr = {
        "mean": es(nc.sbuf_tensor("s1_mean", [128, 1], F32)),
        "t1": es(nc.sbuf_tensor("s1_t1", [128, 1], F32)),
        "rstd": es(nc.sbuf_tensor("s1_rstd", [128, 1], F32)),
        "sq": es(nc.sbuf_tensor("s1_sq", [128, 1024], F32)),
    }

    ph = [es(nc.psum_tensor(f"s1_ph{i}", [128, 512], F32)) for i in range(2)]
    ptr = [es(nc.psum_tensor(f"s1_ptr{i}", [128, 1024], BF16)) for i in range(2)]
    pav = [es(nc.psum_tensor(f"s1_pav{i}", [128, 512], F32)) for i in range(2)]
    po = [es(nc.psum_tensor(f"s1_po{i}", [128, 512], F32)) for i in range(2)]

    v_hbm = nc.dram_tensor("s1_vhbm", [4 * 16 * 128, 512], BF16,
                           kind="Internal").ap()
    v_blk = v_hbm.rearrange("(e k p) n -> e k p n", k=16, p=128)
    v_band_ap = v_hbm.rearrange("(e q) n -> e q n", q=16 * 128)
    aT_hbm = nc.dram_tensor("s1_aThbm", [KC * 128, S], BF16, kind="Internal").ap()
    aT_c = aT_hbm.rearrange("(c p) t -> c p t", p=128)

    wuv_hbm = W["s1_wuv"].rearrange("(c p) n -> c p n", p=128)
    wout_hbm = W["s1_wout"].rearrange("(c p) n -> c p n", p=128)
    sd0 = sd.n
    with nc.Block() as blk:
        @blk.gpsimd
        def _(g):
            for kc in range(KC):
                sd.inc(g.dma_start(wuv[:, kc * 4224:(kc + 1) * 4224], wuv_hbm[kc]), 16)
            for ec in range(16):
                sd.inc(g.dma_start(wout[:, ec * 1024:(ec + 1) * 1024], wout_hbm[ec]), 16)
            sd.inc(g.dma_start(buv_row[:], W["s1_buv"][None, :]), 16)
            sd.inc(g.dma_start(bout_row[:], W["s1_bout"][None, :]), 16)
            sd.inc(g.dma_start(buv_col[:], W["s1_buvc"]), 16)
            sd.inc(g.dma_start(gb[:], W["s1_gbT"]), 16)

        @blk.vector
        def _(v):
            v.wait_ge(sd.h, sd.n)
            sv.inc(v.tensor_scalar_mul(gbq[:], gb[:, 0:2], 2.0 ** -11))

    for s in range(NSEQ):
        for i in range(TPS):
            g_tile = s * TPS + i
            with nc.Block() as blk:
                @blk.sync
                def _(sy):
                    sd.inc(sy.dma_start(xt[:], x_t[g_tile]), 16)

                @blk.vector
                def _(v):
                    v.wait_ge(sd.h, sd.n)
            emit_ln_tile(ctx, xt[:], a_bf[:], scr)
            dsts = [aTt[:, kc * 128:(kc + 1) * 128] for kc in range(KC)]
            emit_transpose_tile(ctx, a_bf, dsts, ptr)
            with nc.Block() as blk:
                @blk.sync
                def _(sy):
                    for kc in range(KC):
                        so.inc(sy.dma_start(aT_c[kc][:, i * 128:(i + 1) * 128],
                                            aTt[:, kc * 128:(kc + 1) * 128]), 16)

            st0, sa0 = st.n, sa.n
            so_prev = so.n
            with nc.Block() as blk:
                @blk.tensor
                def _(t):
                    for j in range(4):
                        pj = ph[j % 2]
                        if j >= 2:
                            t.wait_ge(sa.h, sa0 + j - 1)
                        for kc in range(KC):
                            t.matmul(pj[:], aTt[:, kc * 128:(kc + 1) * 128],
                                     wuv[:, kc * 4224 + 2048 + j * 512: kc * 4224 + 2560 + j * 512],
                                     start=(kc == 0), stop=False)
                        st.inc(t.matmul(pj[:], ctx.ones_row[:],
                                        buv_row[:, 2048 + j * 512: 2560 + j * 512],
                                        start=False, stop=True))
                    t.wait_ge(sa.h, sa0 + 3)
                    for kc in range(KC):
                        t.matmul(ph[0][:, 0:128], aTt[:, kc * 128:(kc + 1) * 128],
                                 wuv[:, kc * 4224 + 4096: kc * 4224 + 4224],
                                 start=(kc == 0), stop=False)
                    st.inc(t.matmul(ph[0][:, 0:128], ctx.ones_row[:],
                                    buv_row[:, 4096:4224], start=False, stop=True))

                @blk.scalar
                def _(sc):
                    sc.wait_ge(so.h, so_prev)
                    for j in range(4):
                        sc.wait_ge(st.h, st0 + j + 1)
                        sa.inc(sc.activation(vtile[:, j * 512:(j + 1) * 512],
                                             ph[j % 2][:], AF.Silu))
                    sc.wait_ge(st.h, st0 + 5)
                    sa.inc(sc.activation(base_bf[:], ph[0][:, 0:128], AF.Silu))

            st1, sv1 = st.n, sv.n
            with nc.Block() as blk:
                @blk.sync
                def _(sy):
                    for eg in range(4):
                        so.inc(sy.dma_start(v_blk[eg, i], vtile[:, eg * 512:(eg + 1) * 512]), 16)

                @blk.tensor
                def _(t):
                    st.inc(t.transpose(ptr[0][:, 0:128], base_bf[:], ctx.ident[:]))

                @blk.vector
                def _(v):
                    v.wait_ge(st.h, st1 + 1)
                    sv.inc(v.tensor_scalar(qT[:, i * 128:(i + 1) * 128], ptr[0][:, 0:128],
                                           gbq[:, 0:1], gbq[:, 1:2], ALU.mult, ALU.add))
                    sv.inc(v.tensor_scalar(kT[:, i * 128:(i + 1) * 128], ptr[0][:, 0:128],
                                           gb[:, 2:3], gb[:, 3:4], ALU.mult, ALU.add))

        for g in range(4):
            tq0 = g * 512
            sda = sd.n
            so_w = so.n
            with nc.Block() as blk:
                @blk.sync
                def _(sy):
                    sy.wait_ge(so.h, so_w)
                    for kc in range(KC):
                        sd.inc(sy.dma_start(aTg[:, kc * 512:(kc + 1) * 512],
                                            aT_c[kc][:, tq0:tq0 + 512]), 16)
            st0, sa0 = st.n, sa.n
            with nc.Block() as blk:
                @blk.tensor
                def _(t):
                    t.wait_ge(sd.h, sda + 16 * KC)
                    for ec in range(16):
                        pj = ph[ec % 2]
                        if ec >= 2:
                            t.wait_ge(sa.h, sa0 + ec - 1)
                        for kc in range(KC):
                            last = t.matmul(pj[:], wuv[:, kc * 4224 + ec * 128: kc * 4224 + ec * 128 + 128],
                                            aTg[:, kc * 512:(kc + 1) * 512],
                                            start=(kc == 0), stop=(kc == KC - 1))
                        st.inc(last)

                @blk.scalar
                def _(sc):
                    for ec in range(16):
                        sc.wait_ge(st.h, st0 + ec + 1)
                        sa.inc(sc.activation(uT[:, ec * 512:(ec + 1) * 512],
                                             ph[ec % 2][:], AF.Silu,
                                             bias=buv_col[:, ec:ec + 1]))

            st0, sv0, sa0 = st.n, sv.n, sa.n
            with nc.Block() as blk:
                @blk.tensor
                def _(t):
                    for tkc in range(16):
                        pj = pav[tkc % 2]
                        if tkc >= 2:
                            t.wait_ge(sv.h, sv0 + tkc - 1)
                        st.inc(t.matmul(pj[:], kT[:, tkc * 128:(tkc + 1) * 128],
                                        qT[:, tq0:tq0 + 512], start=True, stop=True))

                @blk.vector
                def _(v):
                    for tkc in range(16):
                        v.wait_ge(st.h, st0 + tkc + 1)
                        if tkc >= 2:
                            v.wait_ge(sa.h, sa0 + tkc - 1)
                        tm = tmp0 if tkc % 2 == 0 else tmp1
                        sv.inc(v.tensor_scalar_max(tm[:], pav[tkc % 2][:], 0.0))

                @blk.scalar
                def _(sc):
                    for tkc in range(16):
                        sc.wait_ge(sv.h, sv0 + tkc + 1)
                        tm = tmp0 if tkc % 2 == 0 else tmp1
                        sa.inc(sc.activation(sqT[:, tkc * 512:(tkc + 1) * 512],
                                             tm[:], AF.Square))

            for eg in range(4):
                st0, sv0 = st.n, sv.n
                sdv = sd.n
                so_w2 = so.n
                with nc.Block() as blk:
                    @blk.sync
                    def _(sy):
                        sy.wait_ge(so.h, so_w2)
                        sd.inc(sy.dma_start(vband[:], v_band_ap[eg]), 16)

                    @blk.tensor
                    def _(t):
                        t.wait_ge(sd.h, sdv + 16)
                        for ecl in range(4):
                            pj = pav[ecl % 2]
                            if ecl >= 2:
                                t.wait_ge(sv.h, sv0 + ecl - 1)
                            for tkc in range(16):
                                last = t.matmul(pj[:], vband[:, tkc * 512 + ecl * 128: tkc * 512 + ecl * 128 + 128],
                                                sqT[:, tkc * 512:(tkc + 1) * 512],
                                                start=(tkc == 0), stop=(tkc == 15))
                            st.inc(last)

                    @blk.vector
                    def _(v):
                        for ecl in range(4):
                            ec = eg * 4 + ecl
                            v.wait_ge(st.h, st0 + ecl + 1)
                            sv.inc(v.tensor_mul(uT[:, ec * 512:(ec + 1) * 512],
                                                pav[ecl % 2][:],
                                                uT[:, ec * 512:(ec + 1) * 512]))

            for ti in range(4):
                i = g * 4 + ti
                g_tile = s * TPS + i
                col = ti * 128
                st0, sv0 = st.n, sv.n
                sdx = sd.n
                so0 = so.n
                with nc.Block() as blk:
                    @blk.sync
                    def _(sy):
                        sd.inc(sy.dma_start(xt[:], x_t[g_tile]), 16)

                    @blk.tensor
                    def _(t):
                        for half in range(2):
                            pj = po[half]
                            for ec in range(16):
                                t.matmul(pj[:], uT[:, ec * 512 + col: ec * 512 + col + 128],
                                         wout[:, ec * 1024 + half * 512: ec * 1024 + half * 512 + 512],
                                         start=(ec == 0), stop=False)
                            st.inc(t.matmul(pj[:], ctx.ones_row[:],
                                            bout_row[:, half * 512:half * 512 + 512],
                                            start=False, stop=True))

                    @blk.vector
                    def _(v):
                        v.wait_ge(st.h, st0 + 2)
                        v.wait_ge(sd.h, sdx + 16)
                        v.wait_ge(so.h, so0)
                        for half in range(2):
                            sv.inc(v.tensor_add(xnew[:, half * 512:half * 512 + 512],
                                                po[half][:],
                                                xt[:, half * 512:half * 512 + 512]))

                    @blk.gpsimd
                    def _(gp):
                        gp.wait_ge(sv.h, sv.n)
                        so.inc(gp.dma_start(xo_t[g_tile], xnew[:]), 16)

    with nc.Block() as blk:
        @blk.sync
        def _(sy):
            sy.wait_ge(so.h, so.n)





def emit_moe(ctx, es, resid_in, resid_out, W, tag):
    nc = ctx.nc
    sd, so, st, sv, sa = ctx.sd, ctx.so, ctx.st, ctx.sv, ctx.sa
    NG = NT // 4

    x_t = resid_in.rearrange("(n p) d -> n p d", p=128)
    xo_t = resid_out.rearrange("(n p) d -> n p d", p=128)

    HFF = FF // 2
    w1sb = es(nc.sbuf_tensor(f"{tag}_w1", [128, KC * HFF], BF16))
    w2sb = es(nc.sbuf_tensor(f"{tag}_w2", [128, 16 * 1024], BF16))
    hT = [es(nc.sbuf_tensor(f"{tag}_hT{i}", [128, 16 * 512], BF16)) for i in range(2)]
    aTg = [es(nc.sbuf_tensor(f"{tag}_aTg{i}", [128, KC * 512], BF16)) for i in range(2)]
    acc_sb = [es(nc.sbuf_tensor(f"{tag}_acc{i}", [128, 4 * 1024], F32)) for i in range(2)]
    cw_sb = es(nc.sbuf_tensor(f"{tag}_cw", [128, NT * NEXP], F32))
    cwT_bf = es(nc.sbuf_tensor(f"{tag}_cwTbf", [8, 128], BF16))
    b1c = es(nc.sbuf_tensor(f"{tag}_b1c", [128, NEXP * 32], F32))
    b2sb = es(nc.sbuf_tensor(f"{tag}_b2", [8, 1024], BF16))
    rwT32 = es(nc.sbuf_tensor(f"{tag}_rwT", [128, KC * 8], F32))
    rb_row = es(nc.sbuf_tensor(f"{tag}_rb", [1, 8], F32))
    lnwB = es(nc.sbuf_tensor(f"{tag}_lnwB", [128, 1024], F32))
    lnbB = es(nc.sbuf_tensor(f"{tag}_lnbB", [128, 1024], F32))
    xt = es(nc.sbuf_tensor(f"{tag}_xt", [128, 1024], F32))
    a_bf = es(nc.sbuf_tensor(f"{tag}_abf", [128, 1024], BF16))
    a32 = es(nc.sbuf_tensor(f"{tag}_a32", [128, 1024], F32))
    aTt = es(nc.sbuf_tensor(f"{tag}_aTt", [128, KC * 128], BF16))
    aTt32 = es(nc.sbuf_tensor(f"{tag}_aTt32", [128, KC * 128], F32))
    L = es(nc.sbuf_tensor(f"{tag}_L", [128, 8], F32))
    Lm = es(nc.sbuf_tensor(f"{tag}_Lm", [128, 8], F32))
    c1 = es(nc.sbuf_tensor(f"{tag}_c1", [128, 8], F32))
    cs = es(nc.sbuf_tensor(f"{tag}_cs", [128, 8], F32))
    fsts = es(nc.sbuf_tensor(f"{tag}_fst", [128, 8], F32))
    snds = es(nc.sbuf_tensor(f"{tag}_snd", [128, 8], F32))
    m1 = es(nc.sbuf_tensor(f"{tag}_m1", [128, 1], F32))
    m2 = es(nc.sbuf_tensor(f"{tag}_m2", [128, 1], F32))
    p1 = es(nc.sbuf_tensor(f"{tag}_p1", [128, 1], F32))
    p2 = es(nc.sbuf_tensor(f"{tag}_p2", [128, 1], F32))
    nrm = es(nc.sbuf_tensor(f"{tag}_nrm", [128, 1024], F32))
    xnew = es(nc.sbuf_tensor(f"{tag}_xnew", [128, 1024], F32))
    scr = {
        "mean": es(nc.sbuf_tensor(f"{tag}_mean", [128, 1], F32)),
        "t1": es(nc.sbuf_tensor(f"{tag}_t1", [128, 1], F32)),
        "rstd": es(nc.sbuf_tensor(f"{tag}_rstd", [128, 1], F32)),
        "sq": es(nc.sbuf_tensor(f"{tag}_sq", [128, 1024], F32)),
    }

    ph = [es(nc.psum_tensor(f"{tag}_ph{i}", [128, 512], F32)) for i in range(2)]
    py = [es(nc.psum_tensor(f"{tag}_py{i}", [128, 512], F32)) for i in range(2)]
    pt = [es(nc.psum_tensor(f"{tag}_pt{i}", [128, 1024], BF16)) for i in range(2)]

    aT_hbm = nc.dram_tensor(f"{tag}_aThbm", [KC * 128, T], BF16, kind="Internal").ap()
    aT_c = aT_hbm.rearrange("(c p) t -> c p t", p=128)
    acc_hbm = nc.dram_tensor(f"{tag}_acchbm", [T, 1024], F32, kind="Internal").ap()
    acc_t = acc_hbm.rearrange("(n p) d -> n p d", p=128)

    w1T_hbm = W["moe_w1T"]
    w2T_hbm = W["moe_w2T"]

    with nc.Block() as blk:
        @blk.gpsimd
        def _(g):
            sd.inc(g.dma_start(b1c[:], W["moe_b1c"]), 16)
            sd.inc(g.dma_start(b2sb[:], W["moe_b2"]), 16)
            sd.inc(g.dma_start(rwT32[:], W["moe_rwT"]), 16)
            sd.inc(g.dma_start(rb_row[:], W["moe_rb"][None, :]), 16)
            sd.inc(g.dma_start(lnwB[:], W["moe_lnwB"]), 16)
            sd.inc(g.dma_start(lnbB[:], W["moe_lnbB"]), 16)

        @blk.vector
        def _(v):
            v.wait_ge(sd.h, sd.n)

    for t in range(NT):
        so_prev = so.n
        with nc.Block() as blk:
            @blk.sync
            def _(sy):
                sd.inc(sy.dma_start(xt[:], x_t[t]), 16)

            @blk.vector
            def _(v):
                v.wait_ge(sd.h, sd.n)
                v.wait_ge(so.h, so_prev)
        emit_ln_tile(ctx, xt[:], a_bf[:], scr, out_f32=a32[:])
        with nc.Block() as blk:
            @blk.sync
            def _(sy):
                so.inc(sy.dma_start(acc_t[t], a32[:]), 16)
        dsts = [aTt[:, kc * 128:(kc + 1) * 128] for kc in range(KC)]
        emit_transpose_tile(ctx, a_bf, dsts, pt)
        with nc.Block() as blk:
            @blk.sync
            def _(sy):
                for kc in range(KC):
                    so.inc(sy.dma_start(aT_c[kc][:, t * 128:(t + 1) * 128],
                                        aTt[:, kc * 128:(kc + 1) * 128]), 16)
        st0, sv0 = st.n, sv.n
        with nc.Block() as blk:
            @blk.tensor
            def _(t_):
                for kc in range(KC):
                    if kc >= 2:
                        t_.wait_ge(sv.h, sv0 + kc - 1)
                    st.inc(t_.transpose(py[kc % 2][:, 0:128],
                                        a32[:, kc * 128:(kc + 1) * 128],
                                        ctx.ident32[:]))

            @blk.vector
            def _(v):
                for kc in range(KC):
                    v.wait_ge(st.h, st0 + kc + 1)
                    sv.inc(v.tensor_copy(aTt32[:, kc * 128:(kc + 1) * 128],
                                         py[kc % 2][:, 0:128]))
        st0 = st.n
        with nc.Block() as blk:
            @blk.tensor
            def _(t_):
                for kc in range(KC):
                    t_.matmul(ph[0][:, 0:8], aTt32[:, kc * 128:(kc + 1) * 128],
                              rwT32[:, kc * 8:(kc + 1) * 8],
                              start=(kc == 0), stop=False)
                st.inc(t_.matmul(ph[0][:, 0:8], ctx.ones32[:], rb_row[:],
                                 start=False, stop=True))
        sv0, sa0 = sv.n, sa.n
        with nc.Block() as blk:
            @blk.vector
            def _(v):
                v.wait_ge(st.h, st0 + 1)
                sv.inc(v.tensor_copy(L[:], ph[0][:, 0:8]))
                v.wait_ge(sv.h, sv0 + 1)
                sv.inc(v.reduce_max(m1[:], L[:], axis=AX.X))
                v.wait_ge(sv.h, sv0 + 2)
                sv.inc(v.tensor_single_scalar(c1[:], L[:], m1[:], ALU.is_equal))
                v.wait_ge(sv.h, sv0 + 3)
                sv.inc(v.tensor_tensor_scan(cs[:], c1[:], c1[:], 0.0, ALU.add, ALU.bypass))
                v.wait_ge(sv.h, sv0 + 4)
                sv.inc(v.tensor_single_scalar(cs[:], cs[:], 1.0, ALU.is_equal))
                v.wait_ge(sv.h, sv0 + 5)
                sv.inc(v.tensor_mul(fsts[:], cs[:], c1[:]))
                v.wait_ge(sv.h, sv0 + 6)
                sv.inc(v.scalar_tensor_tensor(Lm[:], fsts[:], -1e30, L[:],
                                              ALU.mult, ALU.add))
                v.wait_ge(sv.h, sv0 + 7)
                sv.inc(v.reduce_max(m2[:], Lm[:], axis=AX.X))
                v.wait_ge(sv.h, sv0 + 8)
                sv.inc(v.tensor_single_scalar(c1[:], Lm[:], m2[:], ALU.is_equal))
                v.wait_ge(sv.h, sv0 + 9)
                sv.inc(v.tensor_tensor_scan(cs[:], c1[:], c1[:], 0.0, ALU.add, ALU.bypass))
                v.wait_ge(sv.h, sv0 + 10)
                sv.inc(v.tensor_single_scalar(cs[:], cs[:], 1.0, ALU.is_equal))
                v.wait_ge(sv.h, sv0 + 11)
                sv.inc(v.tensor_mul(snds[:], cs[:], c1[:]))
                sv.inc(v.tensor_sub(m2[:], m1[:], m2[:]))
                v.wait_ge(sa.h, sa0 + 1)
                sv.inc(v.tensor_scalar(p2[:], p1[:], -1.0, 1.0, ALU.mult, ALU.add))
                v.wait_ge(sv.h, sv0 + 14)
                sv.inc(v.tensor_single_scalar(fsts[:], fsts[:], p1[:], ALU.mult))
                sv.inc(v.tensor_single_scalar(snds[:], snds[:], p2[:], ALU.mult))
                v.wait_ge(sv.h, sv0 + 16)
                sv.inc(v.tensor_add(cw_sb[:, t * 8:(t + 1) * 8], fsts[:], snds[:]))

            @blk.scalar
            def _(sc):
                sc.wait_ge(sv.h, sv0 + 13)
                sa.inc(sc.activation(p1[:], m2[:], AF.Sigmoid))

    pa, pb, sw1, sw2 = ctx.pa, ctx.pb, ctx.sw1, ctx.sw2
    acc_store_cnt = [so.n] * NG
    sweeps = [(e, hf) for e in range(NEXP) for hf in range(2)]
    NSW = len(sweeps)
    h_done = {}
    h_done_nc = {}
    y_done = {}
    y_mm = {}
    y_evac = {}
    gelu_done = {}
    g_store_so = {}
    pa0, pb0 = pa.n, pb.n
    sw10, sw20 = sw1.n, sw2.n

    def psem(gi):
        return pa if gi % 2 == 0 else pb

    def load_tgt(gi):
        if gi % 2 == 0:
            return pa, pa0 + 192 * (gi // 2 + 1)
        return pb, pb0 + 192 * ((gi - 1) // 2 + 1)

    def w1_tgt(si):
        return sw10 + 128 * (si + 1)

    def w2_tgt(si):
        return sw20 + 256 * (si + 1)

    sa_m1 = sa.n

    def gelu_val(gi, nc_):
        return sa_m1 + gi * 16 + nc_ + 1

    def emit_group_load(sy, gi):
        sem = psem(gi)
        g = gi % NG
        buf = aTg[gi % 2]
        for kc in range(KC):
            sem.inc(sy.dma_start(buf[:, kc * 512:(kc + 1) * 512],
                                 aT_c[kc][:, g * 512:(g + 1) * 512]), 16)
        for tt in range(4):
            sem.inc(sy.dma_start(acc_sb[gi % 2][:, tt * 1024:(tt + 1) * 1024],
                                 acc_t[g * 4 + tt]), 16)

    so_m0 = so.n
    with nc.Block() as blk:
        @blk.gpsimd
        def _(g):
            e0, hf0 = sweeps[0]
            w1e = w1T_hbm[e0].rearrange("(c p) n -> c p n", p=128)
            for kc in range(KC):
                sw1.inc(g.dma_start(w1sb[:, kc * HFF:(kc + 1) * HFF],
                                    w1e[kc][:, hf0 * HFF:(hf0 + 1) * HFF]), 16)
            w2e = w2T_hbm[e0].rearrange("(c p) n -> c p n", p=128)
            for nc_ in range(16):
                sw2.inc(g.dma_start(w2sb[:, nc_ * 1024:(nc_ + 1) * 1024],
                                    w2e[hf0 * 16 + nc_]), 16)

        @blk.sync
        def _(sy):
            sy.wait_ge(so.h, so_m0)
            emit_group_load(sy, 0)
            emit_group_load(sy, 1)

    for si, (e, hf) in enumerate(sweeps):
        with nc.Block() as blk:
            @blk.tensor
            def _(t_):
                for g in range(NG):
                    gi = si * NG + g
                    buf = aTg[gi % 2]
                    sem, val = load_tgt(gi)
                    t_.wait_ge(sem.h, val)
                    if g == 0:
                        t_.wait_ge(sw1.h, w1_tgt(si))
                    for nc_ in range(16):
                        prev = (gi, nc_ - 2) if nc_ >= 2 else (gi - 1, 14 + nc_)
                        if prev[0] >= 0:
                            t_.wait_ge(sa.h, gelu_val(*prev))
                        for kc in range(KC):
                            last = t_.matmul(ph[nc_ % 2][:],
                                             w1sb[:, kc * HFF + nc_ * 128: kc * HFF + nc_ * 128 + 128],
                                             buf[:, kc * 512:(kc + 1) * 512],
                                             start=(kc == 0), stop=(kc == KC - 1))
                        st.inc(last)
                        h_done_nc[(gi, nc_)] = st.n
                    h_done[gi] = st.n
                    if g == 0:
                        t_.wait_ge(sw2.h, w2_tgt(si))
                    t_.wait_ge(sa.h, gelu_val(gi, 15))
                    idx = 0
                    for tt in range(4):
                        for mg in range(2):
                            pj = py[idx % 2]
                            prev_evac = y_evac.get((gi, idx - 2))
                            if prev_evac is None and idx < 2:
                                prev_evac = y_evac.get((gi - 1, 6 + idx))
                            if prev_evac is not None:
                                t_.wait_ge(sv.h, prev_evac)
                            for nc_ in range(16):
                                last = t_.matmul(pj[:], hT[gi % 2][:, nc_ * 512 + tt * 128: nc_ * 512 + tt * 128 + 128],
                                                 w2sb[:, nc_ * 1024 + mg * 512: nc_ * 1024 + mg * 512 + 512],
                                                 start=(nc_ == 0), stop=(nc_ == 15))
                            st.inc(last)
                            y_mm[(gi, idx)] = st.n
                            idx += 1
                    y_done[gi] = st.n

            @blk.scalar
            def _(sc):
                for g in range(NG):
                    gi = si * NG + g
                    for nc_ in range(16):
                        sc.wait_ge(st.h, h_done_nc[(gi, nc_)])
                        if nc_ == 0 and (gi - 2) in y_done:
                            sc.wait_ge(st.h, y_done[gi - 2])
                        sa.inc(sc.activation(hT[gi % 2][:, nc_ * 512:(nc_ + 1) * 512],
                                             ph[nc_ % 2][:], AF.Gelu,
                                             bias=b1c[:, e * 32 + hf * 16 + nc_: e * 32 + hf * 16 + nc_ + 1]))
                        gelu_done[(gi, nc_)] = sa.n

            @blk.vector
            def _(v):
                for g in range(NG):
                    gi = si * NG + g
                    sem, val = load_tgt(gi)
                    v.wait_ge(sem.h, val)
                    idx = 0
                    for tt in range(4):
                        gt = g * 4 + tt
                        for mg in range(2):
                            v.wait_ge(st.h, y_mm[(gi, idx)])
                            sv.inc(v.scalar_tensor_tensor(
                                acc_sb[gi % 2][:, tt * 1024 + mg * 512: tt * 1024 + mg * 512 + 512],
                                py[idx % 2][:],
                                cw_sb[:, gt * 8 + e: gt * 8 + e + 1],
                                acc_sb[gi % 2][:, tt * 1024 + mg * 512: tt * 1024 + mg * 512 + 512],
                                ALU.mult, ALU.add))
                            y_evac[(gi, idx)] = sv.n
                            idx += 1

            @blk.gpsimd
            def _(gp):
                for g in range(NG):
                    gi = si * NG + g
                    if g == NG - 1 and si + 1 < NSW:
                        e1, hf1 = sweeps[si + 1]
                        gp.wait_ge(st.h, h_done[si * NG + NG - 1])
                        w1e = w1T_hbm[e1].rearrange("(c p) n -> c p n", p=128)
                        for kc in range(KC):
                            sw1.inc(gp.dma_start(w1sb[:, kc * HFF:(kc + 1) * HFF],
                                                 w1e[kc][:, hf1 * HFF:(hf1 + 1) * HFF]), 16)
                    for tt in range(4):
                        gp.wait_ge(sv.h, y_evac[(gi, tt * 2 + 1)])
                        so.inc(gp.dma_start(acc_t[g * 4 + tt],
                                            acc_sb[gi % 2][:, tt * 1024:(tt + 1) * 1024]), 16)
                    acc_store_cnt[g] = so.n
                    g_store_so[gi] = so.n
                if si + 1 < NSW:
                    e1, hf1 = sweeps[si + 1]
                    gp.wait_ge(st.h, y_done[si * NG + NG - 1])
                    w2e = w2T_hbm[e1].rearrange("(c p) n -> c p n", p=128)
                    for nc_ in range(16):
                        sw2.inc(gp.dma_start(w2sb[:, nc_ * 1024:(nc_ + 1) * 1024],
                                             w2e[hf1 * 16 + nc_]), 16)

            @blk.sync
            def _(sy):
                for g in range(2, NG):
                    gi = si * NG + g
                    sy.wait_ge(st.h, h_done[gi - 2])
                    sy.wait_ge(so.h, g_store_so[gi - 2])
                    emit_group_load(sy, gi)
                if si + 1 < NSW:
                    gi0 = (si + 1) * NG
                    sy.wait_ge(st.h, h_done[gi0 - 2])
                    sy.wait_ge(so.h, g_store_so[gi0 - 2])
                    emit_group_load(sy, gi0)
                    sy.wait_ge(st.h, h_done[gi0 - 1])
                    sy.wait_ge(so.h, g_store_so[gi0 - 1])
                    emit_group_load(sy, gi0 + 1)

    for t in range(NT):
        so_prev = so.n
        sdx = sd.n
        st0 = st.n
        with nc.Block() as blk:
            @blk.sync
            def _(sy):
                sy.wait_ge(so.h, acc_store_cnt[t // 4])
                sd.inc(sy.dma_start(xt[:], x_t[t]), 16)
                sd.inc(sy.dma_start(a32[:], acc_t[t]), 16)

            @blk.vector
            def _(v):
                v.wait_ge(sd.h, sd.n)
                v.wait_ge(so.h, so_prev)
                sv.inc(v.tensor_copy(a_bf[:, 0:8], cw_sb[:, t * 8:(t + 1) * 8]))

            @blk.tensor
            def _(t_):
                t_.wait_ge(sv.h, sv.n)
                st.inc(t_.transpose(pt[0][0:8, 0:128], a_bf[:, 0:8], ctx.ident[:]))

        sv0 = sv.n
        with nc.Block() as blk:
            @blk.vector
            def _(v):
                v.wait_ge(st.h, st0 + 1)
                sv.inc(v.tensor_copy(cwT_bf[:], pt[0][0:8, 0:128]))

            @blk.tensor
            def _(t_):
                t_.wait_ge(sv.h, sv0 + 1)
                for mg in range(2):
                    st.inc(t_.matmul(py[mg][:], cwT_bf[:],
                                     b2sb[:, mg * 512:(mg + 1) * 512],
                                     start=True, stop=True))

        sv0 = sv.n
        with nc.Block() as blk:
            @blk.vector
            def _(v):
                v.wait_ge(st.h, st0 + 3)
                for mg in range(2):
                    sv.inc(v.tensor_add(a32[:, mg * 512:(mg + 1) * 512],
                                        a32[:, mg * 512:(mg + 1) * 512], py[mg][:]))
        emit_ln_tile(ctx, a32[:], None, scr, eps=1e-12, out_f32=nrm[:])
        sv0 = sv.n
        with nc.Block() as blk:
            @blk.vector
            def _(v):
                sv.inc(v.tensor_mul(nrm[:], nrm[:], lnwB[:]))
                v.wait_ge(sv.h, sv0 + 1)
                sv.inc(v.tensor_add(nrm[:], nrm[:], lnbB[:]))
                v.wait_ge(sv.h, sv0 + 2)
                sv.inc(v.tensor_add(xnew[:], nrm[:], xt[:]))

            @blk.gpsimd
            def _(gp):
                gp.wait_ge(sv.h, sv.n)
                so.inc(gp.dma_start(xo_t[t], xnew[:]), 16)

    with nc.Block() as blk:
        @blk.sync
        def _(sy):
            sy.wait_ge(so.h, so.n)




def alloc_ln_bufs(ctx, es, tag, sq_alias=None):
    nc = ctx.nc
    return {
        "xt": es(nc.sbuf_tensor(f"{tag}_xt", [128, 1024], F32)),
        "a_bf": es(nc.sbuf_tensor(f"{tag}_abf", [128, 1024], BF16)),
        "aTt": es(nc.sbuf_tensor(f"{tag}_aTt", [128, KC * 128], BF16)),
        "scr": {
            "mean": es(nc.sbuf_tensor(f"{tag}_mean", [128, 1], F32)),
            "t1": es(nc.sbuf_tensor(f"{tag}_t1", [128, 1], F32)),
            "rstd": es(nc.sbuf_tensor(f"{tag}_rstd", [128, 1], F32)),
            "sq": (sq_alias if sq_alias is not None
                   else es(nc.sbuf_tensor(f"{tag}_sq", [128, 1024], F32))),
        },
    }


def emit_lnaT_phase(ctx, bufs, x_t, aT_c, pt, tiles, eps=1e-5, double_ln=False):
    nc = ctx.nc
    sd, so, sv = ctx.sd, ctx.so, ctx.sv
    xt, a_bf, aTt, scr = bufs["xt"], bufs["a_bf"], bufs["aTt"], bufs["scr"]
    for t in tiles:
        so_prev = so.n
        with nc.Block() as blk:
            @blk.sync
            def _(sy):
                sd.inc(sy.dma_start(xt[:], x_t[t]), 16)

            @blk.vector
            def _(v):
                v.wait_ge(sd.h, sd.n)
                v.wait_ge(so.h, so_prev)
        if double_ln:
            emit_ln_tile(ctx, xt[:], None, scr, eps=eps, out_f32=bufs["a_bf2"][:])
            emit_ln_tile(ctx, bufs["a_bf2"][:], a_bf[:], scr, eps=eps)
        else:
            emit_ln_tile(ctx, xt[:], a_bf[:], scr, eps=eps)
        dsts = [aTt[:, kc * 128:(kc + 1) * 128] for kc in range(KC)]
        emit_transpose_tile(ctx, a_bf, dsts, pt)
        with nc.Block() as blk:
            @blk.sync
            def _(sy):
                for kc in range(KC):
                    so.inc(sy.dma_start(aT_c[kc][:, t * 128:(t + 1) * 128],
                                        aTt[:, kc * 128:(kc + 1) * 128]), 16)



def emit_s3(ctx, es, resid_in, resid_out, W):
    nc = ctx.nc
    sd, so, st, sv, sa = ctx.sd, ctx.so, ctx.st, ctx.sv, ctx.sa
    tag = "s3"

    x_t = resid_in.rearrange("(n p) d -> n p d", p=128)
    xo_t = resid_out.rearrange("(n p) d -> n p d", p=128)

    wqkvT = es(nc.sbuf_tensor(f"{tag}_wqkvT", [128, KC * 3072], BF16))
    woutT = es(nc.sbuf_tensor(f"{tag}_woutT", [128, KC * 1024], BF16))
    bout_row = es(nc.sbuf_tensor(f"{tag}_bout", [1, 1024], BF16))
    gb_row = es(nc.sbuf_tensor(f"{tag}_gb", [1, 32], F32))
    qT_hg = es(nc.sbuf_tensor(f"{tag}_qT", [128, 2 * S], BF16))
    kT_hg = es(nc.sbuf_tensor(f"{tag}_kT", [128, 2 * S], BF16))
    v_hg = es(nc.sbuf_tensor(f"{tag}_v", [128, 16 * 256], BF16))
    dotsT = es(nc.sbuf_tensor(f"{tag}_dotsT", [128, 16 * 512], BF16))
    sq_bf = es(nc.sbuf_tensor(f"{tag}_sqbf", [128, 512], BF16))
    attnvT = es(nc.sbuf_tensor(f"{tag}_attnvT", [128, KC * S], BF16))
    rn_b = es(nc.sbuf_tensor(f"{tag}_rnb", [64, 512], F32))
    aTg = es(nc.sbuf_tensor(f"{tag}_aTg", [128, KC * 512], BF16))
    csum_r = es(nc.sbuf_tensor(f"{tag}_csum", [1, 256], F32))
    bc_row = es(nc.sbuf_tensor(f"{tag}_bcr", [1, 256], F32))
    bc_col = es(nc.sbuf_tensor(f"{tag}_bcc", [64, 4], F32))
    nr_row = es(nc.sbuf_tensor(f"{tag}_nr", [1, 512], F32))
    rn_row = es(nc.sbuf_tensor(f"{tag}_rn", [1, 512], F32))
    xnew = es(nc.sbuf_tensor(f"{tag}_xnew", [128, 1024], F32))
    lnb = alloc_ln_bufs(ctx, es, tag)
    xt = lnb["xt"]

    ps = [es(nc.psum_tensor(f"{tag}_ps{i}", [128, 512], F32)) for i in range(2)]
    pn = es(nc.psum_tensor(f"{tag}_pn", [128, 512], F32))
    pdv = [es(nc.psum_tensor(f"{tag}_pdv{i}", [128, 512], F32)) for i in range(2)]
    pt = [es(nc.psum_tensor(f"{tag}_pt{i}", [128, 1024], BF16)) for i in range(2)]
    ones_col = es(nc.sbuf_tensor(f"{tag}_onesc", [128, 1], BF16))

    aT_hbm = nc.dram_tensor(f"{tag}_aThbm", [KC * 128, T], BF16, kind="Internal").ap()
    aT_c = aT_hbm.rearrange("(c p) t -> c p t", p=128)

    wq_hbm = W["s3_wqkvT"].rearrange("(c p) n -> c p n", p=128)
    wo_hbm = W["s3_woutT"].rearrange("(c p) n -> c p n", p=128)
    with nc.Block() as blk:
        @blk.gpsimd
        def _(g):
            for kc in range(KC):
                sd.inc(g.dma_start(wqkvT[:, kc * 3072:(kc + 1) * 3072], wq_hbm[kc]), 16)
                sd.inc(g.dma_start(woutT[:, kc * 1024:(kc + 1) * 1024], wo_hbm[kc]), 16)
            sd.inc(g.dma_start(bout_row[:], W["s3_bout"][None, :]), 16)
            sd.inc(g.dma_start(gb_row[:], W["s3_gb"][None, :]), 16)
            g.memset(ones_col[:], 1.0)

        @blk.vector
        def _(v):
            v.wait_ge(sd.h, sd.n)

    for sq_ in range(NSEQ):
        tiles = range(sq_ * TPS, (sq_ + 1) * TPS)
        emit_lnaT_phase(ctx, lnb, x_t, aT_c, pt, tiles)

        for hg in range(4):
            for tg in range(4):
                sda = sd.n
                so_w = so.n
                with nc.Block() as blk:
                    @blk.sync
                    def _(sy):
                        sy.wait_ge(so.h, so_w)
                        for kc in range(KC):
                            sd.inc(sy.dma_start(aTg[:, kc * 512:(kc + 1) * 512],
                                                aT_c[kc][:, sq_ * S + tg * 512: sq_ * S + tg * 512 + 512]), 16)
                st0, sv0 = st.n, sv.n
                with nc.Block() as blk:
                    @blk.tensor
                    def _(t_):
                        t_.wait_ge(sd.h, sda + 16 * KC)
                        idx = 0
                        for (dst, oc) in [(qT_hg, hg * 2), (qT_hg, hg * 2 + 1),
                                          (kT_hg, 8 + hg * 2), (kT_hg, 8 + hg * 2 + 1)]:
                            pj = ps[idx % 2]
                            if idx >= 2:
                                t_.wait_ge(sv.h, sv0 + idx - 1)
                            for kc in range(KC):
                                last = t_.matmul(pj[:], wqkvT[:, kc * 3072 + oc * 128: kc * 3072 + oc * 128 + 128],
                                                 aTg[:, kc * 512:(kc + 1) * 512],
                                                 start=(kc == 0), stop=(kc == KC - 1))
                            st.inc(last)
                            idx += 1
                        for ti in range(4):
                            pj = pdv[ti % 2]
                            if ti >= 2:
                                t_.wait_ge(sv.h, sv0 + 4 + ti - 1)
                            for kc in range(KC):
                                last = t_.matmul(pj[:, 0:256],
                                                 aTg[:, kc * 512 + ti * 128: kc * 512 + ti * 128 + 128],
                                                 wqkvT[:, kc * 3072 + 2048 + hg * 256: kc * 3072 + 2048 + hg * 256 + 256],
                                                 start=(kc == 0), stop=(kc == KC - 1))
                            st.inc(last)

                    @blk.vector
                    def _(v):
                        idx = 0
                        for (dst, lc) in [(qT_hg, 0), (qT_hg, 1), (kT_hg, 0), (kT_hg, 1)]:
                            v.wait_ge(st.h, st0 + idx + 1)
                            sv.inc(v.tensor_copy(dst[:, lc * S + tg * 512: lc * S + tg * 512 + 512],
                                                 ps[idx % 2][:]))
                            idx += 1
                        for ti in range(4):
                            v.wait_ge(st.h, st0 + 4 + ti + 1)
                            sv.inc(v.tensor_copy(v_hg[:, (tg * 4 + ti) * 256:(tg * 4 + ti) * 256 + 256],
                                                 pdv[ti % 2][:, 0:256]))

            st0, sv0, sa0 = st.n, sv.n, sa.n
            with nc.Block() as blk:
                @blk.tensor
                def _(t_):
                    for tkc in range(16):
                        st.inc(t_.matmul(pn[0:1, 0:256], ones_col[:],
                                         v_hg[:, tkc * 256:(tkc + 1) * 256],
                                         start=(tkc == 0), stop=(tkc == 15)))

                @blk.vector
                def _(v):
                    v.wait_ge(st.h, st0 + 16)
                    sv.inc(v.tensor_copy(csum_r[:], pn[0:1, 0:256]))
                    for lh in range(4):
                        v.wait_ge(sv.h, sv0 + 1 + lh)
                        sv.inc(v.tensor_single_scalar(bc_row[0:1, lh * 64:(lh + 1) * 64],
                                                      csum_r[0:1, lh * 64:(lh + 1) * 64],
                                                      gb_row[0:1, 16 + hg * 4 + lh: 17 + hg * 4 + lh],
                                                      ALU.mult))
            st0, sv0 = st.n, sv.n
            with nc.Block() as blk:
                @blk.tensor
                def _(t_):
                    for lh in range(4):
                        if lh >= 2:
                            t_.wait_ge(sv.h, sv0 + lh - 1)
                        st.inc(t_.matmul(pdv[lh % 2][0:64, 0:1],
                                         bc_row[0:1, lh * 64:(lh + 1) * 64],
                                         ctx.ones32[0:1, 0:1], start=True, stop=True))

                @blk.vector
                def _(v):
                    for lh in range(4):
                        v.wait_ge(st.h, st0 + lh + 1)
                        sv.inc(v.tensor_copy(bc_col[:, lh:lh + 1], pdv[lh % 2][0:64, 0:1]))

            for lh in range(4):
                h = hg * 4 + lh
                lc = lh // 2
                off = (lh % 2) * 64
                for tg in range(4):
                    tq0 = tg * 512
                    st0, sv0, sa0 = st.n, sv.n, sa.n
                    with nc.Block() as blk:
                        @blk.tensor
                        def _(t_):
                            for tkc in range(16):
                                pj = ps[tkc % 2]
                                if tkc >= 2:
                                    t_.wait_ge(sv.h, sv0 + tkc - 1)
                                st.inc(t_.matmul(pj[:], kT_hg[off:off + 64, lc * S + tkc * 128: lc * S + tkc * 128 + 128],
                                                 qT_hg[off:off + 64, lc * S + tq0: lc * S + tq0 + 512],
                                                 start=True, stop=True))
                            for tkc in range(16):
                                t_.wait_ge(sa.h, sa0 + tkc + 1)
                                st.inc(t_.matmul(pn[0:1, 0:512], ones_col[:], sq_bf[:],
                                                 start=(tkc == 0), stop=(tkc == 15)))
                            t_.wait_ge(sv.h, sv0 + 19)
                            st.inc(t_.matmul(ps[0][0:64, 0:512],
                                             ctx.ones32[0:1, 0:64], rn_row[:],
                                             start=True, stop=True))
                            for tkc in range(16):
                                last = t_.matmul(pdv[0][0:64, 0:512],
                                                 v_hg[:, tkc * 256 + lh * 64: tkc * 256 + lh * 64 + 64],
                                                 dotsT[:, tkc * 512:(tkc + 1) * 512],
                                                 start=(tkc == 0), stop=(tkc == 15))
                            st.inc(last)

                        @blk.vector
                        def _(v):
                            for tkc in range(16):
                                v.wait_ge(st.h, st0 + tkc + 1)
                                sv.inc(v.tensor_copy(dotsT[:, tkc * 512:(tkc + 1) * 512],
                                                     ps[tkc % 2][:]))
                            v.wait_ge(sa.h, sa0 + 17)
                            sv.inc(v.tensor_scalar_max(nr_row[:], nr_row[:], 1e-12))
                            v.wait_ge(sv.h, sv0 + 17)
                            sv.inc(v.reciprocal(rn_row[:], nr_row[:]))
                            v.wait_ge(sv.h, sv0 + 18)
                            sv.inc(v.tensor_single_scalar(rn_row[:], rn_row[:],
                                                          gb_row[0:1, h:h + 1], ALU.mult))
                            v.wait_ge(st.h, st0 + 33)
                            sv.inc(v.tensor_copy(rn_b[:], ps[0][0:64, 0:512]))
                            v.wait_ge(st.h, st0 + 34)
                            sv.inc(v.tensor_mul(rn_b[:], pdv[0][0:64, 0:512], rn_b[:]))
                            v.wait_ge(sv.h, sv0 + 21)
                            sv.inc(v.tensor_scalar_add(
                                attnvT[(h % 2) * 64:(h % 2) * 64 + 64,
                                       (h // 2) * S + tq0:(h // 2) * S + tq0 + 512],
                                rn_b[:], bc_col[:, lh:lh + 1]))

                        @blk.scalar
                        def _(sc):
                            for tkc in range(16):
                                sc.wait_ge(sv.h, sv0 + tkc + 1)
                                if tkc >= 1:
                                    sc.wait_ge(st.h, st0 + 16 + tkc)
                                sa.inc(sc.activation(sq_bf[:], dotsT[:, tkc * 512:(tkc + 1) * 512],
                                                     AF.Square))
                            sc.wait_ge(st.h, st0 + 32)
                            sa.inc(sc.activation(nr_row[:], pn[0:1, 0:512], AF.Sqrt))

        for i in range(TPS):
            g_tile = sq_ * TPS + i
            st0, sv0 = st.n, sv.n
            sdx = sd.n
            so0 = so.n
            with nc.Block() as blk:
                @blk.sync
                def _(sy):
                    sd.inc(sy.dma_start(xt[:], x_t[g_tile]), 16)

                @blk.tensor
                def _(t_):
                    for half in range(2):
                        pj = ps[half]
                        for mc in range(KC):
                            t_.matmul(pj[:], attnvT[:, mc * S + i * 128: mc * S + i * 128 + 128],
                                      woutT[:, mc * 1024 + half * 512: mc * 1024 + half * 512 + 512],
                                      start=(mc == 0), stop=False)
                        st.inc(t_.matmul(pj[:], ctx.ones_row[:],
                                         bout_row[:, half * 512:half * 512 + 512],
                                         start=False, stop=True))

                @blk.vector
                def _(v):
                    v.wait_ge(st.h, st0 + 2)
                    v.wait_ge(sd.h, sdx + 16)
                    v.wait_ge(so.h, so0)
                    for half in range(2):
                        sv.inc(v.tensor_add(xnew[:, half * 512:half * 512 + 512],
                                            ps[half][:],
                                            xt[:, half * 512:half * 512 + 512]))

                @blk.gpsimd
                def _(gp):
                    gp.wait_ge(sv.h, sv.n)
                    so.inc(gp.dma_start(xo_t[g_tile], xnew[:]), 16)

    with nc.Block() as blk:
        @blk.sync
        def _(sy):
            sy.wait_ge(so.h, so.n)





def emit_s5(ctx, es, resid_in, resid_out, W):
    nc = ctx.nc
    sd, so, st, sv, sa = ctx.sd, ctx.so, ctx.st, ctx.sv, ctx.sa
    tag = "s5"

    x_t = resid_in.rearrange("(n p) d -> n p d", p=128)
    xo_t = resid_out.rearrange("(n p) d -> n p d", p=128)

    wqkv = es(nc.sbuf_tensor(f"{tag}_wqkv", [128, KC * 3072], BF16))
    wout = es(nc.sbuf_tensor(f"{tag}_wout", [128, KC * 1024], BF16))
    bqc = es(nc.sbuf_tensor(f"{tag}_bqc", [128, 24], F32))
    bv_row = es(nc.sbuf_tensor(f"{tag}_bvr", [1, 1024], BF16))
    bout_row = es(nc.sbuf_tensor(f"{tag}_bout", [1, 1024], BF16))
    qT_hg = es(nc.sbuf_tensor(f"{tag}_qT", [128, 2 * S], BF16))
    kT_hg = es(nc.sbuf_tensor(f"{tag}_kT", [128, 2 * S], BF16))
    v_hg = es(nc.sbuf_tensor(f"{tag}_v", [128, 16 * 256], BF16))
    PT = es(nc.sbuf_tensor(f"{tag}_PT", [128, 16 * 512], BF16))
    attnoT = es(nc.sbuf_tensor(f"{tag}_attnoT", [128, KC * S], BF16))
    f_b = es(nc.sbuf_tensor(f"{tag}_fb", [64, 512], F32))
    aTg = es(nc.sbuf_tensor(f"{tag}_aTg", [128, KC * 512], BF16))
    mxcols = es(nc.sbuf_tensor(f"{tag}_mxc", [128, 16], F32))
    mxa = es(nc.sbuf_tensor(f"{tag}_mxa", [128, 1], F32))
    mrow = es(nc.sbuf_tensor(f"{tag}_mrow", [1, 1], F32))
    negM = es(nc.sbuf_tensor(f"{tag}_negM", [128, 1], F32))
    f_row = es(nc.sbuf_tensor(f"{tag}_frow", [1, 512], F32))
    xnew = es(nc.sbuf_tensor(f"{tag}_xnew", [128, 1024], F32))
    ones_col = es(nc.sbuf_tensor(f"{tag}_onesc", [128, 1], BF16))
    lnb = alloc_ln_bufs(ctx, es, tag)
    xt = lnb["xt"]

    ps = [es(nc.psum_tensor(f"{tag}_ps{i}", [128, 512], F32)) for i in range(2)]
    pn = es(nc.psum_tensor(f"{tag}_pn", [128, 512], F32))
    pdv = [es(nc.psum_tensor(f"{tag}_pdv{i}", [128, 512], F32)) for i in range(2)]
    pt = [es(nc.psum_tensor(f"{tag}_pt{i}", [128, 1024], BF16)) for i in range(2)]

    aT_hbm = nc.dram_tensor(f"{tag}_aThbm", [KC * 128, T], BF16, kind="Internal").ap()
    aT_c = aT_hbm.rearrange("(c p) t -> c p t", p=128)

    wq_hbm = W["s5_wqkv"].rearrange("(c p) n -> c p n", p=128)
    wo_hbm = W["s5_wout"].rearrange("(c p) n -> c p n", p=128)
    with nc.Block() as blk:
        @blk.gpsimd
        def _(g):
            for kc in range(KC):
                sd.inc(g.dma_start(wqkv[:, kc * 3072:(kc + 1) * 3072], wq_hbm[kc]), 16)
                sd.inc(g.dma_start(wout[:, kc * 1024:(kc + 1) * 1024], wo_hbm[kc]), 16)
            sd.inc(g.dma_start(bqc[:], W["s5_bqc"]), 16)
            sd.inc(g.dma_start(bv_row[:], W["s5_bqkv"][None, 2048:3072]), 16)
            sd.inc(g.dma_start(bout_row[:], W["s5_bout"][None, :]), 16)
            g.memset(ones_col[:], 1.0)

        @blk.vector
        def _(v):
            v.wait_ge(sd.h, sd.n)

    for sq_ in range(NSEQ):
        tiles = range(sq_ * TPS, (sq_ + 1) * TPS)
        emit_lnaT_phase(ctx, lnb, x_t, aT_c, pt, tiles)

        for hg in range(4):
            for tg in range(4):
                sda = sd.n
                so_w = so.n
                with nc.Block() as blk:
                    @blk.sync
                    def _(sy):
                        sy.wait_ge(so.h, so_w)
                        for kc in range(KC):
                            sd.inc(sy.dma_start(aTg[:, kc * 512:(kc + 1) * 512],
                                                aT_c[kc][:, sq_ * S + tg * 512: sq_ * S + tg * 512 + 512]), 16)
                st0, sv0, sa0 = st.n, sv.n, sa.n
                with nc.Block() as blk:
                    @blk.tensor
                    def _(t_):
                        t_.wait_ge(sd.h, sda + 16 * KC)
                        idx = 0
                        for oc in [hg * 2, hg * 2 + 1, 8 + hg * 2, 8 + hg * 2 + 1]:
                            pj = ps[idx % 2]
                            if idx >= 2:
                                t_.wait_ge(sv.h, sv0 + idx - 1)
                            for kc in range(KC):
                                last = t_.matmul(pj[:], wqkv[:, kc * 3072 + oc * 128: kc * 3072 + oc * 128 + 128],
                                                 aTg[:, kc * 512:(kc + 1) * 512],
                                                 start=(kc == 0), stop=(kc == KC - 1))
                            st.inc(last)
                            idx += 1
                        for ti in range(4):
                            pj = pdv[ti % 2]
                            if ti >= 2:
                                t_.wait_ge(sv.h, sv0 + 4 + ti - 1)
                            for kc in range(KC):
                                t_.matmul(pj[:, 0:256],
                                          aTg[:, kc * 512 + ti * 128: kc * 512 + ti * 128 + 128],
                                          wqkv[:, kc * 3072 + 2048 + hg * 256: kc * 3072 + 2048 + hg * 256 + 256],
                                          start=(kc == 0), stop=False)
                            st.inc(t_.matmul(pj[:, 0:256], ctx.ones_row[:],
                                             bv_row[:, hg * 256:(hg + 1) * 256],
                                             start=False, stop=True))


                    @blk.vector
                    def _(v):
                        for idx, oc in enumerate([hg * 2, hg * 2 + 1, 8 + hg * 2, 8 + hg * 2 + 1]):
                            v.wait_ge(st.h, st0 + idx + 1)
                            if idx < 2:
                                sv.inc(v.tensor_scalar(qT_hg[:, idx * S + tg * 512: idx * S + tg * 512 + 512],
                                                       ps[idx % 2][:], bqc[:, oc:oc + 1], 0.125,
                                                       ALU.add, ALU.mult))
                            else:
                                sv.inc(v.tensor_scalar_add(kT_hg[:, (idx - 2) * S + tg * 512: (idx - 2) * S + tg * 512 + 512],
                                                           ps[idx % 2][:], bqc[:, oc:oc + 1]))
                        for ti in range(4):
                            v.wait_ge(st.h, st0 + 4 + ti + 1)
                            sv.inc(v.tensor_copy(v_hg[:, (tg * 4 + ti) * 256:(tg * 4 + ti) * 256 + 256],
                                                 pdv[ti % 2][:, 0:256]))

            for lh in range(4):
                h = hg * 4 + lh
                lc = lh // 2
                off = (lh % 2) * 64
                for tg in range(4):
                    tq0 = tg * 512
                    nk = tg * 4 + 4
                    st0, sv0 = st.n, sv.n
                    with nc.Block() as blk:
                        @blk.tensor
                        def _(t_):
                            for tkc in range(nk):
                                pj = ps[tkc % 2]
                                if tkc >= 2:
                                    t_.wait_ge(sv.h, sv0 + tkc - 1)
                                st.inc(t_.matmul(pj[:], kT_hg[off:off + 64, lc * S + tkc * 128: lc * S + tkc * 128 + 128],
                                                 qT_hg[off:off + 64, lc * S + tq0: lc * S + tq0 + 512],
                                                 start=True, stop=True))

                        @blk.vector
                        def _(v):
                            for tkc in range(nk):
                                v.wait_ge(st.h, st0 + tkc + 1)
                                sv.inc(v.reduce_max(mxcols[:, tkc:tkc + 1],
                                                    ps[tkc % 2][:], axis=AX.X))
                            v.wait_ge(sv.h, sv0 + nk)
                            sv.inc(v.reduce_max(mxa[:], mxcols[:, 0:nk], axis=AX.X))
                    st0, sv0 = st.n, sv.n
                    with nc.Block() as blk:
                        @blk.tensor
                        def _(t_):
                            st.inc(t_.transpose(pn[0:1, 0:128], mxa[:], ctx.ident32[:]))

                        @blk.vector
                        def _(v):
                            v.wait_ge(st.h, st0 + 1)
                            sv.inc(v.reduce_max(mrow[:], pn[0:1, 0:128], axis=AX.X))
                            v.wait_ge(sv.h, sv0 + 1)
                            sv.inc(v.tensor_scalar_mul(mrow[:], mrow[:], -1.0))
                    st0, sv0 = st.n, sv.n
                    with nc.Block() as blk:
                        @blk.tensor
                        def _(t_):
                            st.inc(t_.matmul(pn[:, 0:1], ctx.ones32[:], mrow[:],
                                             start=True, stop=True))

                        @blk.vector
                        def _(v):
                            v.wait_ge(st.h, st0 + 1)
                            sv.inc(v.tensor_copy(negM[:], pn[:, 0:1]))
                    st0, sv0, sa0 = st.n, sv.n, sa.n
                    with nc.Block() as blk:
                        @blk.tensor
                        def _(t_):
                            for tkc in range(nk):
                                pj = ps[tkc % 2]
                                if tkc >= 2:
                                    t_.wait_ge(sa.h, sa0 + tkc - 1)
                                st.inc(t_.matmul(pj[:], kT_hg[off:off + 64, lc * S + tkc * 128: lc * S + tkc * 128 + 128],
                                                 qT_hg[off:off + 64, lc * S + tq0: lc * S + tq0 + 512],
                                                 start=True, stop=True))

                        @blk.scalar
                        def _(sc):
                            for tkc in range(nk):
                                sc.wait_ge(st.h, st0 + tkc + 1)
                                sa.inc(sc.activation(PT[:, tkc * 512:(tkc + 1) * 512],
                                                     ps[tkc % 2][:], AF.Exp,
                                                     bias=negM[:]))

                        @blk.vector
                        def _(v):
                            for tkc in range(nk):
                                v.wait_ge(sa.h, sa0 + tkc + 1)
                                t0i = tq0 // 128
                                for j in range(4):
                                    tqt = t0i + j
                                    if tkc == tqt:
                                        sv.inc(v.tensor_mul(PT[:, tkc * 512 + j * 128: tkc * 512 + j * 128 + 128],
                                                            PT[:, tkc * 512 + j * 128: tkc * 512 + j * 128 + 128],
                                                            ctx.ltri[:]))
                                    elif tkc > tqt:
                                        sv.inc(v.memset(PT[:, tkc * 512 + j * 128: tkc * 512 + j * 128 + 128], 0.0))
                    st0, sv0 = st.n, sv.n
                    with nc.Block() as blk:
                        @blk.tensor
                        def _(t_):
                            for tkc in range(nk):
                                st.inc(t_.matmul(pn[0:1, 0:512], ones_col[:],
                                                 PT[:, tkc * 512:(tkc + 1) * 512],
                                                 start=(tkc == 0), stop=(tkc == nk - 1)))
                            for tkc in range(nk):
                                last = t_.matmul(pdv[0][0:64, 0:512],
                                                 v_hg[:, tkc * 256 + lh * 64: tkc * 256 + lh * 64 + 64],
                                                 PT[:, tkc * 512:(tkc + 1) * 512],
                                                 start=(tkc == 0), stop=(tkc == nk - 1))
                            st.inc(last)

                        @blk.vector
                        def _(v):
                            v.wait_ge(st.h, st0 + nk)
                            sv.inc(v.reciprocal(f_row[:], pn[0:1, 0:512]))
                    st0, sv0 = st.n, sv.n
                    with nc.Block() as blk:
                        @blk.tensor
                        def _(t_):
                            st.inc(t_.matmul(ps[0][0:64, 0:512],
                                             ctx.ones32[0:1, 0:64], f_row[:],
                                             start=True, stop=True))

                        @blk.vector
                        def _(v):
                            v.wait_ge(st.h, st0 + 1)
                            sv.inc(v.tensor_copy(f_b[:], ps[0][0:64, 0:512]))
                            v.wait_ge(sv.h, sv0 + 1)
                            sv.inc(v.tensor_mul(
                                attnoT[(h % 2) * 64:(h % 2) * 64 + 64,
                                       (h // 2) * S + tq0:(h // 2) * S + tq0 + 512],
                                pdv[0][0:64, 0:512], f_b[:]))

        for i in range(TPS):
            g_tile = sq_ * TPS + i
            st0, sv0 = st.n, sv.n
            sdx = sd.n
            so0 = so.n
            with nc.Block() as blk:
                @blk.sync
                def _(sy):
                    sd.inc(sy.dma_start(xt[:], x_t[g_tile]), 16)

                @blk.tensor
                def _(t_):
                    for half in range(2):
                        pj = ps[half]
                        for mc in range(KC):
                            t_.matmul(pj[:], attnoT[:, mc * S + i * 128: mc * S + i * 128 + 128],
                                      wout[:, mc * 1024 + half * 512: mc * 1024 + half * 512 + 512],
                                      start=(mc == 0), stop=False)
                        st.inc(t_.matmul(pj[:], ctx.ones_row[:],
                                         bout_row[:, half * 512:half * 512 + 512],
                                         start=False, stop=True))

                @blk.vector
                def _(v):
                    v.wait_ge(st.h, st0 + 2)
                    v.wait_ge(sd.h, sdx + 16)
                    v.wait_ge(so.h, so0)
                    for half in range(2):
                        sv.inc(v.tensor_add(xnew[:, half * 512:half * 512 + 512],
                                            ps[half][:],
                                            xt[:, half * 512:half * 512 + 512]))

                @blk.gpsimd
                def _(gp):
                    gp.wait_ge(sv.h, sv.n)
                    so.inc(gp.dma_start(xo_t[g_tile], xnew[:]), 16)

    with nc.Block() as blk:
        @blk.sync
        def _(sy):
            sy.wait_ge(so.h, so.n)





S6_DEBUG_STAGE = "all"
S6_DEBUG_DUMP = False


def emit_s6(ctx, es, resid_in, resid_out, W, tag, scratch, quant):
    nc = ctx.nc
    sd, so, st, sv, sa = ctx.sd, ctx.so, ctx.st, ctx.sv, ctx.sa
    NG = NT // 4

    x_t = resid_in.rearrange("(n p) d -> n p d", p=128)
    xo_t = resid_out.rearrange("(n p) d -> n p d", p=128)
    t1g_c = scratch["t1g"].rearrange("(c p) n -> c p n", p=128)
    t2g_c = scratch["t2g"].rearrange("(c p) n -> c p n", p=128)

    if quant and S6_DEBUG_STAGE != "compute":
        import contextlib as _ctl
        with _ctl.ExitStack() as qs:
            qe = lambda cm: qs.enter_context(cm)
            wfull = qe(nc.sbuf_tensor(f"{tag}_wfull", [128, KC * FF], F32))
            tabs = qe(nc.sbuf_tensor(f"{tag}_tabs", [128, FF], F32))
            tsgn = qe(nc.sbuf_tensor(f"{tag}_tsgn", [128, FF], BF16))
            traw = qe(nc.sbuf_tensor(f"{tag}_traw", [128, FF], BF16))
            tg_buf = qe(nc.sbuf_tensor(f"{tag}_tgbuf", [128, FF], BF16))
            sacc = qe(nc.sbuf_tensor(f"{tag}_sacc", [128, 1], F32))
            scol = qe(nc.sbuf_tensor(f"{tag}_scol", [128, 1], F32))
            srow = qe(nc.sbuf_tensor(f"{tag}_srow", [1, 4], F32))
            thrcol = qe(nc.sbuf_tensor(f"{tag}_thrcol", [128, 1], F32))
            invcol = qe(nc.sbuf_tensor(f"{tag}_invcol", [128, 1], F32))
            g1c = qe(nc.sbuf_tensor(f"{tag}_g1c", [128, 8], F32))
            c1cb = qe(nc.sbuf_tensor(f"{tag}_c1cb", [128, 8], BF16))
            g2s = qe(nc.sbuf_tensor(f"{tag}_g2s", [128, 32], F32))
            c2cb = qe(nc.sbuf_tensor(f"{tag}_c2cb", [128, 32], BF16))
            b1c_s = qe(nc.sbuf_tensor(f"{tag}_b1c", [128, 32], F32))
            b2row = qe(nc.sbuf_tensor(f"{tag}_b2row", [1, 1024], F32))
            rrow = qe(nc.sbuf_tensor(f"{tag}_rrow", [1, FF], F32))
            rcol = qe(nc.sbuf_tensor(f"{tag}_rcol", [128, 32], F32))
            onescb = qe(nc.sbuf_tensor(f"{tag}_onescb", [128, 1], BF16))
            prr = [qe(nc.psum_tensor(f"{tag}_prr{i}", [128, 512], F32)) for i in range(8)]

            with nc.Block() as blk:
                @blk.gpsimd
                def _(g):
                    sd.inc(g.dma_start(g1c[:], W["s6_g1c"]), 16)
                    sd.inc(g.dma_start(c1cb[:], W["s6_b1lc"]), 16)
                    sd.inc(g.dma_start(g2s[:], W["s6_g2c"]), 16)
                    sd.inc(g.dma_start(c2cb[:], W["s6_b2lc"]), 16)
                    sd.inc(g.dma_start(b1c_s[:], W["s6_b1c"]), 16)
                    sd.inc(g.dma_start(b2row[:], W["s6_b2"][None, :]), 16)
                    g.memset(onescb[:], 1.0)

                @blk.vector
                def _(v):
                    v.wait_ge(sd.h, sd.n)

            for wi in range(2):
                nchunks = KC if wi == 0 else 32
                width = FF if wi == 0 else 1024
                whbm = (W["s6_w1T"] if wi == 0 else W["s6_w2T"]).rearrange(
                    "(c p) n -> c p n", p=128)
                nelem = float(4096 * 1024)
                sdw = sd.n
                with nc.Block() as blk:
                    @blk.sync
                    def _(sy):
                        for mc in range(nchunks):
                            sd.inc(sy.dma_start(wfull[:, mc * width:(mc + 1) * width],
                                                whbm[mc]), 16)
                sv0, sa0 = sv.n, sa.n
                with nc.Block() as blk:
                    @blk.vector
                    def _(v):
                        v.wait_ge(sd.h, sdw + 16 * nchunks)
                        for mc in range(nchunks):
                            v.wait_ge(sv.h, sv.n)
                            if mc == 0:
                                sv.inc(v.reduce_sum(sacc[:], wfull[:, 0:width], axis=AX.X,
                                                    apply_absolute_value=True))
                            else:
                                sv.inc(v.reduce_sum(scol[:], wfull[:, mc * width:(mc + 1) * width],
                                                    axis=AX.X, apply_absolute_value=True))
                                v.wait_ge(sv.h, sv.n)
                                sv.inc(v.tensor_add(sacc[:], sacc[:], scol[:]))
                st0, sv0 = st.n, sv.n
                with nc.Block() as blk:
                    @blk.tensor
                    def _(t_):
                        st.inc(t_.transpose(prr[0][0:1, 0:128], sacc[:], ctx.ident32[:]))

                    @blk.vector
                    def _(v):
                        v.wait_ge(st.h, st0 + 1)
                        sv.inc(v.reduce_sum(srow[0:1, 0:1], prr[0][0:1, 0:128], axis=AX.X))
                        v.wait_ge(sv.h, sv0 + 1)
                        sv.inc(v.tensor_scalar(srow[0:1, 0:1], srow[0:1, 0:1],
                                               1.0 / nelem, 1e-5, ALU.mult, ALU.max))
                        v.wait_ge(sv.h, sv0 + 2)
                        sv.inc(v.tensor_scalar_mul(srow[0:1, 2:3], srow[0:1, 0:1], 0.5))
                        sv.inc(v.reciprocal(srow[0:1, 1:2], srow[0:1, 0:1]))
                st0, sv0 = st.n, sv.n
                with nc.Block() as blk:
                    @blk.tensor
                    def _(t_):
                        st.inc(t_.matmul(prr[0][:, 0:1], ctx.ones32[:], srow[0:1, 2:3],
                                         start=True, stop=True))
                        st.inc(t_.matmul(prr[1][:, 0:1], ctx.ones32[:], srow[0:1, 0:1],
                                         start=True, stop=True))

                    @blk.vector
                    def _(v):
                        v.wait_ge(st.h, st0 + 2)
                        sv.inc(v.tensor_copy(thrcol[:], prr[0][:, 0:1]))
                        sv.inc(v.tensor_copy(invcol[:], prr[1][:, 0:1]))
                if wi == 0:
                    with nc.Block() as blk:
                        @blk.sync
                        def _(sy):
                            so.inc(sy.dma_start(scratch["rbs1"], invcol[:]), 16)
                else:
                    sv0 = sv.n
                    with nc.Block() as blk:
                        @blk.vector
                        def _(v):
                            sv.inc(v.tensor_single_scalar(g2s[:], g2s[:], invcol[:], ALU.mult))

                for mc in range(nchunks):
                    sv0, sa0, st0 = sv.n, sa.n, st.n
                    so_prev_q = so.n
                    wsl = wfull[:, mc * width:(mc + 1) * width]
                    with nc.Block() as blk:
                        @blk.scalar
                        def _(sc):
                            sa.inc(sc.activation(tabs[:, 0:width], wsl, AF.Abs))
                            sa.inc(sc.activation(tsgn[:, 0:width], wsl, AF.Sign))

                        @blk.vector
                        def _(v):
                            v.wait_ge(so.h, so_prev_q)
                            v.wait_ge(sa.h, sa0 + 1)
                            sv.inc(v.tensor_single_scalar(tabs[:, 0:width], tabs[:, 0:width],
                                                          thrcol[:], ALU.is_ge))
                            v.wait_ge(sa.h, sa0 + 2)
                            v.wait_ge(sv.h, sv0 + 1)
                            sv.inc(v.tensor_mul(traw[:, 0:width], tabs[:, 0:width],
                                                tsgn[:, 0:width]))
                            v.wait_ge(sv.h, sv0 + 2)
                            if wi == 0:
                                sv.inc(v.tensor_single_scalar(tg_buf[:, 0:width],
                                                              traw[:, 0:width],
                                                              g1c[:, mc:mc + 1], ALU.mult))
                            else:
                                sv.inc(v.tensor_single_scalar(tg_buf[:, 0:width],
                                                              traw[:, 0:width],
                                                              g2s[:, mc:mc + 1], ALU.mult))
                    st0 = st.n
                    with nc.Block() as blk:
                        @blk.sync
                        def _(sy):
                            if wi == 0:
                                so.inc(sy.dma_start(t1g_c[mc], tg_buf[:, 0:width]), 16)
                            else:
                                so.inc(sy.dma_start(t2g_c[mc], tg_buf[:, 0:width]), 16)

                        @blk.tensor
                        def _(t_):
                            nsl = width // 512
                            bcol = c1cb if wi == 0 else c2cb
                            for ns in range(nsl):
                                st.inc(t_.matmul(prr[ns][0:1, 0:512],
                                                 bcol[:, mc:mc + 1],
                                                 traw[:, ns * 512:(ns + 1) * 512],
                                                 start=(mc == 0), stop=(mc == nchunks - 1)))
                            if wi == 1:
                                for oh in range(2):
                                    st.inc(t_.matmul(prr[2 + oh][0:1, 0:512],
                                                     onescb[:],
                                                     tg_buf[:, oh * 512:(oh + 1) * 512],
                                                     start=(mc == 0), stop=(mc == nchunks - 1)))
                if wi == 0:
                    st0, sv0 = st.n, sv.n
                    with nc.Block() as blk:
                        @blk.vector
                        def _(v):
                            for ns in range(8):
                                sv.inc(v.tensor_copy(rrow[0:1, ns * 512:(ns + 1) * 512],
                                                     prr[ns][0:1, 0:512]))
                    st0, sv0 = st.n, sv.n
                    with nc.Block() as blk:
                        @blk.tensor
                        def _(t_):
                            for a in range(32):
                                if a >= 8:
                                    t_.wait_ge(sv.h, sv0 + a - 7)
                                st.inc(t_.matmul(prr[a % 8][:, 0:1],
                                                 rrow[0:1, a * 128:(a + 1) * 128],
                                                 ctx.ones32[0:1, 0:1], start=True, stop=True))

                        @blk.vector
                        def _(v):
                            for a in range(32):
                                v.wait_ge(st.h, st0 + a + 1)
                                sv.inc(v.tensor_copy(rcol[:, a:a + 1], prr[a % 8][:, 0:1]))
                    sv0 = sv.n
                    with nc.Block() as blk:
                        @blk.vector
                        def _(v):
                            sv.inc(v.tensor_single_scalar(rcol[:], rcol[:], invcol[:], ALU.mult))
                            v.wait_ge(sv.h, sv0 + 1)
                            sv.inc(v.tensor_add(rcol[:], rcol[:], b1c_s[:]))

                        @blk.gpsimd
                        def _(gp):
                            gp.wait_ge(sv.h, sv.n)
                            so.inc(gp.dma_start(scratch["bcomb"], rcol[:]), 16)
                else:
                    sv0 = sv.n
                    with nc.Block() as blk:
                        @blk.vector
                        def _(v):
                            for oh in range(2):
                                sv.inc(v.tensor_copy(rrow[0:1, oh * 512:(oh + 1) * 512],
                                                     prr[oh][0:1, 0:512]))
                                sv.inc(v.tensor_copy(rrow[0:1, 1024 + oh * 512:1024 + (oh + 1) * 512],
                                                     prr[2 + oh][0:1, 0:512]))
                            v.wait_ge(sv.h, sv0 + 4)
                            sv.inc(v.tensor_single_scalar(rrow[0:1, 0:1024], rrow[0:1, 0:1024],
                                                          srow[0:1, 0:1], ALU.mult))
                            v.wait_ge(sv.h, sv0 + 5)
                            sv.inc(v.tensor_add(rrow[0:1, 0:1024], rrow[0:1, 0:1024],
                                                b2row[:]))

                        @blk.gpsimd
                        def _(gp):
                            gp.wait_ge(sv.h, sv.n)
                            so.inc(gp.dma_start(scratch["ep"], rrow[0:1, 0:1024]), 16)
                            so.inc(gp.dma_start(scratch["cp"], rrow[0:1, 1024:2048]), 16)
            with nc.Block() as blk:
                @blk.sync
                def _(sy):
                    sy.wait_ge(so.h, so.n)

    if S6_DEBUG_DUMP and quant:
        d_t1g = nc.dram_tensor("dbg_t1g", [KC * 128, FF], BF16, kind="ExternalOutput").ap()
        d_t2g = nc.dram_tensor("dbg_t2g", [32 * 128, 1024], BF16, kind="ExternalOutput").ap()
        d_bcomb = nc.dram_tensor("dbg_bcomb", [128, 32], F32, kind="ExternalOutput").ap()
        d_rbs1 = nc.dram_tensor("dbg_rbs1", [128, 1], F32, kind="ExternalOutput").ap()
        d_cp = nc.dram_tensor("dbg_cp", [1, 1024], BF16, kind="ExternalOutput").ap()
        d_ep = nc.dram_tensor("dbg_ep", [1, 1024], BF16, kind="ExternalOutput").ap()
        with nc.Block() as blk:
            @blk.sync
            def _(sy):
                so.inc(sy.dma_start(d_t1g, scratch["t1g"]), 16)
                so.inc(sy.dma_start(d_t2g, scratch["t2g"]), 16)
                so.inc(sy.dma_start(d_bcomb, scratch["bcomb"]), 16)
                so.inc(sy.dma_start(d_rbs1, scratch["rbs1"]), 16)
                so.inc(sy.dma_start(d_cp, scratch["cp"]), 16)
                so.inc(sy.dma_start(d_ep, scratch["ep"]), 16)
                sy.wait_ge(so.h, so.n)

    if S6_DEBUG_STAGE == "quant":
        return
    t1g = es(nc.sbuf_tensor(f"{tag}_t1g", [128, KC * FF], BF16))
    t2g = es(nc.sbuf_tensor(f"{tag}_t2g", [128, 32 * 1024], BF16))
    hT = es(nc.sbuf_tensor(f"{tag}_hT", [128, 32 * 512], BF16))
    aTg = es(nc.sbuf_tensor(f"{tag}_aTg", [128, KC * 512], BF16))
    sqbf = [es(nc.sbuf_tensor(f"{tag}_sqbf{i}", [128, 512], BF16)) for i in range(2)]
    bcomb = es(nc.sbuf_tensor(f"{tag}_bcomb", [128, 32], F32))
    rbs1 = es(nc.sbuf_tensor(f"{tag}_rbs1", [128, 1], F32))
    cp_bf = es(nc.sbuf_tensor(f"{tag}_cpbf", [1, 1024], BF16))
    ep_bf = es(nc.sbuf_tensor(f"{tag}_epbf", [1, 1024], BF16))
    murow = es(nc.sbuf_tensor(f"{tag}_murow", [1, 512], F32))
    ssrow = es(nc.sbuf_tensor(f"{tag}_ssrow", [1, 512], F32))
    r2row = es(nc.sbuf_tensor(f"{tag}_r2row", [1, 512], F32))
    alrow = es(nc.sbuf_tensor(f"{tag}_alrow", [1, 512], BF16))
    r2col = es(nc.sbuf_tensor(f"{tag}_r2col", [128, 4], F32))
    tmp_ps = es(nc.sbuf_tensor(f"{tag}_tmp", [128, 512], F32))
    xnew = es(nc.sbuf_tensor(f"{tag}_xnew", [128, 1024], F32))
    t1h = [es(nc.sbuf_tensor(f"{tag}_t1h{i}", [128, 512], F32)) for i in range(2)]
    onescb2 = es(nc.sbuf_tensor(f"{tag}_onescb2", [128, 1], BF16))
    lnb = alloc_ln_bufs(ctx, es, tag, sq_alias=xnew)
    lnb["a_bf2"] = es(nc.sbuf_tensor(f"{tag}_abf2", [128, 1024], F32))
    xt = lnb["xt"]

    ph = [es(nc.psum_tensor(f"{tag}_ph{i}", [128, 512], F32)) for i in range(2)]
    py = [es(nc.psum_tensor(f"{tag}_py{i}", [128, 512], F32)) for i in range(2)]
    pn1 = es(nc.psum_tensor(f"{tag}_pn1", [128, 512], F32))
    pn2 = es(nc.psum_tensor(f"{tag}_pn2", [128, 512], F32))
    pt = [es(nc.psum_tensor(f"{tag}_pt{i}", [128, 1024], BF16)) for i in range(2)]

    aT_hbm = nc.dram_tensor(f"{tag}_aThbm", [KC * 128, T], BF16, kind="Internal").ap()
    aT_c = aT_hbm.rearrange("(c p) t -> c p t", p=128)

    with nc.Block() as blk:
        @blk.gpsimd
        def _(g):
            for mc in range(KC):
                sd.inc(g.dma_start(t1g[:, mc * FF:(mc + 1) * FF], t1g_c[mc]), 16)
            for nc_ in range(32):
                sd.inc(g.dma_start(t2g[:, nc_ * 1024:(nc_ + 1) * 1024], t2g_c[nc_]), 16)
            sd.inc(g.dma_start(bcomb[:], scratch["bcomb"]), 16)
            sd.inc(g.dma_start(rbs1[:], scratch["rbs1"]), 16)
            sd.inc(g.dma_start(cp_bf[:], scratch["cp"]), 16)
            sd.inc(g.dma_start(ep_bf[:], scratch["ep"]), 16)
            g.memset(onescb2[:], 1.0)

        @blk.vector
        def _(v):
            v.wait_ge(sd.h, sd.n)

    emit_lnaT_phase(ctx, lnb, x_t, aT_c, pt, range(NT), double_ln=True)

    if S6_DEBUG_STAGE == "c_ln":
        with nc.Block() as blk:
            @blk.sync
            def _(sy):
                sy.wait_ge(so.h, so.n)
        return

    for g in range(NG if S6_DEBUG_STAGE not in ("c_h", "c_stats", "c_y") else 2):
        sda = sd.n
        so_w = so.n
        with nc.Block() as blk:
            @blk.sync
            def _(sy):
                sy.wait_ge(so.h, so_w)
                for kc in range(KC):
                    sd.inc(sy.dma_start(aTg[:, kc * 512:(kc + 1) * 512],
                                        aT_c[kc][:, g * 512:(g + 1) * 512]), 16)

        st0, sv0, sa0 = st.n, sv.n, sa.n
        h_st = {}
        stats_st = {}
        with nc.Block() as blk:
            @blk.tensor
            def _(t_):
                def emit_stats(k):
                    t_.wait_ge(sa.h, sa0 + 2 * k + 1)
                    st.inc(t_.matmul(pn1[0:1, 0:512], onescb2[:],
                                     hT[:, k * 512:(k + 1) * 512],
                                     start=(k == 0), stop=(k == 31)))
                    t_.wait_ge(sa.h, sa0 + 2 * k + 2)
                    st.inc(t_.matmul(pn2[0:1, 0:512], onescb2[:], sqbf[k % 2][:],
                                     start=(k == 0), stop=(k == 31)))
                    stats_st[k] = st.n
                t_.wait_ge(sd.h, sda + 16 * KC)
                for nc_ in range(32):
                    pj = ph[nc_ % 2]
                    if nc_ >= 2:
                        t_.wait_ge(sv.h, sv0 + nc_ - 1)
                    for kc in range(KC):
                        last = t_.matmul(pj[:], t1g[:, kc * FF + nc_ * 128: kc * FF + nc_ * 128 + 128],
                                         aTg[:, kc * 512:(kc + 1) * 512],
                                         start=(kc == 0), stop=(kc == KC - 1))
                    st.inc(last)
                    h_st[nc_] = st.n
                    if nc_ >= 2:
                        emit_stats(nc_ - 2)
                emit_stats(30)
                emit_stats(31)

            @blk.vector
            def _(v):
                for nc_ in range(32):
                    v.wait_ge(st.h, h_st[nc_])
                    if nc_ >= 2:
                        v.wait_ge(sa.h, sa0 + 2 * (nc_ - 2) + 1)
                    sv.inc(v.tensor_scalar(t1h[nc_ % 2][:], ph[nc_ % 2][:], rbs1[:],
                                           bcomb[:, nc_:nc_ + 1], ALU.mult, ALU.add))

            @blk.scalar
            def _(sc):
                for nc_ in range(32):
                    sc.wait_ge(sv.h, sv0 + nc_ + 1)
                    sa.inc(sc.activation(hT[:, nc_ * 512:(nc_ + 1) * 512],
                                         t1h[nc_ % 2][:], AF.Gelu))
                    if nc_ >= 2:
                        sc.wait_ge(st.h, stats_st[nc_ - 2])
                    sa.inc(sc.activation(sqbf[nc_ % 2][:], hT[:, nc_ * 512:(nc_ + 1) * 512],
                                         AF.Square))

        if S6_DEBUG_STAGE == "c_h":
            continue
        sv0, sa0, st0 = sv.n, sa.n, st.n
        with nc.Block() as blk:
            @blk.vector
            def _(v):
                sv.inc(v.tensor_scalar_mul(murow[:], pn1[0:1, 0:512], 1.0 / FF))
                v.wait_ge(sv.h, sv0 + 1)
                sv.inc(v.tensor_mul(ssrow[:], murow[:], murow[:]))
                v.wait_ge(sv.h, sv0 + 2)
                sv.inc(v.scalar_tensor_tensor(ssrow[:], pn2[0:1, 0:512], 1.0 / FF,
                                              ssrow[:], ALU.mult, ALU.subtract))
                v.wait_ge(sv.h, sv0 + 3)
                sv.inc(v.tensor_scalar_add(ssrow[:], ssrow[:], 1e-5))
                v.wait_ge(sa.h, sa0 + 1)
                sv.inc(v.reciprocal(r2row[:], ssrow[:]))
                v.wait_ge(sv.h, sv0 + 5)
                sv.inc(v.tensor_mul(murow[:], murow[:], r2row[:]))
                v.wait_ge(sv.h, sv0 + 6)
                sv.inc(v.tensor_scalar_mul(alrow[:], murow[:], -1.0))

            @blk.scalar
            def _(sc):
                sc.wait_ge(sv.h, sv0 + 4)
                sa.inc(sc.activation(ssrow[:], ssrow[:], AF.Sqrt))
        st0, sv0 = st.n, sv.n
        with nc.Block() as blk:
            @blk.tensor
            def _(t_):
                for ti in range(4):
                    if ti >= 2:
                        t_.wait_ge(sv.h, sv0 + ti - 1)
                    st.inc(t_.matmul(py[ti % 2][:, 0:1],
                                     r2row[0:1, ti * 128:(ti + 1) * 128],
                                     ctx.ones32[0:1, 0:1], start=True, stop=True))

            @blk.vector
            def _(v):
                for ti in range(4):
                    v.wait_ge(st.h, st0 + ti + 1)
                    sv.inc(v.tensor_copy(r2col[:, ti:ti + 1], py[ti % 2][:, 0:1]))

        if S6_DEBUG_STAGE == "c_stats":
            continue
        for ti in range(4):
            gt = g * 4 + ti
            st0, sv0 = st.n, sv.n
            sdx = sd.n
            so0 = so.n
            with nc.Block() as blk:
                @blk.sync
                def _(sy):
                    sd.inc(sy.dma_start(xt[:], x_t[gt]), 16)

                @blk.tensor
                def _(t_):
                    for mg in range(2):
                        pj = py[mg]
                        for nc_ in range(32):
                            last = t_.matmul(pj[:], hT[:, nc_ * 512 + ti * 128: nc_ * 512 + ti * 128 + 128],
                                             t2g[:, nc_ * 1024 + mg * 512: nc_ * 1024 + mg * 512 + 512],
                                             start=(nc_ == 0), stop=(nc_ == 31))
                        st.inc(last)
                        t_.matmul(pn1[:, 0:512], alrow[0:1, ti * 128:(ti + 1) * 128],
                                  cp_bf[:, mg * 512:(mg + 1) * 512],
                                  start=True, stop=False)
                        st.inc(t_.matmul(pn1[:, 0:512], ctx.ones_row[:],
                                         ep_bf[:, mg * 512:(mg + 1) * 512],
                                         start=False, stop=True))
                        t_.wait_ge(sv.h, sv0 + 2 * mg + 1)

                @blk.vector
                def _(v):
                    v.wait_ge(sd.h, sdx + 16)
                    v.wait_ge(so.h, so0)
                    for mg in range(2):
                        v.wait_ge(st.h, st0 + 2 * mg + 2)
                        if mg == 1:
                            v.wait_ge(sv.h, sv0 + 2)
                        sv.inc(v.tensor_add(tmp_ps[:], pn1[:, 0:512],
                                            xt[:, mg * 512:(mg + 1) * 512]))
                        v.wait_ge(sv.h, sv0 + 2 * mg + 1)
                        sv.inc(v.scalar_tensor_tensor(xnew[:, mg * 512:(mg + 1) * 512],
                                                      py[mg][:], r2col[:, ti:ti + 1],
                                                      tmp_ps[:], ALU.mult, ALU.add))

                @blk.gpsimd
                def _(gp):
                    gp.wait_ge(sv.h, sv.n)
                    so.inc(gp.dma_start(xo_t[gt], xnew[:]), 16)

    with nc.Block() as blk:
        @blk.sync
        def _(sy):
            sy.wait_ge(so.h, so.n)





S7_DEBUG_DUMP = False


def emit_s7(ctx, es, resid_in, resid_out, W):
    nc = ctx.nc
    sd, so, st, sv, sa = ctx.sd, ctx.so, ctx.st, ctx.sv, ctx.sa
    tag = "s7"
    dbg = {}
    if S7_DEBUG_DUMP:
        dbg["sT"] = nc.dram_tensor("dbg_sT", [128, 4 * 512], BF16, kind="ExternalOutput").ap()
        dbg["den"] = nc.dram_tensor("dbg_den", [1, 512], F32, kind="ExternalOutput").ap()
        dbg["qband"] = nc.dram_tensor("dbg_qband", [128, KC * 512], BF16, kind="ExternalOutput").ap()
        dbg["kblk"] = nc.dram_tensor("dbg_kblk", [128, KC * 128], BF16, kind="ExternalOutput").ap()
        dbg["numT"] = nc.dram_tensor("dbg_numT", [128, KC * 512], BF16, kind="ExternalOutput").ap()
        dbg["rd"] = nc.dram_tensor("dbg_rd", [1, 512], F32, kind="ExternalOutput").ap()
        dbg["v03"] = nc.dram_tensor("dbg_v03", [4 * 128, 1024], BF16, kind="ExternalOutput").ap()

    x_t = resid_in.rearrange("(n p) d -> n p d", p=128)
    xo_t = resid_out.rearrange("(n p) d -> n p d", p=128)

    wqkv = es(nc.sbuf_tensor(f"{tag}_wqkv", [128, KC * 3072], BF16))
    wgate = es(nc.sbuf_tensor(f"{tag}_wgate", [128, KC * 1024], BF16))
    wproj = es(nc.sbuf_tensor(f"{tag}_wproj", [128, KC * 1024], BF16))
    gc = es(nc.sbuf_tensor(f"{tag}_gc", [128, 8], F32))
    bc_bf = es(nc.sbuf_tensor(f"{tag}_bcbf", [128, 8], BF16))
    bq_row = es(nc.sbuf_tensor(f"{tag}_bqrow", [1, 3072], F32))
    bg_row = es(nc.sbuf_tensor(f"{tag}_bgrow", [1, 1024], F32))
    bp_row = es(nc.sbuf_tensor(f"{tag}_bprow", [1, 1024], BF16))
    rqg_row = es(nc.sbuf_tensor(f"{tag}_rqg", [1, 4096], F32))
    rqg_bf = es(nc.sbuf_tensor(f"{tag}_rqgbf", [1, 4096], BF16))
    rcols = es(nc.sbuf_tensor(f"{tag}_rcols", [128, 32], F32))
    bprojB = es(nc.sbuf_tensor(f"{tag}_bprojB", [128, 1024], F32))
    aTg = es(nc.sbuf_tensor(f"{tag}_aTg", [128, KC * 512], BF16))
    qband = es(nc.sbuf_tensor(f"{tag}_qband", [128, KC * 512], BF16))
    sT = es(nc.sbuf_tensor(f"{tag}_sT", [128, 16 * 512], BF16))
    numT = es(nc.sbuf_tensor(f"{tag}_numT", [128, KC * 512], BF16))
    kblk = [es(nc.sbuf_tensor(f"{tag}_kblk{i}", [128, KC * 128], BF16)) for i in range(2)]
    vblk = [es(nc.sbuf_tensor(f"{tag}_vblk{i}", [128, 512], BF16)) for i in range(2)]
    gateT = es(nc.sbuf_tensor(f"{tag}_gateT", [128, 512], F32))
    tmp1 = es(nc.sbuf_tensor(f"{tag}_tmp1", [128, 512], F32))
    tmp2 = es(nc.sbuf_tensor(f"{tag}_tmp2", [128, 512], F32))
    tmp3 = es(nc.sbuf_tensor(f"{tag}_tmp3", [128, 512], F32))
    kpbuf = es(nc.sbuf_tensor(f"{tag}_kpbuf", [128, 512], BF16))
    vbuf = es(nc.sbuf_tensor(f"{tag}_vbuf", [128, 1024], BF16))
    den_row = es(nc.sbuf_tensor(f"{tag}_den", [1, 512], F32))
    rd_row = es(nc.sbuf_tensor(f"{tag}_rd", [1, 512], F32))
    rdcol = es(nc.sbuf_tensor(f"{tag}_rdcol", [128, 1], F32))
    xnew = es(nc.sbuf_tensor(f"{tag}_xnew", [128, 1024], F32))
    ones_colb = es(nc.sbuf_tensor(f"{tag}_onescb", [128, 1], BF16))
    lnb = alloc_ln_bufs(ctx, es, tag, sq_alias=xnew)
    lnb["a_bf2"] = es(nc.sbuf_tensor(f"{tag}_abf2", [128, 1024], F32))
    xt = lnb["xt"]

    pfa = [es(nc.psum_tensor(f"{tag}_pf{i}", [128, 512], F32)) for i in range(6)]
    pt = [es(nc.psum_tensor(f"{tag}_pt{i}", [128, 1024], BF16)) for i in range(2)]

    aT_hbm = nc.dram_tensor(f"{tag}_aThbm", [KC * 128, T], BF16, kind="Internal").ap()
    aT_c = aT_hbm.rearrange("(c p) t -> c p t", p=128)
    kpp_hbm = nc.dram_tensor(f"{tag}_kpph", [TPS * KC * 128, 128], BF16, kind="Internal").ap()
    kpp_b = kpp_hbm.rearrange("(t c p) n -> t c p n", c=KC, p=128)
    qp_hbm = nc.dram_tensor(f"{tag}_qph", [KC * 128, S], BF16, kind="Internal").ap()
    qp_c = qp_hbm.rearrange("(c p) t -> c p t", p=128)
    v_hbm = nc.dram_tensor(f"{tag}_vh", [TPS * 128, 1024], BF16, kind="Internal").ap()
    v_tl = v_hbm.rearrange("(t p) n -> t p n", p=128)

    wq_hbm = W["s7_wqkv"].rearrange("(c p) n -> c p n", p=128)
    wg_hbm = W["s7_wgate"].rearrange("(c p) n -> c p n", p=128)
    wp_hbm = W["s7_wproj"].rearrange("(c p) n -> c p n", p=128)

    with nc.Block() as blk:
        @blk.gpsimd
        def _(g):
            for kc in range(KC):
                sd.inc(g.dma_start(wqkv[:, kc * 3072:(kc + 1) * 3072], wq_hbm[kc]), 16)
                sd.inc(g.dma_start(wgate[:, kc * 1024:(kc + 1) * 1024], wg_hbm[kc]), 16)
                sd.inc(g.dma_start(wproj[:, kc * 1024:(kc + 1) * 1024], wp_hbm[kc]), 16)
            sd.inc(g.dma_start(gc[:], W["s7_gc"]), 16)
            sd.inc(g.dma_start(bc_bf[:], W["s7_bc"]), 16)
            sd.inc(g.dma_start(bq_row[:], W["s7_bqkv"][None, :]), 16)
            sd.inc(g.dma_start(bg_row[:], W["s7_bgate"][None, :]), 16)
            sd.inc(g.dma_start(bp_row[:], W["s7_bproj"][None, :]), 16)
            g.memset(ones_colb[:], 1.0)

        @blk.vector
        def _(v):
            v.wait_ge(sd.h, sd.n)

    st0 = st.n
    with nc.Block() as blk:
        @blk.tensor
        def _(t_):
            for ns in range(6):
                for mc in range(KC):
                    last = t_.matmul(pfa[ns][0:1, 0:512], bc_bf[:, mc:mc + 1],
                                     wqkv[:, mc * 3072 + ns * 512: mc * 3072 + (ns + 1) * 512],
                                     start=(mc == 0), stop=(mc == KC - 1))
                st.inc(last)

        @blk.vector
        def _(v):
            for ns in range(6):
                v.wait_ge(st.h, st0 + ns + 1)
                sv.inc(v.tensor_copy(rqg_row[0:1, ns * 512:(ns + 1) * 512],
                                     pfa[ns][0:1, 0:512]))
    st0 = st.n
    with nc.Block() as blk:
        @blk.tensor
        def _(t_):
            for ns in range(2):
                for mc in range(KC):
                    last = t_.matmul(pfa[ns][0:1, 0:512], bc_bf[:, mc:mc + 1],
                                     wgate[:, mc * 1024 + ns * 512: mc * 1024 + (ns + 1) * 512],
                                     start=(mc == 0), stop=(mc == KC - 1))
                st.inc(last)

        @blk.vector
        def _(v):
            for ns in range(2):
                v.wait_ge(st.h, st0 + ns + 1)
                sv.inc(v.tensor_copy(rqg_row[0:1, 3072 + ns * 512: 3072 + (ns + 1) * 512],
                                     pfa[ns][0:1, 0:512]))
    sv0 = sv.n
    with nc.Block() as blk:
        @blk.vector
        def _(v):
            sv.inc(v.tensor_add(rqg_row[0:1, 0:3072], rqg_row[0:1, 0:3072], bq_row[:]))
            sv.inc(v.tensor_add(rqg_row[0:1, 3072:4096], rqg_row[0:1, 3072:4096], bg_row[:]))
            v.wait_ge(sv.h, sv0 + 2)
            sv.inc(v.tensor_copy(rqg_bf[:], rqg_row[:]))
    st0, sv0 = st.n, sv.n
    with nc.Block() as blk:
        @blk.tensor
        def _(t_):
            for a in range(32):
                if a >= 6:
                    t_.wait_ge(sv.h, sv0 + a - 5)
                st.inc(t_.matmul(pfa[a % 6][:, 0:1], rqg_row[0:1, a * 128:(a + 1) * 128],
                                 ctx.ones32[0:1, 0:1], start=True, stop=True))
            t_.wait_ge(sv.h, sv0 + 32)
            st.inc(t_.matmul(pfa[0][:, 0:512], ctx.ones_row[:], bp_row[:, 0:512],
                             start=True, stop=True))
            st.inc(t_.matmul(pfa[1][:, 0:512], ctx.ones_row[:], bp_row[:, 512:1024],
                             start=True, stop=True))

        @blk.vector
        def _(v):
            for a in range(32):
                v.wait_ge(st.h, st0 + a + 1)
                sv.inc(v.tensor_copy(rcols[:, a:a + 1], pfa[a % 6][:, 0:1]))
            v.wait_ge(st.h, st0 + 34)
            sv.inc(v.tensor_copy(bprojB[:, 0:512], pfa[0][:, 0:512]))
            sv.inc(v.tensor_copy(bprojB[:, 512:1024], pfa[1][:, 0:512]))

    sv0 = sv.n
    with nc.Block() as blk:
        @blk.vector
        def _(v):
            for kc in range(KC):
                sv.inc(v.tensor_single_scalar(wqkv[:, kc * 3072:(kc + 1) * 3072],
                                              wqkv[:, kc * 3072:(kc + 1) * 3072],
                                              gc[:, kc:kc + 1], ALU.mult))
                sv.inc(v.tensor_single_scalar(wgate[:, kc * 1024:(kc + 1) * 1024],
                                              wgate[:, kc * 1024:(kc + 1) * 1024],
                                              gc[:, kc:kc + 1], ALU.mult))

    for sq_ in range(NSEQ):
        tiles = range(sq_ * TPS, (sq_ + 1) * TPS)
        emit_lnaT_phase(ctx, lnb, x_t, aT_c, pt, tiles, double_ln=True)

        for tg in range(4):
            sda = sd.n
            so_w = so.n
            with nc.Block() as blk:
                @blk.sync
                def _(sy):
                    sy.wait_ge(so.h, so_w)
                    for kc in range(KC):
                        sd.inc(sy.dma_start(aTg[:, kc * 512:(kc + 1) * 512],
                                            aT_c[kc][:, sq_ * S + tg * 512: sq_ * S + tg * 512 + 512]), 16)
            for oc in range(KC):
                st0, sv0, sa0 = st.n, sv.n, sa.n
                so0 = so.n
                with nc.Block() as blk:
                    @blk.tensor
                    def _(t_):
                        t_.wait_ge(sd.h, sda + 16 * KC)
                        for kc in range(KC):
                            last = t_.matmul(pfa[0][:], wqkv[:, kc * 3072 + oc * 128: kc * 3072 + oc * 128 + 128],
                                             aTg[:, kc * 512:(kc + 1) * 512],
                                             start=(kc == 0), stop=(kc == KC - 1))
                        st.inc(last)

                    @blk.vector
                    def _(v):
                        v.wait_ge(st.h, st0 + 1)
                        v.wait_ge(so.h, so0)
                        sv.inc(v.tensor_scalar(tmp1[:], pfa[0][:], rcols[:, oc:oc + 1],
                                               0.0, ALU.add, ALU.min))
                        sv.inc(v.tensor_scalar(tmp2[:], pfa[0][:], rcols[:, oc:oc + 1],
                                               0.0, ALU.add, ALU.max))

                    @blk.scalar
                    def _(sc):
                        sc.wait_ge(sv.h, sv0 + 1)
                        sa.inc(sc.activation(tmp3[:], tmp1[:], AF.Exp))

                    @blk.gpsimd
                    def _(gp):
                        pass
                sv0b = sv.n
                so_b = so.n
                with nc.Block() as blk:
                    @blk.vector
                    def _(v):
                        v.wait_ge(so.h, so_b)
                        sv.inc(v.tensor_add(kpbuf[:], tmp2[:], tmp3[:]))

                    @blk.sync
                    def _(sy):
                        sy.wait_ge(sv.h, sv0b + 1)
                        so.inc(sy.dma_start(qp_c[oc][:, tg * 512:(tg + 1) * 512], kpbuf[:]), 16)
            for oc in range(KC):
                st0, sv0, sa0 = st.n, sv.n, sa.n
                so0 = so.n
                with nc.Block() as blk:
                    @blk.tensor
                    def _(t_):
                        for kc in range(KC):
                            last = t_.matmul(pfa[1][:], wgate[:, kc * 1024 + oc * 128: kc * 1024 + oc * 128 + 128],
                                             aTg[:, kc * 512:(kc + 1) * 512],
                                             start=(kc == 0), stop=(kc == KC - 1))
                        st.inc(last)
                        for kc in range(KC):
                            last = t_.matmul(pfa[2][:], wqkv[:, kc * 3072 + 1024 + oc * 128: kc * 3072 + 1024 + oc * 128 + 128],
                                             aTg[:, kc * 512:(kc + 1) * 512],
                                             start=(kc == 0), stop=(kc == KC - 1))
                        st.inc(last)

                    @blk.scalar
                    def _(sc):
                        sc.wait_ge(st.h, st0 + 1)
                        sa.inc(sc.activation(gateT[:], pfa[1][:], AF.Sigmoid,
                                             bias=rcols[:, 24 + oc:25 + oc]))

                    @blk.vector
                    def _(v):
                        v.wait_ge(st.h, st0 + 2)
                        v.wait_ge(so.h, so0)
                        sv.inc(v.tensor_scalar_add(tmp1[:], pfa[2][:], rcols[:, 8 + oc:9 + oc]))
                        v.wait_ge(sa.h, sa0 + 1)
                        v.wait_ge(sv.h, sv0 + 1)
                        sv.inc(v.tensor_mul(tmp1[:], tmp1[:], gateT[:]))
                        v.wait_ge(sv.h, sv0 + 2)
                        sv.inc(v.tensor_scalar_min(tmp2[:], tmp1[:], 0.0))
                        sv.inc(v.tensor_relu(tmp1[:], tmp1[:]))

                    @blk.scalar
                    def _(sc2):
                        sc2.wait_ge(sv.h, sv0 + 3)
                        sa.inc(sc2.activation(tmp3[:], tmp2[:], AF.Exp))
                sv0b = sv.n
                so_b = so.n
                with nc.Block() as blk:
                    @blk.vector
                    def _(v):
                        v.wait_ge(so.h, so_b)
                        sv.inc(v.tensor_add(kpbuf[:], tmp1[:], tmp3[:]))

                    @blk.sync
                    def _(sy):
                        sy.wait_ge(sv.h, sv0b + 1)
                        for j in range(4):
                            so.inc(sy.dma_start(kpp_b[tg * 4 + j, oc],
                                                kpbuf[:, j * 128:(j + 1) * 128]), 16)
            for ti in range(4):
                st0, sv0 = st.n, sv.n
                so0 = so.n
                with nc.Block() as blk:
                    @blk.tensor
                    def _(t_):
                        for vh in range(2):
                            for kc in range(KC):
                                t_.matmul(pfa[3 + vh][:],
                                          aTg[:, kc * 512 + ti * 128: kc * 512 + ti * 128 + 128],
                                          wqkv[:, kc * 3072 + 2048 + vh * 512: kc * 3072 + 2048 + vh * 512 + 512],
                                          start=(kc == 0), stop=False)
                            st.inc(t_.matmul(pfa[3 + vh][:], ctx.ones_row[:],
                                             rqg_bf[:, 2048 + vh * 512: 2048 + (vh + 1) * 512],
                                             start=False, stop=True))

                    @blk.vector
                    def _(v):
                        v.wait_ge(st.h, st0 + 2)
                        v.wait_ge(so.h, so0)
                        for vh in range(2):
                            sv.inc(v.tensor_copy(vbuf[:, vh * 512:(vh + 1) * 512],
                                                 pfa[3 + vh][:]))

                    @blk.sync
                    def _(sy):
                        sy.wait_ge(sv.h, sv.n)
                        so.inc(sy.dma_start(v_tl[tg * 4 + ti], vbuf[:]), 16)

        for tg in range(4):
            tq0 = tg * 512
            nk = tg * 4 + 4
            sdq = sd.n
            so_w = so.n
            with nc.Block() as blk:
                @blk.sync
                def _(sy):
                    sy.wait_ge(so.h, so_w)
                    for kc in range(KC):
                        sd.inc(sy.dma_start(qband[:, kc * 512:(kc + 1) * 512],
                                            qp_c[kc][:, tq0:tq0 + 512]), 16)
            st0, sv0 = st.n, sv.n
            sdk0 = sd.n
            copy_done = {}
            with nc.Block() as blk:
                @blk.vector
                def _(v):
                    for tkc in range(nk):
                        v.wait_ge(st.h, st0 + tkc + 1)
                        sv.inc(v.tensor_copy(sT[:, tkc * 512:(tkc + 1) * 512],
                                             pfa[tkc % 2][:]))
                        copy_done[tkc] = sv.n
                        for j in range(4):
                            tqt = tg * 4 + j
                            if tkc == tqt:
                                v.wait_ge(sv.h, sv.n)
                                sv.inc(v.tensor_mul(sT[:, tkc * 512 + j * 128: tkc * 512 + j * 128 + 128],
                                                    sT[:, tkc * 512 + j * 128: tkc * 512 + j * 128 + 128],
                                                    ctx.ltri[:]))
                            elif tkc > tqt:
                                v.wait_ge(sv.h, sv.n)
                                sv.inc(v.memset(sT[:, tkc * 512 + j * 128: tkc * 512 + j * 128 + 128], 0.0))

                @blk.sync
                def _(sy):
                    sy.wait_ge(sd.h, sdq + 16 * KC)
                    for tkc in range(nk):
                        if tkc >= 1:
                            sy.wait_ge(st.h, st0 + tkc)
                        sd.inc(sy.dma_start(kblk[0][:].rearrange("p (c n) -> p c n", n=128),
                                            kpp_b[tkc].rearrange("c p n -> p c n")), 16)

                @blk.tensor
                def _(t_):
                    for tkc in range(nk):
                        t_.wait_ge(sd.h, sdk0 + 16 * (tkc + 1))
                        if tkc >= 2:
                            t_.wait_ge(sv.h, copy_done[tkc - 2])
                        for kc in range(KC):
                            last = t_.matmul(pfa[tkc % 2][:],
                                             kblk[0][:, kc * 128:(kc + 1) * 128],
                                             qband[:, kc * 512:(kc + 1) * 512],
                                             start=(kc == 0), stop=(kc == KC - 1))
                        st.inc(last)

            st0, sv0, sa0 = st.n, sv.n, sa.n
            with nc.Block() as blk:
                @blk.tensor
                def _(t_):
                    for tkc in range(nk):
                        last = t_.matmul(pfa[2][0:1, 0:512], ones_colb[:],
                                         sT[:, tkc * 512:(tkc + 1) * 512],
                                         start=(tkc == 0), stop=(tkc == nk - 1))
                    st.inc(last)

                @blk.vector
                def _(v):
                    v.wait_ge(st.h, st0 + 1)
                    sv.inc(v.tensor_scalar_add(den_row[:], pfa[2][0:1, 0:512], 1e-6))
                    v.wait_ge(sv.h, sv0 + 1)
                    sv.inc(v.reciprocal(rd_row[:], den_row[:]))
            if S7_DEBUG_DUMP and sq_ == 3 and tg == 0:
                with nc.Block() as blk:
                    @blk.sync
                    def _(sy):
                        so.inc(sy.dma_start(dbg["v03"], v_hbm[0:4 * 128, :]), 16)
                        so.inc(sy.dma_start(dbg["sT"], sT[:, 0:4 * 512]), 16)
                        so.inc(sy.dma_start(dbg["den"], den_row[:]), 16)
                        so.inc(sy.dma_start(dbg["qband"], qband[:]), 16)
                        so.inc(sy.dma_start(dbg["kblk"], kblk[(nk - 1) % 2][:]), 16)
                        so.inc(sy.dma_start(dbg["rd"], rd_row[:]), 16)
                        sy.wait_ge(so.h, so.n)
            for half in range(2):
                st0, sv0 = st.n, sv.n
                sdv0 = sd.n
                with nc.Block() as blk:
                    @blk.sync
                    def _(sy):
                        for tkc in range(nk):
                            if tkc >= 1:
                                sy.wait_ge(st.h, st0 + tkc)
                            sd.inc(sy.dma_start(vblk[0][:],
                                                v_tl[tkc][:, half * 512:(half + 1) * 512]), 16)

                    @blk.tensor
                    def _(t_):
                        for tkc in range(nk):
                            t_.wait_ge(sd.h, sdv0 + 16 * (tkc + 1))
                            for dc in range(4):
                                last = t_.matmul(pfa[dc][:],
                                                 vblk[0][:, dc * 128:(dc + 1) * 128],
                                                 sT[:, tkc * 512:(tkc + 1) * 512],
                                                 start=(tkc == 0), stop=(tkc == nk - 1))
                            st.inc(last)

                    @blk.vector
                    def _(v):
                        v.wait_ge(st.h, st0 + nk)
                        for dc in range(4):
                            sv.inc(v.tensor_copy(numT[:, (half * 4 + dc) * 512:(half * 4 + dc) * 512 + 512],
                                                 pfa[dc][:]))
            if S7_DEBUG_DUMP and sq_ == 3 and tg == 0:
                with nc.Block() as blk:
                    @blk.sync
                    def _(sy):
                        so.inc(sy.dma_start(dbg["numT"], numT[:]), 16)
                        sy.wait_ge(so.h, so.n)
            for ti in range(4):
                gt = sq_ * TPS + tg * 4 + ti
                st0, sv0 = st.n, sv.n
                sdx = sd.n
                so0 = so.n
                with nc.Block() as blk:
                    @blk.sync
                    def _(sy):
                        sd.inc(sy.dma_start(xt[:], x_t[gt]), 16)

                    @blk.tensor
                    def _(t_):
                        st.inc(t_.matmul(pfa[2][:, 0:1],
                                         rd_row[0:1, ti * 128:(ti + 1) * 128],
                                         ctx.ones32[0:1, 0:1], start=True, stop=True))
                        for half in range(2):
                            pj = pfa[half]
                            for mc in range(KC):
                                last = t_.matmul(pj[:], numT[:, mc * 512 + ti * 128: mc * 512 + ti * 128 + 128],
                                                 wproj[:, mc * 1024 + half * 512: mc * 1024 + half * 512 + 512],
                                                 start=(mc == 0), stop=(mc == KC - 1))
                            st.inc(last)

                    @blk.vector
                    def _(v):
                        v.wait_ge(st.h, st0 + 1)
                        sv.inc(v.tensor_copy(rdcol[:], pfa[2][:, 0:1]))
                        v.wait_ge(st.h, st0 + 3)
                        v.wait_ge(sd.h, sdx + 16)
                        v.wait_ge(so.h, so0)
                        v.wait_ge(sv.h, sv0 + 1)
                        for half in range(2):
                            sv.inc(v.scalar_tensor_tensor(
                                xnew[:, half * 512:half * 512 + 512],
                                pfa[half][:], rdcol[:],
                                xt[:, half * 512:half * 512 + 512],
                                ALU.mult, ALU.add))
                        v.wait_ge(sv.h, sv0 + 3)
                        sv.inc(v.tensor_add(xnew[:], xnew[:], bprojB[:]))

                    @blk.gpsimd
                    def _(gp):
                        gp.wait_ge(sv.h, sv.n)
                        so.inc(gp.dma_start(xo_t[gt], xnew[:]), 16)

    with nc.Block() as blk:
        @blk.sync
        def _(sy):
            sy.wait_ge(so.h, so.n)



def prep_weights(inputs):
    W = {}
    W["s1_wuv"] = inputs["L0_attn_a_W_uv"]
    W["s1_buv"] = inputs["L0_attn_a_b_uv"]
    W["s1_gbT"] = np.stack([inputs["L0_attn_a_gamma"][0], inputs["L0_attn_a_beta"][0],
                            inputs["L0_attn_a_gamma"][1], inputs["L0_attn_a_beta"][1]],
                           axis=1)
    W["s1_wout"] = inputs["L0_attn_a_W_out"]
    W["s1_bout"] = inputs["L0_attn_a_b_out"]
    W["s1_buvc"] = np.asarray(inputs["L0_attn_a_b_uv"]).reshape(33, 128).T
    W["moe_w1T"] = np.asarray(inputs["L0_ffn_w1"]).transpose(0, 2, 1)
    W["moe_w2T"] = np.asarray(inputs["L0_ffn_w2"]).transpose(0, 2, 1)
    W["moe_b1c"] = np.asarray(inputs["L0_ffn_b1"]).reshape(8, 32, 128).transpose(2, 0, 1).reshape(128, 256)
    W["moe_b2"] = np.asarray(inputs["L0_ffn_b2"])
    W["moe_rwT"] = np.asarray(inputs["L0_ffn_router_w"]).T.reshape(8, 128, 8).transpose(1, 0, 2).reshape(128, 64)
    W["moe_rb"] = np.asarray(inputs["L0_ffn_router_b"])
    W["moe_lnwB"] = np.broadcast_to(np.asarray(inputs["L0_ffn_ln_w"])[None, :], (128, 1024))
    W["moe_lnbB"] = np.broadcast_to(np.asarray(inputs["L0_ffn_ln_b"])[None, :], (128, 1024))
    W["s3_wqkvT"] = np.asarray(inputs["L0_attn_b_w_qkv"]).T
    W["s3_woutT"] = np.asarray(inputs["L0_attn_b_w_out"]).T
    W["s3_bout"] = np.asarray(inputs["L0_attn_b_b_out"])
    W["s3_gb"] = np.concatenate([np.asarray(inputs["L0_attn_b_norm_g"]).reshape(16),
                                 np.asarray(inputs["L0_attn_b_norm_b"]).reshape(16)])
    W["s5_wqkv"] = np.asarray(inputs["L1_attn_a_w_qkv"])
    W["s5_bqkv"] = np.asarray(inputs["L1_attn_a_b_qkv"])
    W["s5_bqc"] = np.asarray(inputs["L1_attn_a_b_qkv"]).reshape(24, 128).T
    W["s5_wout"] = np.asarray(inputs["L1_attn_a_w_out"])
    W["s5_bout"] = np.asarray(inputs["L1_attn_a_b_out"])
    W["s6_w1T"] = np.asarray(inputs["L1_ffn_w1"]).T
    W["s6_w2T"] = np.asarray(inputs["L1_ffn_w2"]).T
    W["s6_g1c"] = np.asarray(inputs["L1_ffn_ln1_g"]).reshape(8, 128).T
    W["s6_b1lc"] = np.asarray(inputs["L1_ffn_ln1_b"]).reshape(8, 128).T
    W["s6_g2c"] = np.asarray(inputs["L1_ffn_ln2_g"]).reshape(32, 128).T
    W["s6_b2lc"] = np.asarray(inputs["L1_ffn_ln2_b"]).reshape(32, 128).T
    W["s6_b1c"] = np.asarray(inputs["L1_ffn_b1"]).reshape(32, 128).T
    W["s6_b2"] = np.asarray(inputs["L1_ffn_b2"])
    W["s7_wqkv"] = np.asarray(inputs["L1_attn_b_w_qkv"])
    W["s7_wgate"] = np.asarray(inputs["L1_attn_b_w_gate"])
    W["s7_wproj"] = np.asarray(inputs["L1_attn_b_w_proj"])
    W["s7_bqkv"] = np.asarray(inputs["L1_attn_b_b_qkv"])
    W["s7_bgate"] = np.asarray(inputs["L1_attn_b_b_gate"])
    W["s7_bproj"] = np.asarray(inputs["L1_attn_b_b_proj"])
    W["s7_gc"] = np.asarray(inputs["L1_attn_b_ln_g"]).reshape(8, 128).T
    W["s7_bc"] = np.asarray(inputs["L1_attn_b_ln_b"]).reshape(8, 128).T
    return {k: np.ascontiguousarray(np.asarray(v, dtype=np.float32)) for k, v in W.items()}


def declare_weight_inputs(nc, Wnp):
    return {k: nc.dram_tensor("in_" + k, list(v.shape), F32, kind="ExternalInput").ap()
            for k, v in Wnp.items()}


PHASES_ALL = ["s1", "moe1", "s3", "moe2", "s5", "s6a", "s7", "s6b"]

EMITTERS = {}


def _register():
    EMITTERS["s1"] = lambda ctx, es, ri, ro, W: emit_s1(ctx, es, ri, ro, W)
    EMITTERS["moe1"] = lambda ctx, es, ri, ro, W: emit_moe(ctx, es, ri, ro, W, "m1")
    EMITTERS["moe2"] = lambda ctx, es, ri, ro, W: emit_moe(ctx, es, ri, ro, W, "m2")
    EMITTERS["s3"] = lambda ctx, es, ri, ro, W: emit_s3(ctx, es, ri, ro, W)
    EMITTERS["s5"] = lambda ctx, es, ri, ro, W: emit_s5(ctx, es, ri, ro, W)
    EMITTERS["s7"] = lambda ctx, es, ri, ro, W: emit_s7(ctx, es, ri, ro, W)


_register()


def build_program(Wnp, phases):
    nc = bass.Bass("TRN2", target_bir_lowering=False)
    x_in = nc.dram_tensor("x_in", [T, D], F32, kind="ExternalInput").ap()
    y_out = nc.dram_tensor("y_out", [T, D], F32, kind="ExternalOutput").ap()
    resid_mid = None
    if len(phases) > 1:
        resid_mid = nc.dram_tensor("resid_mid", [T, D], F32, kind="Internal").ap()
    Waps = declare_weight_inputs(nc, Wnp)
    ctx = Ctx(nc)
    s6_scratch = None
    if any(p in ("s6a", "s6b") for p in phases):
        s6_scratch = {
            "t1g": nc.dram_tensor("s6_t1g", [KC * 128, FF], BF16, kind="Internal").ap(),
            "t2g": nc.dram_tensor("s6_t2g", [32 * 128, 1024], BF16, kind="Internal").ap(),
            "bcomb": nc.dram_tensor("s6_bcombh", [128, 32], F32, kind="Internal").ap(),
            "rbs1": nc.dram_tensor("s6_rbs1h", [128, 1], F32, kind="Internal").ap(),
            "cp": nc.dram_tensor("s6_cph", [1, 1024], BF16, kind="Internal").ap(),
            "ep": nc.dram_tensor("s6_eph", [1, 1024], BF16, kind="Internal").ap(),
        }
    first_s6 = True
    with contextlib.ExitStack() as stack:
        ctx.alloc_consts(lambda cm: stack.enter_context(cm))
        for pi, phname in enumerate(phases):
            rin = x_in if pi == 0 else resid_mid
            rout = y_out if pi == len(phases) - 1 else resid_mid
            with contextlib.ExitStack() as ps:
                pes = lambda cm: ps.enter_context(cm)
                if phname in ("s6a", "s6b"):
                    emit_s6(ctx, pes, rin, rout, Waps, phname, s6_scratch,
                            quant=first_s6)
                    first_s6 = False
                else:
                    EMITTERS[phname](ctx, pes, rin, rout, Waps)
    return nc


def kernel(**inputs):
    Wnp = prep_weights(inputs)
    x = np.ascontiguousarray(np.asarray(inputs["L0_attn_a_inputs"], dtype=np.float32))
    nc = build_program(Wnp, PHASES_ALL)
    in_maps = []
    for c in range(NCORES):
        m = {"x_in": np.ascontiguousarray(x[c * NSEQ:(c + 1) * NSEQ].reshape(T, D))}
        m.update({"in_" + k: v for k, v in Wnp.items()})
        in_maps.append(m)
    res = run_bass_kernel_spmd(nc, in_maps, core_ids=list(range(NCORES)))
    out = np.concatenate([np.asarray(r["y_out"]).reshape(NSEQ, S, D)
                          for r in res.results], axis=0)
    return out


# version 06
# speedup vs baseline: 1.1273x; 1.1273x over previous
import contextlib
import numpy as np
import ml_dtypes
import concourse.bass as bass
import concourse.mybir as mybir
from concourse.bass_utils import run_bass_kernel_spmd

F32 = mybir.dt.float32
BF16 = mybir.dt.bfloat16
AF = mybir.ActivationFunctionType
ALU = mybir.AluOpType
AX = mybir.AxisListType

NCORES = 8
B_TOT = 32
S = 2048
D = 1024
NSEQ = B_TOT // NCORES
T = NSEQ * S
NT = T // 128
TPS = S // 128
KC = D // 128
FF = 4096
NEXP = 8


class Sem:

    def __init__(self, nc, name):
        self._cm = nc.semaphore(name)
        self.h = self._cm.__enter__()
        self.n = 0

    def inc(self, instr, k=1):
        instr.then_inc(self.h, k)
        self.n += k
        return instr


class Ctx:
    def __init__(self, nc):
        self.nc = nc
        self.sd = Sem(nc, "sd")
        self.so = Sem(nc, "so")
        self.st = Sem(nc, "st")
        self.sv = Sem(nc, "sv")
        self.sa = Sem(nc, "sa")
        self.pa = Sem(nc, "pa")
        self.pb = Sem(nc, "pb")
        self.sw1 = Sem(nc, "sw1")
        self.sw2 = Sem(nc, "sw2")

    def alloc_consts(self, es):
        nc = self.nc
        self.ident = es(nc.sbuf_tensor("c_ident", [128, 128], BF16))
        self.ident32 = es(nc.sbuf_tensor("c_ident32", [128, 128], F32))
        self.ones_row = es(nc.sbuf_tensor("c_ones", [1, 128], BF16))
        self.ones32 = es(nc.sbuf_tensor("c_ones32", [1, 128], F32))
        self.ltri = es(nc.sbuf_tensor("c_ltri", [128, 128], BF16))
        with nc.Block() as blk:
            @blk.gpsimd
            def _(g):
                g.memset(self.ident[:], 0.0)
                g.affine_select(out=self.ident[:], in_=self.ident[:],
                                compare_op=ALU.not_equal, fill=1.0, base=0,
                                pattern=[[-1, 128]], channel_multiplier=1)
                g.memset(self.ident32[:], 0.0)
                g.affine_select(out=self.ident32[:], in_=self.ident32[:],
                                compare_op=ALU.not_equal, fill=1.0, base=0,
                                pattern=[[-1, 128]], channel_multiplier=1)
                g.memset(self.ones_row[:], 1.0)
                g.memset(self.ones32[:], 1.0)
                g.memset(self.ltri[:], 1.0)
                g.affine_select(out=self.ltri[:], in_=self.ltri[:],
                                compare_op=ALU.is_ge, fill=0.0, base=0,
                                pattern=[[1, 128]], channel_multiplier=-1)


def emit_ln_tile(ctx, x_tile_ap, out_bf, scr, eps=1e-5, out_f32=None,
                 W=1024, blk=None):
    nc = ctx.nc
    sv, sa = ctx.sv, ctx.sa
    mean, t1, rstd, sq = scr["mean"], scr["t1"], scr["rstd"], scr["sq"]
    inv = 1.0 / W
    sv0, sa0 = sv.n, sa.n
    import contextlib as _c
    own = blk is None
    cmgr = nc.Block() if own else _c.nullcontext(blk)
    with cmgr as blk_:
        @blk_.vector
        def _(v):
            sv.inc(v.reduce_sum(mean[:], x_tile_ap, axis=AX.X))
            v.wait_ge(sv.h, sv0 + 1)
            sv.inc(v.tensor_scalar_mul(mean[:], mean[:], -inv))
            v.wait_ge(sa.h, sa0 + 1)
            sv.inc(v.tensor_scalar(rstd[:], t1[:], inv, eps, ALU.mult, ALU.add))
            v.wait_ge(sa.h, sa0 + 2)
            sv.inc(v.reciprocal(rstd[:], t1[:]))
            v.wait_ge(sv.h, sv0 + 4)
            if out_bf is not None:
                sv.inc(v.tensor_scalar(out_bf, x_tile_ap, mean[:], rstd[:],
                                       ALU.add, ALU.mult))
            if out_f32 is not None:
                sv.inc(v.tensor_scalar(out_f32, x_tile_ap, mean[:], rstd[:],
                                       ALU.add, ALU.mult))

        @blk_.scalar
        def _(sc):
            sc.wait_ge(sv.h, sv0 + 2)
            sa.inc(sc.activation(sq[:, 0:W], x_tile_ap, AF.Square,
                                 bias=mean[:], accum_out=t1[:]))
            sc.wait_ge(sv.h, sv0 + 3)
            sa.inc(sc.activation(t1[:], rstd[:], AF.Sqrt))


def emit_transpose_tile(ctx, src_bf, dst_slices, ptr, blk=None,
                        src_ready_sv=None):
    nc = ctx.nc
    st, sv = ctx.st, ctx.sv
    n = len(dst_slices)
    st0, sv0 = st.n, sv.n
    import contextlib as _c
    own = blk is None
    cmgr = nc.Block() if own else _c.nullcontext(blk)
    with cmgr as blk_:
        @blk_.tensor
        def _(t):
            if src_ready_sv is not None:
                t.wait_ge(sv.h, src_ready_sv)
            for k in range(n):
                if k >= 2:
                    t.wait_ge(sv.h, sv0 + k - 1)
                slot = ptr[k % 2][:, 0:128]
                st.inc(t.transpose(slot, src_bf[:, k * 128:(k + 1) * 128],
                                   ctx.ident[:]))

        @blk_.vector
        def _(v):
            for k in range(n):
                v.wait_ge(st.h, st0 + k + 1)
                slot = ptr[k % 2][:, 0:128]
                sv.inc(v.tensor_copy(dst_slices[k], slot))



def emit_s1(ctx, es, resid_in, resid_out, W):
    nc = ctx.nc
    sd, so, st, sv, sa = ctx.sd, ctx.so, ctx.st, ctx.sv, ctx.sa

    x_t = resid_in.rearrange("(n p) d -> n p d", p=128)
    xo_t = resid_out.rearrange("(n p) d -> n p d", p=128)

    wuv = es(nc.sbuf_tensor("s1_wuv", [128, KC * 4224], BF16))
    wout = es(nc.sbuf_tensor("s1_wout", [128, 16 * 1024], BF16))
    buv_row = es(nc.sbuf_tensor("s1_buvr", [1, 4224], BF16))
    buv_col = es(nc.sbuf_tensor("s1_buvc", [128, 33], F32))
    bout_row = es(nc.sbuf_tensor("s1_boutr", [1, 1024], BF16))
    gb = es(nc.sbuf_tensor("s1_gb", [128, 4], F32))
    gbq = es(nc.sbuf_tensor("s1_gbq", [128, 2], F32))
    aTt = es(nc.sbuf_tensor("s1_aTt", [128, KC * 128], BF16))
    aTg = es(nc.sbuf_tensor("s1_aTg", [128, KC * 512], BF16))
    qT = es(nc.sbuf_tensor("s1_qT", [128, S], BF16))
    kT = es(nc.sbuf_tensor("s1_kT", [128, S], BF16))
    uT = es(nc.sbuf_tensor("s1_uT", [128, 16 * 512], BF16))
    sqT = es(nc.sbuf_tensor("s1_sqT", [128, 16 * 512], BF16))
    vband = es(nc.sbuf_tensor("s1_vband", [128, 16 * 512], BF16))
    xt = es(nc.sbuf_tensor("s1_xt", [128, 1024], F32))
    a_bf = es(nc.sbuf_tensor("s1_abf", [128, 1024], BF16))
    vtile = es(nc.sbuf_tensor("s1_vtile", [128, 2048], BF16))
    base_bf = es(nc.sbuf_tensor("s1_basebf", [128, 128], BF16))
    tmp0 = es(nc.sbuf_tensor("s1_tmp0", [128, 512], F32))
    tmp1 = es(nc.sbuf_tensor("s1_tmp1", [128, 512], F32))
    xnew = es(nc.sbuf_tensor("s1_xnew", [128, 1024], F32))
    scr = {
        "mean": es(nc.sbuf_tensor("s1_mean", [128, 1], F32)),
        "t1": es(nc.sbuf_tensor("s1_t1", [128, 1], F32)),
        "rstd": es(nc.sbuf_tensor("s1_rstd", [128, 1], F32)),
        "sq": es(nc.sbuf_tensor("s1_sq", [128, 1024], F32)),
    }

    ph = [es(nc.psum_tensor(f"s1_ph{i}", [128, 512], F32)) for i in range(2)]
    ptr = [es(nc.psum_tensor(f"s1_ptr{i}", [128, 1024], BF16)) for i in range(2)]
    pav = [es(nc.psum_tensor(f"s1_pav{i}", [128, 512], F32)) for i in range(2)]
    po = [es(nc.psum_tensor(f"s1_po{i}", [128, 512], F32)) for i in range(2)]

    v_hbm = nc.dram_tensor("s1_vhbm", [4 * 16 * 128, 512], BF16,
                           kind="Internal").ap()
    v_blk = v_hbm.rearrange("(e k p) n -> e k p n", k=16, p=128)
    v_band_ap = v_hbm.rearrange("(e q) n -> e q n", q=16 * 128)
    aT_hbm = nc.dram_tensor("s1_aThbm", [KC * 128, S], BF16, kind="Internal").ap()
    aT_c = aT_hbm.rearrange("(c p) t -> c p t", p=128)

    wuv_hbm = W["s1_wuv"].rearrange("(c p) n -> c p n", p=128)
    wout_hbm = W["s1_wout"].rearrange("(c p) n -> c p n", p=128)
    sd0 = sd.n
    with nc.Block() as blk:
        @blk.gpsimd
        def _(g):
            for kc in range(KC):
                sd.inc(g.dma_start(wuv[:, kc * 4224:(kc + 1) * 4224], wuv_hbm[kc]), 16)
            for ec in range(16):
                sd.inc(g.dma_start(wout[:, ec * 1024:(ec + 1) * 1024], wout_hbm[ec]), 16)
            sd.inc(g.dma_start(buv_row[:], W["s1_buv"][None, :]), 16)
            sd.inc(g.dma_start(bout_row[:], W["s1_bout"][None, :]), 16)
            sd.inc(g.dma_start(buv_col[:], W["s1_buvc"]), 16)
            sd.inc(g.dma_start(gb[:], W["s1_gbT"]), 16)

        @blk.vector
        def _(v):
            v.wait_ge(sd.h, sd.n)
            sv.inc(v.tensor_scalar_mul(gbq[:], gb[:, 0:2], 2.0 ** -11))

    for s in range(NSEQ):
        for i in range(TPS):
            g_tile = s * TPS + i
            with nc.Block() as blk:
                @blk.sync
                def _(sy):
                    sd.inc(sy.dma_start(xt[:], x_t[g_tile]), 16)

                @blk.vector
                def _(v):
                    v.wait_ge(sd.h, sd.n)
            emit_ln_tile(ctx, xt[:], a_bf[:], scr)
            dsts = [aTt[:, kc * 128:(kc + 1) * 128] for kc in range(KC)]
            emit_transpose_tile(ctx, a_bf, dsts, ptr)
            with nc.Block() as blk:
                @blk.sync
                def _(sy):
                    for kc in range(KC):
                        so.inc(sy.dma_start(aT_c[kc][:, i * 128:(i + 1) * 128],
                                            aTt[:, kc * 128:(kc + 1) * 128]), 16)

            st0, sa0 = st.n, sa.n
            so_prev = so.n
            with nc.Block() as blk:
                @blk.tensor
                def _(t):
                    for j in range(4):
                        pj = ph[j % 2]
                        if j >= 2:
                            t.wait_ge(sa.h, sa0 + j - 1)
                        for kc in range(KC):
                            t.matmul(pj[:], aTt[:, kc * 128:(kc + 1) * 128],
                                     wuv[:, kc * 4224 + 2048 + j * 512: kc * 4224 + 2560 + j * 512],
                                     start=(kc == 0), stop=False)
                        st.inc(t.matmul(pj[:], ctx.ones_row[:],
                                        buv_row[:, 2048 + j * 512: 2560 + j * 512],
                                        start=False, stop=True))
                    t.wait_ge(sa.h, sa0 + 3)
                    for kc in range(KC):
                        t.matmul(ph[0][:, 0:128], aTt[:, kc * 128:(kc + 1) * 128],
                                 wuv[:, kc * 4224 + 4096: kc * 4224 + 4224],
                                 start=(kc == 0), stop=False)
                    st.inc(t.matmul(ph[0][:, 0:128], ctx.ones_row[:],
                                    buv_row[:, 4096:4224], start=False, stop=True))

                @blk.scalar
                def _(sc):
                    sc.wait_ge(so.h, so_prev)
                    for j in range(4):
                        sc.wait_ge(st.h, st0 + j + 1)
                        sa.inc(sc.activation(vtile[:, j * 512:(j + 1) * 512],
                                             ph[j % 2][:], AF.Silu))
                    sc.wait_ge(st.h, st0 + 5)
                    sa.inc(sc.activation(base_bf[:], ph[0][:, 0:128], AF.Silu))

            st1, sv1 = st.n, sv.n
            with nc.Block() as blk:
                @blk.sync
                def _(sy):
                    for eg in range(4):
                        so.inc(sy.dma_start(v_blk[eg, i], vtile[:, eg * 512:(eg + 1) * 512]), 16)

                @blk.tensor
                def _(t):
                    st.inc(t.transpose(ptr[0][:, 0:128], base_bf[:], ctx.ident[:]))

                @blk.vector
                def _(v):
                    v.wait_ge(st.h, st1 + 1)
                    sv.inc(v.tensor_scalar(qT[:, i * 128:(i + 1) * 128], ptr[0][:, 0:128],
                                           gbq[:, 0:1], gbq[:, 1:2], ALU.mult, ALU.add))
                    sv.inc(v.tensor_scalar(kT[:, i * 128:(i + 1) * 128], ptr[0][:, 0:128],
                                           gb[:, 2:3], gb[:, 3:4], ALU.mult, ALU.add))

        for g in range(4):
            tq0 = g * 512
            sda = sd.n
            so_w = so.n
            with nc.Block() as blk:
                @blk.sync
                def _(sy):
                    sy.wait_ge(so.h, so_w)
                    for kc in range(KC):
                        sd.inc(sy.dma_start(aTg[:, kc * 512:(kc + 1) * 512],
                                            aT_c[kc][:, tq0:tq0 + 512]), 16)
            st0, sa0 = st.n, sa.n
            with nc.Block() as blk:
                @blk.tensor
                def _(t):
                    t.wait_ge(sd.h, sda + 16 * KC)
                    for ec in range(16):
                        pj = ph[ec % 2]
                        if ec >= 2:
                            t.wait_ge(sa.h, sa0 + ec - 1)
                        for kc in range(KC):
                            last = t.matmul(pj[:], wuv[:, kc * 4224 + ec * 128: kc * 4224 + ec * 128 + 128],
                                            aTg[:, kc * 512:(kc + 1) * 512],
                                            start=(kc == 0), stop=(kc == KC - 1))
                        st.inc(last)

                @blk.scalar
                def _(sc):
                    for ec in range(16):
                        sc.wait_ge(st.h, st0 + ec + 1)
                        sa.inc(sc.activation(uT[:, ec * 512:(ec + 1) * 512],
                                             ph[ec % 2][:], AF.Silu,
                                             bias=buv_col[:, ec:ec + 1]))

            st0, sv0, sa0 = st.n, sv.n, sa.n
            with nc.Block() as blk:
                @blk.tensor
                def _(t):
                    for tkc in range(16):
                        pj = pav[tkc % 2]
                        if tkc >= 2:
                            t.wait_ge(sv.h, sv0 + tkc - 1)
                        st.inc(t.matmul(pj[:], kT[:, tkc * 128:(tkc + 1) * 128],
                                        qT[:, tq0:tq0 + 512], start=True, stop=True))

                @blk.vector
                def _(v):
                    for tkc in range(16):
                        v.wait_ge(st.h, st0 + tkc + 1)
                        if tkc >= 2:
                            v.wait_ge(sa.h, sa0 + tkc - 1)
                        tm = tmp0 if tkc % 2 == 0 else tmp1
                        sv.inc(v.tensor_scalar_max(tm[:], pav[tkc % 2][:], 0.0))

                @blk.scalar
                def _(sc):
                    for tkc in range(16):
                        sc.wait_ge(sv.h, sv0 + tkc + 1)
                        tm = tmp0 if tkc % 2 == 0 else tmp1
                        sa.inc(sc.activation(sqT[:, tkc * 512:(tkc + 1) * 512],
                                             tm[:], AF.Square))

            for eg in range(4):
                st0, sv0 = st.n, sv.n
                sdv = sd.n
                so_w2 = so.n
                with nc.Block() as blk:
                    @blk.sync
                    def _(sy):
                        sy.wait_ge(so.h, so_w2)
                        sd.inc(sy.dma_start(vband[:], v_band_ap[eg]), 16)

                    @blk.tensor
                    def _(t):
                        t.wait_ge(sd.h, sdv + 16)
                        for ecl in range(4):
                            pj = pav[ecl % 2]
                            if ecl >= 2:
                                t.wait_ge(sv.h, sv0 + ecl - 1)
                            for tkc in range(16):
                                last = t.matmul(pj[:], vband[:, tkc * 512 + ecl * 128: tkc * 512 + ecl * 128 + 128],
                                                sqT[:, tkc * 512:(tkc + 1) * 512],
                                                start=(tkc == 0), stop=(tkc == 15))
                            st.inc(last)

                    @blk.vector
                    def _(v):
                        for ecl in range(4):
                            ec = eg * 4 + ecl
                            v.wait_ge(st.h, st0 + ecl + 1)
                            sv.inc(v.tensor_mul(uT[:, ec * 512:(ec + 1) * 512],
                                                pav[ecl % 2][:],
                                                uT[:, ec * 512:(ec + 1) * 512]))

            for ti in range(4):
                i = g * 4 + ti
                g_tile = s * TPS + i
                col = ti * 128
                st0, sv0 = st.n, sv.n
                sdx = sd.n
                so0 = so.n
                with nc.Block() as blk:
                    @blk.sync
                    def _(sy):
                        sd.inc(sy.dma_start(xt[:], x_t[g_tile]), 16)

                    @blk.tensor
                    def _(t):
                        for half in range(2):
                            pj = po[half]
                            for ec in range(16):
                                t.matmul(pj[:], uT[:, ec * 512 + col: ec * 512 + col + 128],
                                         wout[:, ec * 1024 + half * 512: ec * 1024 + half * 512 + 512],
                                         start=(ec == 0), stop=False)
                            st.inc(t.matmul(pj[:], ctx.ones_row[:],
                                            bout_row[:, half * 512:half * 512 + 512],
                                            start=False, stop=True))

                    @blk.vector
                    def _(v):
                        v.wait_ge(st.h, st0 + 2)
                        v.wait_ge(sd.h, sdx + 16)
                        v.wait_ge(so.h, so0)
                        for half in range(2):
                            sv.inc(v.tensor_add(xnew[:, half * 512:half * 512 + 512],
                                                po[half][:],
                                                xt[:, half * 512:half * 512 + 512]))

                    @blk.gpsimd
                    def _(gp):
                        gp.wait_ge(sv.h, sv.n)
                        so.inc(gp.dma_start(xo_t[g_tile], xnew[:]), 16)

    with nc.Block() as blk:
        @blk.sync
        def _(sy):
            sy.wait_ge(so.h, so.n)





def emit_moe(ctx, es, resid_in, resid_out, W, tag):
    nc = ctx.nc
    sd, so, st, sv, sa = ctx.sd, ctx.so, ctx.st, ctx.sv, ctx.sa
    NG = NT // 4

    x_t = resid_in.rearrange("(n p) d -> n p d", p=128)
    xo_t = resid_out.rearrange("(n p) d -> n p d", p=128)

    HFF = FF // 2
    w1sb = es(nc.sbuf_tensor(f"{tag}_w1", [128, KC * HFF], BF16))
    w2sb = es(nc.sbuf_tensor(f"{tag}_w2", [128, 16 * 1024], BF16))
    hT = [es(nc.sbuf_tensor(f"{tag}_hT{i}", [128, 16 * 512], BF16)) for i in range(2)]
    aTg = [es(nc.sbuf_tensor(f"{tag}_aTg{i}", [128, KC * 512], BF16)) for i in range(2)]
    acc_sb = [es(nc.sbuf_tensor(f"{tag}_acc{i}", [128, 4 * 1024], F32)) for i in range(2)]
    cw_sb = es(nc.sbuf_tensor(f"{tag}_cw", [128, NT * NEXP], F32))
    cwT_bf = es(nc.sbuf_tensor(f"{tag}_cwTbf", [8, 128], BF16))
    b1c = es(nc.sbuf_tensor(f"{tag}_b1c", [128, NEXP * 32], F32))
    b2sb = es(nc.sbuf_tensor(f"{tag}_b2", [8, 1024], BF16))
    rwT32 = es(nc.sbuf_tensor(f"{tag}_rwT", [128, KC * 8], F32))
    rb_row = es(nc.sbuf_tensor(f"{tag}_rb", [1, 8], F32))
    lnwB = es(nc.sbuf_tensor(f"{tag}_lnwB", [128, 1024], F32))
    lnbB = es(nc.sbuf_tensor(f"{tag}_lnbB", [128, 1024], F32))
    xt = es(nc.sbuf_tensor(f"{tag}_xt", [128, 1024], F32))
    a_bf = es(nc.sbuf_tensor(f"{tag}_abf", [128, 1024], BF16))
    a32 = es(nc.sbuf_tensor(f"{tag}_a32", [128, 1024], F32))
    aTt = es(nc.sbuf_tensor(f"{tag}_aTt", [128, KC * 128], BF16))
    aTt32 = es(nc.sbuf_tensor(f"{tag}_aTt32", [128, KC * 128], F32))
    L = es(nc.sbuf_tensor(f"{tag}_L", [128, 8], F32))
    Lm = es(nc.sbuf_tensor(f"{tag}_Lm", [128, 8], F32))
    c1 = es(nc.sbuf_tensor(f"{tag}_c1", [128, 8], F32))
    cs = es(nc.sbuf_tensor(f"{tag}_cs", [128, 8], F32))
    fsts = es(nc.sbuf_tensor(f"{tag}_fst", [128, 8], F32))
    snds = es(nc.sbuf_tensor(f"{tag}_snd", [128, 8], F32))
    m1 = es(nc.sbuf_tensor(f"{tag}_m1", [128, 1], F32))
    m2 = es(nc.sbuf_tensor(f"{tag}_m2", [128, 1], F32))
    p1 = es(nc.sbuf_tensor(f"{tag}_p1", [128, 1], F32))
    p2 = es(nc.sbuf_tensor(f"{tag}_p2", [128, 1], F32))
    nrm = es(nc.sbuf_tensor(f"{tag}_nrm", [128, 1024], F32))
    xnew = es(nc.sbuf_tensor(f"{tag}_xnew", [128, 1024], F32))
    scr = {
        "mean": es(nc.sbuf_tensor(f"{tag}_mean", [128, 1], F32)),
        "t1": es(nc.sbuf_tensor(f"{tag}_t1", [128, 1], F32)),
        "rstd": es(nc.sbuf_tensor(f"{tag}_rstd", [128, 1], F32)),
        "sq": es(nc.sbuf_tensor(f"{tag}_sq", [128, 1024], F32)),
    }

    ph = [es(nc.psum_tensor(f"{tag}_ph{i}", [128, 512], F32)) for i in range(2)]
    py = [es(nc.psum_tensor(f"{tag}_py{i}", [128, 512], F32)) for i in range(2)]
    pt = [es(nc.psum_tensor(f"{tag}_pt{i}", [128, 1024], BF16)) for i in range(2)]

    aT_hbm = nc.dram_tensor(f"{tag}_aThbm", [KC * 128, T], BF16, kind="Internal").ap()
    aT_c = aT_hbm.rearrange("(c p) t -> c p t", p=128)
    acc_hbm = nc.dram_tensor(f"{tag}_acchbm", [T, 1024], F32, kind="Internal").ap()
    acc_t = acc_hbm.rearrange("(n p) d -> n p d", p=128)

    w1T_hbm = W["moe_w1T"]
    w2T_hbm = W["moe_w2T"]

    with nc.Block() as blk:
        @blk.gpsimd
        def _(g):
            sd.inc(g.dma_start(b1c[:], W["moe_b1c"]), 16)
            sd.inc(g.dma_start(b2sb[:], W["moe_b2"]), 16)
            sd.inc(g.dma_start(rwT32[:], W["moe_rwT"]), 16)
            sd.inc(g.dma_start(rb_row[:], W["moe_rb"][None, :]), 16)
            sd.inc(g.dma_start(lnwB[:], W["moe_lnwB"]), 16)
            sd.inc(g.dma_start(lnbB[:], W["moe_lnbB"]), 16)

        @blk.vector
        def _(v):
            v.wait_ge(sd.h, sd.n)

    for t in range(NT):
        so_prev = so.n
        with nc.Block() as blk:
            @blk.sync
            def _(sy):
                sd.inc(sy.dma_start(xt[:], x_t[t]), 16)

            @blk.vector
            def _(v):
                v.wait_ge(sd.h, sd.n)
                v.wait_ge(so.h, so_prev)
        emit_ln_tile(ctx, xt[:], a_bf[:], scr, out_f32=a32[:])
        with nc.Block() as blk:
            @blk.sync
            def _(sy):
                so.inc(sy.dma_start(acc_t[t], a32[:]), 16)
        dsts = [aTt[:, kc * 128:(kc + 1) * 128] for kc in range(KC)]
        emit_transpose_tile(ctx, a_bf, dsts, pt)
        with nc.Block() as blk:
            @blk.sync
            def _(sy):
                for kc in range(KC):
                    so.inc(sy.dma_start(aT_c[kc][:, t * 128:(t + 1) * 128],
                                        aTt[:, kc * 128:(kc + 1) * 128]), 16)
        st0, sv0 = st.n, sv.n
        with nc.Block() as blk:
            @blk.tensor
            def _(t_):
                for kc in range(KC):
                    if kc >= 2:
                        t_.wait_ge(sv.h, sv0 + kc - 1)
                    st.inc(t_.transpose(py[kc % 2][:, 0:128],
                                        a32[:, kc * 128:(kc + 1) * 128],
                                        ctx.ident32[:]))

            @blk.vector
            def _(v):
                for kc in range(KC):
                    v.wait_ge(st.h, st0 + kc + 1)
                    sv.inc(v.tensor_copy(aTt32[:, kc * 128:(kc + 1) * 128],
                                         py[kc % 2][:, 0:128]))
        st0 = st.n
        with nc.Block() as blk:
            @blk.tensor
            def _(t_):
                for kc in range(KC):
                    t_.matmul(ph[0][:, 0:8], aTt32[:, kc * 128:(kc + 1) * 128],
                              rwT32[:, kc * 8:(kc + 1) * 8],
                              start=(kc == 0), stop=False)
                st.inc(t_.matmul(ph[0][:, 0:8], ctx.ones32[:], rb_row[:],
                                 start=False, stop=True))
        sv0, sa0 = sv.n, sa.n
        with nc.Block() as blk:
            @blk.vector
            def _(v):
                v.wait_ge(st.h, st0 + 1)
                sv.inc(v.tensor_copy(L[:], ph[0][:, 0:8]))
                v.wait_ge(sv.h, sv0 + 1)
                sv.inc(v.reduce_max(m1[:], L[:], axis=AX.X))
                v.wait_ge(sv.h, sv0 + 2)
                sv.inc(v.tensor_single_scalar(c1[:], L[:], m1[:], ALU.is_equal))
                v.wait_ge(sv.h, sv0 + 3)
                sv.inc(v.tensor_tensor_scan(cs[:], c1[:], c1[:], 0.0, ALU.add, ALU.bypass))
                v.wait_ge(sv.h, sv0 + 4)
                sv.inc(v.tensor_single_scalar(cs[:], cs[:], 1.0, ALU.is_equal))
                v.wait_ge(sv.h, sv0 + 5)
                sv.inc(v.tensor_mul(fsts[:], cs[:], c1[:]))
                v.wait_ge(sv.h, sv0 + 6)
                sv.inc(v.scalar_tensor_tensor(Lm[:], fsts[:], -1e30, L[:],
                                              ALU.mult, ALU.add))
                v.wait_ge(sv.h, sv0 + 7)
                sv.inc(v.reduce_max(m2[:], Lm[:], axis=AX.X))
                v.wait_ge(sv.h, sv0 + 8)
                sv.inc(v.tensor_single_scalar(c1[:], Lm[:], m2[:], ALU.is_equal))
                v.wait_ge(sv.h, sv0 + 9)
                sv.inc(v.tensor_tensor_scan(cs[:], c1[:], c1[:], 0.0, ALU.add, ALU.bypass))
                v.wait_ge(sv.h, sv0 + 10)
                sv.inc(v.tensor_single_scalar(cs[:], cs[:], 1.0, ALU.is_equal))
                v.wait_ge(sv.h, sv0 + 11)
                sv.inc(v.tensor_mul(snds[:], cs[:], c1[:]))
                sv.inc(v.tensor_sub(m2[:], m1[:], m2[:]))
                v.wait_ge(sa.h, sa0 + 1)
                sv.inc(v.tensor_scalar(p2[:], p1[:], -1.0, 1.0, ALU.mult, ALU.add))
                v.wait_ge(sv.h, sv0 + 14)
                sv.inc(v.tensor_single_scalar(fsts[:], fsts[:], p1[:], ALU.mult))
                sv.inc(v.tensor_single_scalar(snds[:], snds[:], p2[:], ALU.mult))
                v.wait_ge(sv.h, sv0 + 16)
                sv.inc(v.tensor_add(cw_sb[:, t * 8:(t + 1) * 8], fsts[:], snds[:]))

            @blk.scalar
            def _(sc):
                sc.wait_ge(sv.h, sv0 + 13)
                sa.inc(sc.activation(p1[:], m2[:], AF.Sigmoid))

    pa, pb, sw1, sw2 = ctx.pa, ctx.pb, ctx.sw1, ctx.sw2
    acc_store_cnt = [so.n] * NG
    sweeps = [(e, hf) for e in range(NEXP) for hf in range(2)]
    NSW = len(sweeps)
    h_done = {}
    h_done_nc = {}
    y_done = {}
    y_mm = {}
    y_evac = {}
    gelu_done = {}
    g_store_so = {}
    pa0, pb0 = pa.n, pb.n
    sw10, sw20 = sw1.n, sw2.n

    def psem(gi):
        return pa if gi % 2 == 0 else pb

    def load_tgt(gi):
        if gi % 2 == 0:
            return pa, pa0 + 192 * (gi // 2 + 1)
        return pb, pb0 + 192 * ((gi - 1) // 2 + 1)

    def w1_tgt(si):
        return sw10 + 128 * (si + 1)

    def w2_tgt(si):
        return sw20 + 256 * (si + 1)

    sa_m1 = sa.n

    def gelu_val(gi, nc_):
        return sa_m1 + gi * 16 + nc_ + 1

    def emit_group_load(sy, gi):
        sem = psem(gi)
        g = gi % NG
        buf = aTg[gi % 2]
        for kc in range(KC):
            sem.inc(sy.dma_start(buf[:, kc * 512:(kc + 1) * 512],
                                 aT_c[kc][:, g * 512:(g + 1) * 512]), 16)
        for tt in range(4):
            sem.inc(sy.dma_start(acc_sb[gi % 2][:, tt * 1024:(tt + 1) * 1024],
                                 acc_t[g * 4 + tt]), 16)

    so_m0 = so.n
    with nc.Block() as blk:
        @blk.gpsimd
        def _(g):
            e0, hf0 = sweeps[0]
            w1e = w1T_hbm[e0].rearrange("(c p) n -> c p n", p=128)
            for kc in range(KC):
                sw1.inc(g.dma_start(w1sb[:, kc * HFF:(kc + 1) * HFF],
                                    w1e[kc][:, hf0 * HFF:(hf0 + 1) * HFF]), 16)
            w2e = w2T_hbm[e0].rearrange("(c p) n -> c p n", p=128)
            for nc_ in range(16):
                sw2.inc(g.dma_start(w2sb[:, nc_ * 1024:(nc_ + 1) * 1024],
                                    w2e[hf0 * 16 + nc_]), 16)

        @blk.sync
        def _(sy):
            sy.wait_ge(so.h, so_m0)
            emit_group_load(sy, 0)
            emit_group_load(sy, 1)

    for si, (e, hf) in enumerate(sweeps):
        with nc.Block() as blk:
            @blk.tensor
            def _(t_):
                for g in range(NG):
                    gi = si * NG + g
                    buf = aTg[gi % 2]
                    sem, val = load_tgt(gi)
                    t_.wait_ge(sem.h, val)
                    if g == 0:
                        t_.wait_ge(sw1.h, w1_tgt(si))
                    for nc_ in range(16):
                        prev = (gi, nc_ - 2) if nc_ >= 2 else (gi - 1, 14 + nc_)
                        if prev[0] >= 0:
                            t_.wait_ge(sa.h, gelu_val(*prev))
                        for kc in range(KC):
                            last = t_.matmul(ph[nc_ % 2][:],
                                             w1sb[:, kc * HFF + nc_ * 128: kc * HFF + nc_ * 128 + 128],
                                             buf[:, kc * 512:(kc + 1) * 512],
                                             start=(kc == 0), stop=(kc == KC - 1))
                        st.inc(last)
                        h_done_nc[(gi, nc_)] = st.n
                    h_done[gi] = st.n
                    if g == 0:
                        t_.wait_ge(sw2.h, w2_tgt(si))
                    t_.wait_ge(sa.h, gelu_val(gi, 15))
                    idx = 0
                    for tt in range(4):
                        for mg in range(2):
                            pj = py[idx % 2]
                            prev_evac = y_evac.get((gi, idx - 2))
                            if prev_evac is None and idx < 2:
                                prev_evac = y_evac.get((gi - 1, 6 + idx))
                            if prev_evac is not None:
                                t_.wait_ge(sv.h, prev_evac)
                            for nc_ in range(16):
                                last = t_.matmul(pj[:], hT[gi % 2][:, nc_ * 512 + tt * 128: nc_ * 512 + tt * 128 + 128],
                                                 w2sb[:, nc_ * 1024 + mg * 512: nc_ * 1024 + mg * 512 + 512],
                                                 start=(nc_ == 0), stop=(nc_ == 15))
                            st.inc(last)
                            y_mm[(gi, idx)] = st.n
                            idx += 1
                    y_done[gi] = st.n

            @blk.scalar
            def _(sc):
                for g in range(NG):
                    gi = si * NG + g
                    for nc_ in range(16):
                        sc.wait_ge(st.h, h_done_nc[(gi, nc_)])
                        if nc_ == 0 and (gi - 2) in y_done:
                            sc.wait_ge(st.h, y_done[gi - 2])
                        sa.inc(sc.activation(hT[gi % 2][:, nc_ * 512:(nc_ + 1) * 512],
                                             ph[nc_ % 2][:], AF.Gelu,
                                             bias=b1c[:, e * 32 + hf * 16 + nc_: e * 32 + hf * 16 + nc_ + 1]))
                        gelu_done[(gi, nc_)] = sa.n

            @blk.vector
            def _(v):
                for g in range(NG):
                    gi = si * NG + g
                    sem, val = load_tgt(gi)
                    v.wait_ge(sem.h, val)
                    idx = 0
                    for tt in range(4):
                        gt = g * 4 + tt
                        for mg in range(2):
                            v.wait_ge(st.h, y_mm[(gi, idx)])
                            sv.inc(v.scalar_tensor_tensor(
                                acc_sb[gi % 2][:, tt * 1024 + mg * 512: tt * 1024 + mg * 512 + 512],
                                py[idx % 2][:],
                                cw_sb[:, gt * 8 + e: gt * 8 + e + 1],
                                acc_sb[gi % 2][:, tt * 1024 + mg * 512: tt * 1024 + mg * 512 + 512],
                                ALU.mult, ALU.add))
                            y_evac[(gi, idx)] = sv.n
                            idx += 1

            @blk.gpsimd
            def _(gp):
                for g in range(NG):
                    gi = si * NG + g
                    if g == NG - 1 and si + 1 < NSW:
                        e1, hf1 = sweeps[si + 1]
                        gp.wait_ge(st.h, h_done[si * NG + NG - 1])
                        w1e = w1T_hbm[e1].rearrange("(c p) n -> c p n", p=128)
                        for kc in range(KC):
                            sw1.inc(gp.dma_start(w1sb[:, kc * HFF:(kc + 1) * HFF],
                                                 w1e[kc][:, hf1 * HFF:(hf1 + 1) * HFF]), 16)
                    for tt in range(4):
                        gp.wait_ge(sv.h, y_evac[(gi, tt * 2 + 1)])
                        so.inc(gp.dma_start(acc_t[g * 4 + tt],
                                            acc_sb[gi % 2][:, tt * 1024:(tt + 1) * 1024]), 16)
                    acc_store_cnt[g] = so.n
                    g_store_so[gi] = so.n
                if si + 1 < NSW:
                    e1, hf1 = sweeps[si + 1]
                    gp.wait_ge(st.h, y_done[si * NG + NG - 1])
                    w2e = w2T_hbm[e1].rearrange("(c p) n -> c p n", p=128)
                    for nc_ in range(16):
                        sw2.inc(gp.dma_start(w2sb[:, nc_ * 1024:(nc_ + 1) * 1024],
                                             w2e[hf1 * 16 + nc_]), 16)

            @blk.sync
            def _(sy):
                for g in range(2, NG):
                    gi = si * NG + g
                    sy.wait_ge(st.h, h_done[gi - 2])
                    sy.wait_ge(so.h, g_store_so[gi - 2])
                    emit_group_load(sy, gi)
                if si + 1 < NSW:
                    gi0 = (si + 1) * NG
                    sy.wait_ge(st.h, h_done[gi0 - 2])
                    sy.wait_ge(so.h, g_store_so[gi0 - 2])
                    emit_group_load(sy, gi0)
                    sy.wait_ge(st.h, h_done[gi0 - 1])
                    sy.wait_ge(so.h, g_store_so[gi0 - 1])
                    emit_group_load(sy, gi0 + 1)

    for t in range(NT):
        so_prev = so.n
        sdx = sd.n
        st0 = st.n
        with nc.Block() as blk:
            @blk.sync
            def _(sy):
                sy.wait_ge(so.h, acc_store_cnt[t // 4])
                sd.inc(sy.dma_start(xt[:], x_t[t]), 16)
                sd.inc(sy.dma_start(a32[:], acc_t[t]), 16)

            @blk.vector
            def _(v):
                v.wait_ge(sd.h, sd.n)
                v.wait_ge(so.h, so_prev)
                sv.inc(v.tensor_copy(a_bf[:, 0:8], cw_sb[:, t * 8:(t + 1) * 8]))

            @blk.tensor
            def _(t_):
                t_.wait_ge(sv.h, sv.n)
                st.inc(t_.transpose(pt[0][0:8, 0:128], a_bf[:, 0:8], ctx.ident[:]))

        sv0 = sv.n
        with nc.Block() as blk:
            @blk.vector
            def _(v):
                v.wait_ge(st.h, st0 + 1)
                sv.inc(v.tensor_copy(cwT_bf[:], pt[0][0:8, 0:128]))

            @blk.tensor
            def _(t_):
                t_.wait_ge(sv.h, sv0 + 1)
                for mg in range(2):
                    st.inc(t_.matmul(py[mg][:], cwT_bf[:],
                                     b2sb[:, mg * 512:(mg + 1) * 512],
                                     start=True, stop=True))

        sv0 = sv.n
        with nc.Block() as blk:
            @blk.vector
            def _(v):
                v.wait_ge(st.h, st0 + 3)
                for mg in range(2):
                    sv.inc(v.tensor_add(a32[:, mg * 512:(mg + 1) * 512],
                                        a32[:, mg * 512:(mg + 1) * 512], py[mg][:]))
        emit_ln_tile(ctx, a32[:], None, scr, eps=1e-12, out_f32=nrm[:])
        sv0 = sv.n
        with nc.Block() as blk:
            @blk.vector
            def _(v):
                sv.inc(v.tensor_mul(nrm[:], nrm[:], lnwB[:]))
                v.wait_ge(sv.h, sv0 + 1)
                sv.inc(v.tensor_add(nrm[:], nrm[:], lnbB[:]))
                v.wait_ge(sv.h, sv0 + 2)
                sv.inc(v.tensor_add(xnew[:], nrm[:], xt[:]))

            @blk.gpsimd
            def _(gp):
                gp.wait_ge(sv.h, sv.n)
                so.inc(gp.dma_start(xo_t[t], xnew[:]), 16)

    with nc.Block() as blk:
        @blk.sync
        def _(sy):
            sy.wait_ge(so.h, so.n)




def alloc_ln_bufs(ctx, es, tag, sq_alias=None):
    nc = ctx.nc
    return {
        "xt": es(nc.sbuf_tensor(f"{tag}_xt", [128, 1024], F32)),
        "a_bf": es(nc.sbuf_tensor(f"{tag}_abf", [128, 1024], BF16)),
        "aTt": es(nc.sbuf_tensor(f"{tag}_aTt", [128, KC * 128], BF16)),
        "scr": {
            "mean": es(nc.sbuf_tensor(f"{tag}_mean", [128, 1], F32)),
            "t1": es(nc.sbuf_tensor(f"{tag}_t1", [128, 1], F32)),
            "rstd": es(nc.sbuf_tensor(f"{tag}_rstd", [128, 1], F32)),
            "sq": (sq_alias if sq_alias is not None
                   else es(nc.sbuf_tensor(f"{tag}_sq", [128, 1024], F32))),
        },
    }


def emit_lnaT_phase(ctx, bufs, x_t, aT_c, pt, tiles, eps=1e-5, double_ln=False):
    nc = ctx.nc
    sd, so, sv = ctx.sd, ctx.so, ctx.sv
    xt, a_bf, aTt, scr = bufs["xt"], bufs["a_bf"], bufs["aTt"], bufs["scr"]
    for t in tiles:
        so_prev = so.n
        with nc.Block() as blk:
            @blk.sync
            def _(sy):
                sd.inc(sy.dma_start(xt[:], x_t[t]), 16)

            @blk.vector
            def _(v):
                v.wait_ge(sd.h, sd.n)
                v.wait_ge(so.h, so_prev)
        if double_ln:
            emit_ln_tile(ctx, xt[:], None, scr, eps=eps, out_f32=bufs["a_bf2"][:])
            emit_ln_tile(ctx, bufs["a_bf2"][:], a_bf[:], scr, eps=eps)
        else:
            emit_ln_tile(ctx, xt[:], a_bf[:], scr, eps=eps)
        dsts = [aTt[:, kc * 128:(kc + 1) * 128] for kc in range(KC)]
        emit_transpose_tile(ctx, a_bf, dsts, pt)
        with nc.Block() as blk:
            @blk.sync
            def _(sy):
                for kc in range(KC):
                    so.inc(sy.dma_start(aT_c[kc][:, t * 128:(t + 1) * 128],
                                        aTt[:, kc * 128:(kc + 1) * 128]), 16)



def emit_s3(ctx, es, resid_in, resid_out, W):
    nc = ctx.nc
    sd, so, st, sv, sa = ctx.sd, ctx.so, ctx.st, ctx.sv, ctx.sa
    tag = "s3"

    x_t = resid_in.rearrange("(n p) d -> n p d", p=128)
    xo_t = resid_out.rearrange("(n p) d -> n p d", p=128)

    wqkvT = es(nc.sbuf_tensor(f"{tag}_wqkvT", [128, KC * 3072], BF16))
    woutT = es(nc.sbuf_tensor(f"{tag}_woutT", [128, KC * 1024], BF16))
    bout_row = es(nc.sbuf_tensor(f"{tag}_bout", [1, 1024], BF16))
    gb_row = es(nc.sbuf_tensor(f"{tag}_gb", [1, 32], F32))
    qT_hg = es(nc.sbuf_tensor(f"{tag}_qT", [128, 2 * S], BF16))
    kT_hg = es(nc.sbuf_tensor(f"{tag}_kT", [128, 2 * S], BF16))
    v_hg = es(nc.sbuf_tensor(f"{tag}_v", [128, 16 * 256], BF16))
    dotsT = es(nc.sbuf_tensor(f"{tag}_dotsT", [128, 16 * 512], BF16))
    sq_bf = es(nc.sbuf_tensor(f"{tag}_sqbf", [128, 512], BF16))
    attnvT = es(nc.sbuf_tensor(f"{tag}_attnvT", [128, KC * S], BF16))
    rn_b = es(nc.sbuf_tensor(f"{tag}_rnb", [64, 512], F32))
    aTg = es(nc.sbuf_tensor(f"{tag}_aTg", [128, KC * 512], BF16))
    csum_r = es(nc.sbuf_tensor(f"{tag}_csum", [1, 256], F32))
    bc_row = es(nc.sbuf_tensor(f"{tag}_bcr", [1, 256], F32))
    bc_col = es(nc.sbuf_tensor(f"{tag}_bcc", [64, 4], F32))
    nr_row = es(nc.sbuf_tensor(f"{tag}_nr", [1, 512], F32))
    rn_row = es(nc.sbuf_tensor(f"{tag}_rn", [1, 512], F32))
    xnew = es(nc.sbuf_tensor(f"{tag}_xnew", [128, 1024], F32))
    lnb = alloc_ln_bufs(ctx, es, tag)
    xt = lnb["xt"]

    ps = [es(nc.psum_tensor(f"{tag}_ps{i}", [128, 512], F32)) for i in range(2)]
    pn = es(nc.psum_tensor(f"{tag}_pn", [128, 512], F32))
    pdv = [es(nc.psum_tensor(f"{tag}_pdv{i}", [128, 512], F32)) for i in range(2)]
    pt = [es(nc.psum_tensor(f"{tag}_pt{i}", [128, 1024], BF16)) for i in range(2)]
    ones_col = es(nc.sbuf_tensor(f"{tag}_onesc", [128, 1], BF16))

    aT_hbm = nc.dram_tensor(f"{tag}_aThbm", [KC * 128, T], BF16, kind="Internal").ap()
    aT_c = aT_hbm.rearrange("(c p) t -> c p t", p=128)

    wq_hbm = W["s3_wqkvT"].rearrange("(c p) n -> c p n", p=128)
    wo_hbm = W["s3_woutT"].rearrange("(c p) n -> c p n", p=128)
    with nc.Block() as blk:
        @blk.gpsimd
        def _(g):
            for kc in range(KC):
                sd.inc(g.dma_start(wqkvT[:, kc * 3072:(kc + 1) * 3072], wq_hbm[kc]), 16)
                sd.inc(g.dma_start(woutT[:, kc * 1024:(kc + 1) * 1024], wo_hbm[kc]), 16)
            sd.inc(g.dma_start(bout_row[:], W["s3_bout"][None, :]), 16)
            sd.inc(g.dma_start(gb_row[:], W["s3_gb"][None, :]), 16)
            g.memset(ones_col[:], 1.0)

        @blk.vector
        def _(v):
            v.wait_ge(sd.h, sd.n)

    for sq_ in range(NSEQ):
        tiles = range(sq_ * TPS, (sq_ + 1) * TPS)
        emit_lnaT_phase(ctx, lnb, x_t, aT_c, pt, tiles)

        for hg in range(4):
            for tg in range(4):
                sda = sd.n
                so_w = so.n
                with nc.Block() as blk:
                    @blk.sync
                    def _(sy):
                        sy.wait_ge(so.h, so_w)
                        for kc in range(KC):
                            sd.inc(sy.dma_start(aTg[:, kc * 512:(kc + 1) * 512],
                                                aT_c[kc][:, sq_ * S + tg * 512: sq_ * S + tg * 512 + 512]), 16)
                st0, sv0 = st.n, sv.n
                with nc.Block() as blk:
                    @blk.tensor
                    def _(t_):
                        t_.wait_ge(sd.h, sda + 16 * KC)
                        idx = 0
                        for (dst, oc) in [(qT_hg, hg * 2), (qT_hg, hg * 2 + 1),
                                          (kT_hg, 8 + hg * 2), (kT_hg, 8 + hg * 2 + 1)]:
                            pj = ps[idx % 2]
                            if idx >= 2:
                                t_.wait_ge(sv.h, sv0 + idx - 1)
                            for kc in range(KC):
                                last = t_.matmul(pj[:], wqkvT[:, kc * 3072 + oc * 128: kc * 3072 + oc * 128 + 128],
                                                 aTg[:, kc * 512:(kc + 1) * 512],
                                                 start=(kc == 0), stop=(kc == KC - 1))
                            st.inc(last)
                            idx += 1
                        for ti in range(4):
                            pj = pdv[ti % 2]
                            if ti >= 2:
                                t_.wait_ge(sv.h, sv0 + 4 + ti - 1)
                            for kc in range(KC):
                                last = t_.matmul(pj[:, 0:256],
                                                 aTg[:, kc * 512 + ti * 128: kc * 512 + ti * 128 + 128],
                                                 wqkvT[:, kc * 3072 + 2048 + hg * 256: kc * 3072 + 2048 + hg * 256 + 256],
                                                 start=(kc == 0), stop=(kc == KC - 1))
                            st.inc(last)

                    @blk.vector
                    def _(v):
                        idx = 0
                        for (dst, lc) in [(qT_hg, 0), (qT_hg, 1), (kT_hg, 0), (kT_hg, 1)]:
                            v.wait_ge(st.h, st0 + idx + 1)
                            sv.inc(v.tensor_copy(dst[:, lc * S + tg * 512: lc * S + tg * 512 + 512],
                                                 ps[idx % 2][:]))
                            idx += 1
                        for ti in range(4):
                            v.wait_ge(st.h, st0 + 4 + ti + 1)
                            sv.inc(v.tensor_copy(v_hg[:, (tg * 4 + ti) * 256:(tg * 4 + ti) * 256 + 256],
                                                 pdv[ti % 2][:, 0:256]))

            st0, sv0, sa0 = st.n, sv.n, sa.n
            with nc.Block() as blk:
                @blk.tensor
                def _(t_):
                    for tkc in range(16):
                        st.inc(t_.matmul(pn[0:1, 0:256], ones_col[:],
                                         v_hg[:, tkc * 256:(tkc + 1) * 256],
                                         start=(tkc == 0), stop=(tkc == 15)))

                @blk.vector
                def _(v):
                    v.wait_ge(st.h, st0 + 16)
                    sv.inc(v.tensor_copy(csum_r[:], pn[0:1, 0:256]))
                    for lh in range(4):
                        v.wait_ge(sv.h, sv0 + 1 + lh)
                        sv.inc(v.tensor_single_scalar(bc_row[0:1, lh * 64:(lh + 1) * 64],
                                                      csum_r[0:1, lh * 64:(lh + 1) * 64],
                                                      gb_row[0:1, 16 + hg * 4 + lh: 17 + hg * 4 + lh],
                                                      ALU.mult))
            st0, sv0 = st.n, sv.n
            with nc.Block() as blk:
                @blk.tensor
                def _(t_):
                    for lh in range(4):
                        if lh >= 2:
                            t_.wait_ge(sv.h, sv0 + lh - 1)
                        st.inc(t_.matmul(pdv[lh % 2][0:64, 0:1],
                                         bc_row[0:1, lh * 64:(lh + 1) * 64],
                                         ctx.ones32[0:1, 0:1], start=True, stop=True))

                @blk.vector
                def _(v):
                    for lh in range(4):
                        v.wait_ge(st.h, st0 + lh + 1)
                        sv.inc(v.tensor_copy(bc_col[:, lh:lh + 1], pdv[lh % 2][0:64, 0:1]))

            for lh in range(4):
                h = hg * 4 + lh
                lc = lh // 2
                off = (lh % 2) * 64
                for tg in range(4):
                    tq0 = tg * 512
                    st0, sv0, sa0 = st.n, sv.n, sa.n
                    with nc.Block() as blk:
                        @blk.tensor
                        def _(t_):
                            for tkc in range(16):
                                pj = ps[tkc % 2]
                                if tkc >= 2:
                                    t_.wait_ge(sv.h, sv0 + tkc - 1)
                                st.inc(t_.matmul(pj[:], kT_hg[off:off + 64, lc * S + tkc * 128: lc * S + tkc * 128 + 128],
                                                 qT_hg[off:off + 64, lc * S + tq0: lc * S + tq0 + 512],
                                                 start=True, stop=True))
                            for tkc in range(16):
                                t_.wait_ge(sa.h, sa0 + tkc + 1)
                                st.inc(t_.matmul(pn[0:1, 0:512], ones_col[:], sq_bf[:],
                                                 start=(tkc == 0), stop=(tkc == 15)))
                            t_.wait_ge(sv.h, sv0 + 19)
                            st.inc(t_.matmul(ps[0][0:64, 0:512],
                                             ctx.ones32[0:1, 0:64], rn_row[:],
                                             start=True, stop=True))
                            for tkc in range(16):
                                last = t_.matmul(pdv[0][0:64, 0:512],
                                                 v_hg[:, tkc * 256 + lh * 64: tkc * 256 + lh * 64 + 64],
                                                 dotsT[:, tkc * 512:(tkc + 1) * 512],
                                                 start=(tkc == 0), stop=(tkc == 15))
                            st.inc(last)

                        @blk.vector
                        def _(v):
                            for tkc in range(16):
                                v.wait_ge(st.h, st0 + tkc + 1)
                                sv.inc(v.tensor_copy(dotsT[:, tkc * 512:(tkc + 1) * 512],
                                                     ps[tkc % 2][:]))
                            v.wait_ge(sa.h, sa0 + 17)
                            sv.inc(v.tensor_scalar_max(nr_row[:], nr_row[:], 1e-12))
                            v.wait_ge(sv.h, sv0 + 17)
                            sv.inc(v.reciprocal(rn_row[:], nr_row[:]))
                            v.wait_ge(sv.h, sv0 + 18)
                            sv.inc(v.tensor_single_scalar(rn_row[:], rn_row[:],
                                                          gb_row[0:1, h:h + 1], ALU.mult))
                            v.wait_ge(st.h, st0 + 33)
                            sv.inc(v.tensor_copy(rn_b[:], ps[0][0:64, 0:512]))
                            v.wait_ge(st.h, st0 + 34)
                            sv.inc(v.tensor_mul(rn_b[:], pdv[0][0:64, 0:512], rn_b[:]))
                            v.wait_ge(sv.h, sv0 + 21)
                            sv.inc(v.tensor_scalar_add(
                                attnvT[(h % 2) * 64:(h % 2) * 64 + 64,
                                       (h // 2) * S + tq0:(h // 2) * S + tq0 + 512],
                                rn_b[:], bc_col[:, lh:lh + 1]))

                        @blk.scalar
                        def _(sc):
                            for tkc in range(16):
                                sc.wait_ge(sv.h, sv0 + tkc + 1)
                                if tkc >= 1:
                                    sc.wait_ge(st.h, st0 + 16 + tkc)
                                sa.inc(sc.activation(sq_bf[:], dotsT[:, tkc * 512:(tkc + 1) * 512],
                                                     AF.Square))
                            sc.wait_ge(st.h, st0 + 32)
                            sa.inc(sc.activation(nr_row[:], pn[0:1, 0:512], AF.Sqrt))

        for i in range(TPS):
            g_tile = sq_ * TPS + i
            st0, sv0 = st.n, sv.n
            sdx = sd.n
            so0 = so.n
            with nc.Block() as blk:
                @blk.sync
                def _(sy):
                    sd.inc(sy.dma_start(xt[:], x_t[g_tile]), 16)

                @blk.tensor
                def _(t_):
                    for half in range(2):
                        pj = ps[half]
                        for mc in range(KC):
                            t_.matmul(pj[:], attnvT[:, mc * S + i * 128: mc * S + i * 128 + 128],
                                      woutT[:, mc * 1024 + half * 512: mc * 1024 + half * 512 + 512],
                                      start=(mc == 0), stop=False)
                        st.inc(t_.matmul(pj[:], ctx.ones_row[:],
                                         bout_row[:, half * 512:half * 512 + 512],
                                         start=False, stop=True))

                @blk.vector
                def _(v):
                    v.wait_ge(st.h, st0 + 2)
                    v.wait_ge(sd.h, sdx + 16)
                    v.wait_ge(so.h, so0)
                    for half in range(2):
                        sv.inc(v.tensor_add(xnew[:, half * 512:half * 512 + 512],
                                            ps[half][:],
                                            xt[:, half * 512:half * 512 + 512]))

                @blk.gpsimd
                def _(gp):
                    gp.wait_ge(sv.h, sv.n)
                    so.inc(gp.dma_start(xo_t[g_tile], xnew[:]), 16)

    with nc.Block() as blk:
        @blk.sync
        def _(sy):
            sy.wait_ge(so.h, so.n)





def emit_s5(ctx, es, resid_in, resid_out, W):
    nc = ctx.nc
    sd, so, st, sv, sa = ctx.sd, ctx.so, ctx.st, ctx.sv, ctx.sa
    tag = "s5"

    x_t = resid_in.rearrange("(n p) d -> n p d", p=128)
    xo_t = resid_out.rearrange("(n p) d -> n p d", p=128)

    wqkv = es(nc.sbuf_tensor(f"{tag}_wqkv", [128, KC * 3072], BF16))
    wout = es(nc.sbuf_tensor(f"{tag}_wout", [128, KC * 1024], BF16))
    bqc = es(nc.sbuf_tensor(f"{tag}_bqc", [128, 24], F32))
    bv_row = es(nc.sbuf_tensor(f"{tag}_bvr", [1, 1024], BF16))
    bout_row = es(nc.sbuf_tensor(f"{tag}_bout", [1, 1024], BF16))
    qT_hg = es(nc.sbuf_tensor(f"{tag}_qT", [128, 2 * S], BF16))
    kT_hg = es(nc.sbuf_tensor(f"{tag}_kT", [128, 2 * S], BF16))
    v_hg = es(nc.sbuf_tensor(f"{tag}_v", [128, 16 * 256], BF16))
    PT = es(nc.sbuf_tensor(f"{tag}_PT", [128, 16 * 512], BF16))
    attnoT = es(nc.sbuf_tensor(f"{tag}_attnoT", [128, KC * S], BF16))
    f_b = es(nc.sbuf_tensor(f"{tag}_fb", [64, 512], F32))
    aTg = es(nc.sbuf_tensor(f"{tag}_aTg", [128, KC * 512], BF16))
    mxcols = es(nc.sbuf_tensor(f"{tag}_mxc", [128, 16], F32))
    mxa = es(nc.sbuf_tensor(f"{tag}_mxa", [128, 1], F32))
    mrow = es(nc.sbuf_tensor(f"{tag}_mrow", [1, 1], F32))
    negM = es(nc.sbuf_tensor(f"{tag}_negM", [128, 1], F32))
    f_row = es(nc.sbuf_tensor(f"{tag}_frow", [1, 512], F32))
    xnew = es(nc.sbuf_tensor(f"{tag}_xnew", [128, 1024], F32))
    ones_col = es(nc.sbuf_tensor(f"{tag}_onesc", [128, 1], BF16))
    lnb = alloc_ln_bufs(ctx, es, tag)
    xt = lnb["xt"]

    ps = [es(nc.psum_tensor(f"{tag}_ps{i}", [128, 512], F32)) for i in range(2)]
    pn = es(nc.psum_tensor(f"{tag}_pn", [128, 512], F32))
    pdv = [es(nc.psum_tensor(f"{tag}_pdv{i}", [128, 512], F32)) for i in range(2)]
    pt = [es(nc.psum_tensor(f"{tag}_pt{i}", [128, 1024], BF16)) for i in range(2)]

    aT_hbm = nc.dram_tensor(f"{tag}_aThbm", [KC * 128, T], BF16, kind="Internal").ap()
    aT_c = aT_hbm.rearrange("(c p) t -> c p t", p=128)

    wq_hbm = W["s5_wqkv"].rearrange("(c p) n -> c p n", p=128)
    wo_hbm = W["s5_wout"].rearrange("(c p) n -> c p n", p=128)
    with nc.Block() as blk:
        @blk.gpsimd
        def _(g):
            for kc in range(KC):
                sd.inc(g.dma_start(wqkv[:, kc * 3072:(kc + 1) * 3072], wq_hbm[kc]), 16)
                sd.inc(g.dma_start(wout[:, kc * 1024:(kc + 1) * 1024], wo_hbm[kc]), 16)
            sd.inc(g.dma_start(bqc[:], W["s5_bqc"]), 16)
            sd.inc(g.dma_start(bv_row[:], W["s5_bqkv"][None, 2048:3072]), 16)
            sd.inc(g.dma_start(bout_row[:], W["s5_bout"][None, :]), 16)
            g.memset(ones_col[:], 1.0)

        @blk.vector
        def _(v):
            v.wait_ge(sd.h, sd.n)

    for sq_ in range(NSEQ):
        tiles = range(sq_ * TPS, (sq_ + 1) * TPS)
        emit_lnaT_phase(ctx, lnb, x_t, aT_c, pt, tiles)

        for hg in range(4):
            for tg in range(4):
                sda = sd.n
                so_w = so.n
                with nc.Block() as blk:
                    @blk.sync
                    def _(sy):
                        sy.wait_ge(so.h, so_w)
                        for kc in range(KC):
                            sd.inc(sy.dma_start(aTg[:, kc * 512:(kc + 1) * 512],
                                                aT_c[kc][:, sq_ * S + tg * 512: sq_ * S + tg * 512 + 512]), 16)
                st0, sv0, sa0 = st.n, sv.n, sa.n
                with nc.Block() as blk:
                    @blk.tensor
                    def _(t_):
                        t_.wait_ge(sd.h, sda + 16 * KC)
                        idx = 0
                        for oc in [hg * 2, hg * 2 + 1, 8 + hg * 2, 8 + hg * 2 + 1]:
                            pj = ps[idx % 2]
                            if idx >= 2:
                                t_.wait_ge(sv.h, sv0 + idx - 1)
                            for kc in range(KC):
                                last = t_.matmul(pj[:], wqkv[:, kc * 3072 + oc * 128: kc * 3072 + oc * 128 + 128],
                                                 aTg[:, kc * 512:(kc + 1) * 512],
                                                 start=(kc == 0), stop=(kc == KC - 1))
                            st.inc(last)
                            idx += 1
                        for ti in range(4):
                            pj = pdv[ti % 2]
                            if ti >= 2:
                                t_.wait_ge(sv.h, sv0 + 4 + ti - 1)
                            for kc in range(KC):
                                t_.matmul(pj[:, 0:256],
                                          aTg[:, kc * 512 + ti * 128: kc * 512 + ti * 128 + 128],
                                          wqkv[:, kc * 3072 + 2048 + hg * 256: kc * 3072 + 2048 + hg * 256 + 256],
                                          start=(kc == 0), stop=False)
                            st.inc(t_.matmul(pj[:, 0:256], ctx.ones_row[:],
                                             bv_row[:, hg * 256:(hg + 1) * 256],
                                             start=False, stop=True))


                    @blk.vector
                    def _(v):
                        for idx, oc in enumerate([hg * 2, hg * 2 + 1, 8 + hg * 2, 8 + hg * 2 + 1]):
                            v.wait_ge(st.h, st0 + idx + 1)
                            if idx < 2:
                                sv.inc(v.tensor_scalar(qT_hg[:, idx * S + tg * 512: idx * S + tg * 512 + 512],
                                                       ps[idx % 2][:], bqc[:, oc:oc + 1], 0.125,
                                                       ALU.add, ALU.mult))
                            else:
                                sv.inc(v.tensor_scalar_add(kT_hg[:, (idx - 2) * S + tg * 512: (idx - 2) * S + tg * 512 + 512],
                                                           ps[idx % 2][:], bqc[:, oc:oc + 1]))
                        for ti in range(4):
                            v.wait_ge(st.h, st0 + 4 + ti + 1)
                            sv.inc(v.tensor_copy(v_hg[:, (tg * 4 + ti) * 256:(tg * 4 + ti) * 256 + 256],
                                                 pdv[ti % 2][:, 0:256]))

            for lh in range(4):
                h = hg * 4 + lh
                lc = lh // 2
                off = (lh % 2) * 64
                for tg in range(4):
                    tq0 = tg * 512
                    nk = tg * 4 + 4
                    st0, sv0 = st.n, sv.n
                    with nc.Block() as blk:
                        @blk.tensor
                        def _(t_):
                            for tkc in range(nk):
                                pj = ps[tkc % 2]
                                if tkc >= 2:
                                    t_.wait_ge(sv.h, sv0 + tkc - 1)
                                st.inc(t_.matmul(pj[:], kT_hg[off:off + 64, lc * S + tkc * 128: lc * S + tkc * 128 + 128],
                                                 qT_hg[off:off + 64, lc * S + tq0: lc * S + tq0 + 512],
                                                 start=True, stop=True))

                        @blk.vector
                        def _(v):
                            for tkc in range(nk):
                                v.wait_ge(st.h, st0 + tkc + 1)
                                sv.inc(v.reduce_max(mxcols[:, tkc:tkc + 1],
                                                    ps[tkc % 2][:], axis=AX.X))
                            v.wait_ge(sv.h, sv0 + nk)
                            sv.inc(v.reduce_max(mxa[:], mxcols[:, 0:nk], axis=AX.X))
                    st0, sv0 = st.n, sv.n
                    with nc.Block() as blk:
                        @blk.tensor
                        def _(t_):
                            st.inc(t_.transpose(pn[0:1, 0:128], mxa[:], ctx.ident32[:]))

                        @blk.vector
                        def _(v):
                            v.wait_ge(st.h, st0 + 1)
                            sv.inc(v.reduce_max(mrow[:], pn[0:1, 0:128], axis=AX.X))
                            v.wait_ge(sv.h, sv0 + 1)
                            sv.inc(v.tensor_scalar_mul(mrow[:], mrow[:], -1.0))
                    st0, sv0 = st.n, sv.n
                    with nc.Block() as blk:
                        @blk.tensor
                        def _(t_):
                            st.inc(t_.matmul(pn[:, 0:1], ctx.ones32[:], mrow[:],
                                             start=True, stop=True))

                        @blk.vector
                        def _(v):
                            v.wait_ge(st.h, st0 + 1)
                            sv.inc(v.tensor_copy(negM[:], pn[:, 0:1]))
                    st0, sv0, sa0 = st.n, sv.n, sa.n
                    with nc.Block() as blk:
                        @blk.tensor
                        def _(t_):
                            for tkc in range(nk):
                                pj = ps[tkc % 2]
                                if tkc >= 2:
                                    t_.wait_ge(sa.h, sa0 + tkc - 1)
                                st.inc(t_.matmul(pj[:], kT_hg[off:off + 64, lc * S + tkc * 128: lc * S + tkc * 128 + 128],
                                                 qT_hg[off:off + 64, lc * S + tq0: lc * S + tq0 + 512],
                                                 start=True, stop=True))

                        @blk.scalar
                        def _(sc):
                            for tkc in range(nk):
                                sc.wait_ge(st.h, st0 + tkc + 1)
                                sa.inc(sc.activation(PT[:, tkc * 512:(tkc + 1) * 512],
                                                     ps[tkc % 2][:], AF.Exp,
                                                     bias=negM[:]))

                        @blk.vector
                        def _(v):
                            for tkc in range(nk):
                                v.wait_ge(sa.h, sa0 + tkc + 1)
                                t0i = tq0 // 128
                                for j in range(4):
                                    tqt = t0i + j
                                    if tkc == tqt:
                                        sv.inc(v.tensor_mul(PT[:, tkc * 512 + j * 128: tkc * 512 + j * 128 + 128],
                                                            PT[:, tkc * 512 + j * 128: tkc * 512 + j * 128 + 128],
                                                            ctx.ltri[:]))
                                    elif tkc > tqt:
                                        sv.inc(v.memset(PT[:, tkc * 512 + j * 128: tkc * 512 + j * 128 + 128], 0.0))
                    st0, sv0 = st.n, sv.n
                    with nc.Block() as blk:
                        @blk.tensor
                        def _(t_):
                            for tkc in range(nk):
                                st.inc(t_.matmul(pn[0:1, 0:512], ones_col[:],
                                                 PT[:, tkc * 512:(tkc + 1) * 512],
                                                 start=(tkc == 0), stop=(tkc == nk - 1)))
                            for tkc in range(nk):
                                last = t_.matmul(pdv[0][0:64, 0:512],
                                                 v_hg[:, tkc * 256 + lh * 64: tkc * 256 + lh * 64 + 64],
                                                 PT[:, tkc * 512:(tkc + 1) * 512],
                                                 start=(tkc == 0), stop=(tkc == nk - 1))
                            st.inc(last)

                        @blk.vector
                        def _(v):
                            v.wait_ge(st.h, st0 + nk)
                            sv.inc(v.reciprocal(f_row[:], pn[0:1, 0:512]))
                    st0, sv0 = st.n, sv.n
                    with nc.Block() as blk:
                        @blk.tensor
                        def _(t_):
                            st.inc(t_.matmul(ps[0][0:64, 0:512],
                                             ctx.ones32[0:1, 0:64], f_row[:],
                                             start=True, stop=True))

                        @blk.vector
                        def _(v):
                            v.wait_ge(st.h, st0 + 1)
                            sv.inc(v.tensor_copy(f_b[:], ps[0][0:64, 0:512]))
                            v.wait_ge(sv.h, sv0 + 1)
                            sv.inc(v.tensor_mul(
                                attnoT[(h % 2) * 64:(h % 2) * 64 + 64,
                                       (h // 2) * S + tq0:(h // 2) * S + tq0 + 512],
                                pdv[0][0:64, 0:512], f_b[:]))

        for i in range(TPS):
            g_tile = sq_ * TPS + i
            st0, sv0 = st.n, sv.n
            sdx = sd.n
            so0 = so.n
            with nc.Block() as blk:
                @blk.sync
                def _(sy):
                    sd.inc(sy.dma_start(xt[:], x_t[g_tile]), 16)

                @blk.tensor
                def _(t_):
                    for half in range(2):
                        pj = ps[half]
                        for mc in range(KC):
                            t_.matmul(pj[:], attnoT[:, mc * S + i * 128: mc * S + i * 128 + 128],
                                      wout[:, mc * 1024 + half * 512: mc * 1024 + half * 512 + 512],
                                      start=(mc == 0), stop=False)
                        st.inc(t_.matmul(pj[:], ctx.ones_row[:],
                                         bout_row[:, half * 512:half * 512 + 512],
                                         start=False, stop=True))

                @blk.vector
                def _(v):
                    v.wait_ge(st.h, st0 + 2)
                    v.wait_ge(sd.h, sdx + 16)
                    v.wait_ge(so.h, so0)
                    for half in range(2):
                        sv.inc(v.tensor_add(xnew[:, half * 512:half * 512 + 512],
                                            ps[half][:],
                                            xt[:, half * 512:half * 512 + 512]))

                @blk.gpsimd
                def _(gp):
                    gp.wait_ge(sv.h, sv.n)
                    so.inc(gp.dma_start(xo_t[g_tile], xnew[:]), 16)

    with nc.Block() as blk:
        @blk.sync
        def _(sy):
            sy.wait_ge(so.h, so.n)





S6_DEBUG_STAGE = "all"
S6_DEBUG_DUMP = False


def emit_s6(ctx, es, resid_in, resid_out, W, tag, scratch, quant):
    nc = ctx.nc
    sd, so, st, sv, sa = ctx.sd, ctx.so, ctx.st, ctx.sv, ctx.sa
    NG = NT // 4

    x_t = resid_in.rearrange("(n p) d -> n p d", p=128)
    xo_t = resid_out.rearrange("(n p) d -> n p d", p=128)
    t1g_c = scratch["t1g"].rearrange("(c p) n -> c p n", p=128)
    t2g_c = scratch["t2g"].rearrange("(c p) n -> c p n", p=128)

    if quant and S6_DEBUG_STAGE != "compute":
        import contextlib as _ctl
        with _ctl.ExitStack() as qs:
            qe = lambda cm: qs.enter_context(cm)
            wfull = qe(nc.sbuf_tensor(f"{tag}_wfull", [128, KC * FF], F32))
            tabs = qe(nc.sbuf_tensor(f"{tag}_tabs", [128, FF], F32))
            tsgn = qe(nc.sbuf_tensor(f"{tag}_tsgn", [128, FF], BF16))
            traw = qe(nc.sbuf_tensor(f"{tag}_traw", [128, FF], BF16))
            tg_buf = qe(nc.sbuf_tensor(f"{tag}_tgbuf", [128, FF], BF16))
            sacc = qe(nc.sbuf_tensor(f"{tag}_sacc", [128, 1], F32))
            scol = qe(nc.sbuf_tensor(f"{tag}_scol", [128, 1], F32))
            srow = qe(nc.sbuf_tensor(f"{tag}_srow", [1, 4], F32))
            thrcol = qe(nc.sbuf_tensor(f"{tag}_thrcol", [128, 1], F32))
            invcol = qe(nc.sbuf_tensor(f"{tag}_invcol", [128, 1], F32))
            g1c = qe(nc.sbuf_tensor(f"{tag}_g1c", [128, 8], F32))
            c1cb = qe(nc.sbuf_tensor(f"{tag}_c1cb", [128, 8], BF16))
            g2s = qe(nc.sbuf_tensor(f"{tag}_g2s", [128, 32], F32))
            c2cb = qe(nc.sbuf_tensor(f"{tag}_c2cb", [128, 32], BF16))
            b1c_s = qe(nc.sbuf_tensor(f"{tag}_b1c", [128, 32], F32))
            b2row = qe(nc.sbuf_tensor(f"{tag}_b2row", [1, 1024], F32))
            rrow = qe(nc.sbuf_tensor(f"{tag}_rrow", [1, FF], F32))
            rcol = qe(nc.sbuf_tensor(f"{tag}_rcol", [128, 32], F32))
            onescb = qe(nc.sbuf_tensor(f"{tag}_onescb", [128, 1], BF16))
            prr = [qe(nc.psum_tensor(f"{tag}_prr{i}", [128, 512], F32)) for i in range(8)]

            with nc.Block() as blk:
                @blk.gpsimd
                def _(g):
                    sd.inc(g.dma_start(g1c[:], W["s6_g1c"]), 16)
                    sd.inc(g.dma_start(c1cb[:], W["s6_b1lc"]), 16)
                    sd.inc(g.dma_start(g2s[:], W["s6_g2c"]), 16)
                    sd.inc(g.dma_start(c2cb[:], W["s6_b2lc"]), 16)
                    sd.inc(g.dma_start(b1c_s[:], W["s6_b1c"]), 16)
                    sd.inc(g.dma_start(b2row[:], W["s6_b2"][None, :]), 16)
                    g.memset(onescb[:], 1.0)

                @blk.vector
                def _(v):
                    v.wait_ge(sd.h, sd.n)

            for wi in range(2):
                nchunks = KC if wi == 0 else 32
                width = FF if wi == 0 else 1024
                whbm = (W["s6_w1T"] if wi == 0 else W["s6_w2T"]).rearrange(
                    "(c p) n -> c p n", p=128)
                nelem = float(4096 * 1024)
                sdw = sd.n
                with nc.Block() as blk:
                    @blk.sync
                    def _(sy):
                        for mc in range(nchunks):
                            sd.inc(sy.dma_start(wfull[:, mc * width:(mc + 1) * width],
                                                whbm[mc]), 16)
                sv0, sa0 = sv.n, sa.n
                with nc.Block() as blk:
                    @blk.vector
                    def _(v):
                        v.wait_ge(sd.h, sdw + 16 * nchunks)
                        for mc in range(nchunks):
                            v.wait_ge(sv.h, sv.n)
                            if mc == 0:
                                sv.inc(v.reduce_sum(sacc[:], wfull[:, 0:width], axis=AX.X,
                                                    apply_absolute_value=True))
                            else:
                                sv.inc(v.reduce_sum(scol[:], wfull[:, mc * width:(mc + 1) * width],
                                                    axis=AX.X, apply_absolute_value=True))
                                v.wait_ge(sv.h, sv.n)
                                sv.inc(v.tensor_add(sacc[:], sacc[:], scol[:]))
                st0, sv0 = st.n, sv.n
                with nc.Block() as blk:
                    @blk.tensor
                    def _(t_):
                        st.inc(t_.transpose(prr[0][0:1, 0:128], sacc[:], ctx.ident32[:]))

                    @blk.vector
                    def _(v):
                        v.wait_ge(st.h, st0 + 1)
                        sv.inc(v.reduce_sum(srow[0:1, 0:1], prr[0][0:1, 0:128], axis=AX.X))
                        v.wait_ge(sv.h, sv0 + 1)
                        sv.inc(v.tensor_scalar(srow[0:1, 0:1], srow[0:1, 0:1],
                                               1.0 / nelem, 1e-5, ALU.mult, ALU.max))
                        v.wait_ge(sv.h, sv0 + 2)
                        sv.inc(v.tensor_scalar_mul(srow[0:1, 2:3], srow[0:1, 0:1], 0.5))
                        sv.inc(v.reciprocal(srow[0:1, 1:2], srow[0:1, 0:1]))
                st0, sv0 = st.n, sv.n
                with nc.Block() as blk:
                    @blk.tensor
                    def _(t_):
                        st.inc(t_.matmul(prr[0][:, 0:1], ctx.ones32[:], srow[0:1, 2:3],
                                         start=True, stop=True))
                        st.inc(t_.matmul(prr[1][:, 0:1], ctx.ones32[:], srow[0:1, 0:1],
                                         start=True, stop=True))

                    @blk.vector
                    def _(v):
                        v.wait_ge(st.h, st0 + 2)
                        sv.inc(v.tensor_copy(thrcol[:], prr[0][:, 0:1]))
                        sv.inc(v.tensor_copy(invcol[:], prr[1][:, 0:1]))
                if wi == 0:
                    with nc.Block() as blk:
                        @blk.sync
                        def _(sy):
                            so.inc(sy.dma_start(scratch["rbs1"], invcol[:]), 16)
                else:
                    sv0 = sv.n
                    with nc.Block() as blk:
                        @blk.vector
                        def _(v):
                            sv.inc(v.tensor_single_scalar(g2s[:], g2s[:], invcol[:], ALU.mult))

                for mc in range(nchunks):
                    sv0, sa0, st0 = sv.n, sa.n, st.n
                    so_prev_q = so.n
                    wsl = wfull[:, mc * width:(mc + 1) * width]
                    with nc.Block() as blk:
                        @blk.scalar
                        def _(sc):
                            sa.inc(sc.activation(tabs[:, 0:width], wsl, AF.Abs))
                            sa.inc(sc.activation(tsgn[:, 0:width], wsl, AF.Sign))

                        @blk.vector
                        def _(v):
                            v.wait_ge(so.h, so_prev_q)
                            v.wait_ge(sa.h, sa0 + 1)
                            sv.inc(v.tensor_single_scalar(tabs[:, 0:width], tabs[:, 0:width],
                                                          thrcol[:], ALU.is_ge))
                            v.wait_ge(sa.h, sa0 + 2)
                            v.wait_ge(sv.h, sv0 + 1)
                            sv.inc(v.tensor_mul(traw[:, 0:width], tabs[:, 0:width],
                                                tsgn[:, 0:width]))
                            v.wait_ge(sv.h, sv0 + 2)
                            if wi == 0:
                                sv.inc(v.tensor_single_scalar(tg_buf[:, 0:width],
                                                              traw[:, 0:width],
                                                              g1c[:, mc:mc + 1], ALU.mult))
                            else:
                                sv.inc(v.tensor_single_scalar(tg_buf[:, 0:width],
                                                              traw[:, 0:width],
                                                              g2s[:, mc:mc + 1], ALU.mult))
                    st0 = st.n
                    with nc.Block() as blk:
                        @blk.sync
                        def _(sy):
                            if wi == 0:
                                so.inc(sy.dma_start(t1g_c[mc], tg_buf[:, 0:width]), 16)
                            else:
                                so.inc(sy.dma_start(t2g_c[mc], tg_buf[:, 0:width]), 16)

                        @blk.tensor
                        def _(t_):
                            nsl = width // 512
                            bcol = c1cb if wi == 0 else c2cb
                            for ns in range(nsl):
                                st.inc(t_.matmul(prr[ns][0:1, 0:512],
                                                 bcol[:, mc:mc + 1],
                                                 traw[:, ns * 512:(ns + 1) * 512],
                                                 start=(mc == 0), stop=(mc == nchunks - 1)))
                            if wi == 1:
                                for oh in range(2):
                                    st.inc(t_.matmul(prr[2 + oh][0:1, 0:512],
                                                     onescb[:],
                                                     tg_buf[:, oh * 512:(oh + 1) * 512],
                                                     start=(mc == 0), stop=(mc == nchunks - 1)))
                if wi == 0:
                    st0, sv0 = st.n, sv.n
                    with nc.Block() as blk:
                        @blk.vector
                        def _(v):
                            for ns in range(8):
                                sv.inc(v.tensor_copy(rrow[0:1, ns * 512:(ns + 1) * 512],
                                                     prr[ns][0:1, 0:512]))
                    st0, sv0 = st.n, sv.n
                    with nc.Block() as blk:
                        @blk.tensor
                        def _(t_):
                            for a in range(32):
                                if a >= 8:
                                    t_.wait_ge(sv.h, sv0 + a - 7)
                                st.inc(t_.matmul(prr[a % 8][:, 0:1],
                                                 rrow[0:1, a * 128:(a + 1) * 128],
                                                 ctx.ones32[0:1, 0:1], start=True, stop=True))

                        @blk.vector
                        def _(v):
                            for a in range(32):
                                v.wait_ge(st.h, st0 + a + 1)
                                sv.inc(v.tensor_copy(rcol[:, a:a + 1], prr[a % 8][:, 0:1]))
                    sv0 = sv.n
                    with nc.Block() as blk:
                        @blk.vector
                        def _(v):
                            sv.inc(v.tensor_single_scalar(rcol[:], rcol[:], invcol[:], ALU.mult))
                            v.wait_ge(sv.h, sv0 + 1)
                            sv.inc(v.tensor_add(rcol[:], rcol[:], b1c_s[:]))

                        @blk.gpsimd
                        def _(gp):
                            gp.wait_ge(sv.h, sv.n)
                            so.inc(gp.dma_start(scratch["bcomb"], rcol[:]), 16)
                else:
                    sv0 = sv.n
                    with nc.Block() as blk:
                        @blk.vector
                        def _(v):
                            for oh in range(2):
                                sv.inc(v.tensor_copy(rrow[0:1, oh * 512:(oh + 1) * 512],
                                                     prr[oh][0:1, 0:512]))
                                sv.inc(v.tensor_copy(rrow[0:1, 1024 + oh * 512:1024 + (oh + 1) * 512],
                                                     prr[2 + oh][0:1, 0:512]))
                            v.wait_ge(sv.h, sv0 + 4)
                            sv.inc(v.tensor_single_scalar(rrow[0:1, 0:1024], rrow[0:1, 0:1024],
                                                          srow[0:1, 0:1], ALU.mult))
                            v.wait_ge(sv.h, sv0 + 5)
                            sv.inc(v.tensor_add(rrow[0:1, 0:1024], rrow[0:1, 0:1024],
                                                b2row[:]))

                        @blk.gpsimd
                        def _(gp):
                            gp.wait_ge(sv.h, sv.n)
                            so.inc(gp.dma_start(scratch["ep"], rrow[0:1, 0:1024]), 16)
                            so.inc(gp.dma_start(scratch["cp"], rrow[0:1, 1024:2048]), 16)
            with nc.Block() as blk:
                @blk.sync
                def _(sy):
                    sy.wait_ge(so.h, so.n)

    if S6_DEBUG_DUMP and quant:
        d_t1g = nc.dram_tensor("dbg_t1g", [KC * 128, FF], BF16, kind="ExternalOutput").ap()
        d_t2g = nc.dram_tensor("dbg_t2g", [32 * 128, 1024], BF16, kind="ExternalOutput").ap()
        d_bcomb = nc.dram_tensor("dbg_bcomb", [128, 32], F32, kind="ExternalOutput").ap()
        d_rbs1 = nc.dram_tensor("dbg_rbs1", [128, 1], F32, kind="ExternalOutput").ap()
        d_cp = nc.dram_tensor("dbg_cp", [1, 1024], BF16, kind="ExternalOutput").ap()
        d_ep = nc.dram_tensor("dbg_ep", [1, 1024], BF16, kind="ExternalOutput").ap()
        with nc.Block() as blk:
            @blk.sync
            def _(sy):
                so.inc(sy.dma_start(d_t1g, scratch["t1g"]), 16)
                so.inc(sy.dma_start(d_t2g, scratch["t2g"]), 16)
                so.inc(sy.dma_start(d_bcomb, scratch["bcomb"]), 16)
                so.inc(sy.dma_start(d_rbs1, scratch["rbs1"]), 16)
                so.inc(sy.dma_start(d_cp, scratch["cp"]), 16)
                so.inc(sy.dma_start(d_ep, scratch["ep"]), 16)
                sy.wait_ge(so.h, so.n)

    if S6_DEBUG_STAGE == "quant":
        return
    t1g = es(nc.sbuf_tensor(f"{tag}_t1g", [128, KC * FF], BF16))
    t2g = es(nc.sbuf_tensor(f"{tag}_t2g", [128, 32 * 1024], BF16))
    hT = es(nc.sbuf_tensor(f"{tag}_hT", [128, 32 * 512], BF16))
    aTg = es(nc.sbuf_tensor(f"{tag}_aTg", [128, KC * 512], BF16))
    sqbf = [es(nc.sbuf_tensor(f"{tag}_sqbf{i}", [128, 512], BF16)) for i in range(2)]
    bcomb = es(nc.sbuf_tensor(f"{tag}_bcomb", [128, 32], F32))
    rbs1 = es(nc.sbuf_tensor(f"{tag}_rbs1", [128, 1], F32))
    cp_bf = es(nc.sbuf_tensor(f"{tag}_cpbf", [1, 1024], BF16))
    ep_bf = es(nc.sbuf_tensor(f"{tag}_epbf", [1, 1024], BF16))
    murow = es(nc.sbuf_tensor(f"{tag}_murow", [1, 512], F32))
    ssrow = es(nc.sbuf_tensor(f"{tag}_ssrow", [1, 512], F32))
    r2row = es(nc.sbuf_tensor(f"{tag}_r2row", [1, 512], F32))
    alrow = es(nc.sbuf_tensor(f"{tag}_alrow", [1, 512], BF16))
    r2col = es(nc.sbuf_tensor(f"{tag}_r2col", [128, 4], F32))
    tmp_ps = es(nc.sbuf_tensor(f"{tag}_tmp", [128, 512], F32))
    xnew = es(nc.sbuf_tensor(f"{tag}_xnew", [128, 1024], F32))
    t1h = [es(nc.sbuf_tensor(f"{tag}_t1h{i}", [128, 512], F32)) for i in range(2)]
    onescb2 = es(nc.sbuf_tensor(f"{tag}_onescb2", [128, 1], BF16))
    lnb = alloc_ln_bufs(ctx, es, tag, sq_alias=xnew)
    lnb["a_bf2"] = es(nc.sbuf_tensor(f"{tag}_abf2", [128, 1024], F32))
    xt = lnb["xt"]

    ph = [es(nc.psum_tensor(f"{tag}_ph{i}", [128, 512], F32)) for i in range(2)]
    py = [es(nc.psum_tensor(f"{tag}_py{i}", [128, 512], F32)) for i in range(2)]
    pn1 = es(nc.psum_tensor(f"{tag}_pn1", [128, 512], F32))
    pn2 = es(nc.psum_tensor(f"{tag}_pn2", [128, 512], F32))
    pt = [es(nc.psum_tensor(f"{tag}_pt{i}", [128, 1024], BF16)) for i in range(2)]

    aT_hbm = nc.dram_tensor(f"{tag}_aThbm", [KC * 128, T], BF16, kind="Internal").ap()
    aT_c = aT_hbm.rearrange("(c p) t -> c p t", p=128)

    with nc.Block() as blk:
        @blk.gpsimd
        def _(g):
            for mc in range(KC):
                sd.inc(g.dma_start(t1g[:, mc * FF:(mc + 1) * FF], t1g_c[mc]), 16)
            for nc_ in range(32):
                sd.inc(g.dma_start(t2g[:, nc_ * 1024:(nc_ + 1) * 1024], t2g_c[nc_]), 16)
            sd.inc(g.dma_start(bcomb[:], scratch["bcomb"]), 16)
            sd.inc(g.dma_start(rbs1[:], scratch["rbs1"]), 16)
            sd.inc(g.dma_start(cp_bf[:], scratch["cp"]), 16)
            sd.inc(g.dma_start(ep_bf[:], scratch["ep"]), 16)
            g.memset(onescb2[:], 1.0)

        @blk.vector
        def _(v):
            v.wait_ge(sd.h, sd.n)

    emit_lnaT_phase(ctx, lnb, x_t, aT_c, pt, range(NT), double_ln=True)

    if S6_DEBUG_STAGE == "c_ln":
        with nc.Block() as blk:
            @blk.sync
            def _(sy):
                sy.wait_ge(so.h, so.n)
        return

    for g in range(NG if S6_DEBUG_STAGE not in ("c_h", "c_stats", "c_y") else 2):
        sda = sd.n
        so_w = so.n
        with nc.Block() as blk:
            @blk.sync
            def _(sy):
                sy.wait_ge(so.h, so_w)
                for kc in range(KC):
                    sd.inc(sy.dma_start(aTg[:, kc * 512:(kc + 1) * 512],
                                        aT_c[kc][:, g * 512:(g + 1) * 512]), 16)

        st0, sv0, sa0 = st.n, sv.n, sa.n
        h_st = {}
        stats_st = {}
        with nc.Block() as blk:
            @blk.tensor
            def _(t_):
                def emit_stats(k):
                    t_.wait_ge(sa.h, sa0 + 2 * k + 1)
                    st.inc(t_.matmul(pn1[0:1, 0:512], onescb2[:],
                                     hT[:, k * 512:(k + 1) * 512],
                                     start=(k == 0), stop=(k == 31)))
                    t_.wait_ge(sa.h, sa0 + 2 * k + 2)
                    st.inc(t_.matmul(pn2[0:1, 0:512], onescb2[:], sqbf[k % 2][:],
                                     start=(k == 0), stop=(k == 31)))
                    stats_st[k] = st.n
                t_.wait_ge(sd.h, sda + 16 * KC)
                for nc_ in range(32):
                    pj = ph[nc_ % 2]
                    if nc_ >= 2:
                        t_.wait_ge(sv.h, sv0 + nc_ - 1)
                    for kc in range(KC):
                        last = t_.matmul(pj[:], t1g[:, kc * FF + nc_ * 128: kc * FF + nc_ * 128 + 128],
                                         aTg[:, kc * 512:(kc + 1) * 512],
                                         start=(kc == 0), stop=(kc == KC - 1))
                    st.inc(last)
                    h_st[nc_] = st.n
                    if nc_ >= 2:
                        emit_stats(nc_ - 2)
                emit_stats(30)
                emit_stats(31)

            @blk.vector
            def _(v):
                for nc_ in range(32):
                    v.wait_ge(st.h, h_st[nc_])
                    if nc_ >= 2:
                        v.wait_ge(sa.h, sa0 + 2 * (nc_ - 2) + 1)
                    sv.inc(v.tensor_scalar(t1h[nc_ % 2][:], ph[nc_ % 2][:], rbs1[:],
                                           bcomb[:, nc_:nc_ + 1], ALU.mult, ALU.add))

            @blk.scalar
            def _(sc):
                for nc_ in range(32):
                    sc.wait_ge(sv.h, sv0 + nc_ + 1)
                    sa.inc(sc.activation(hT[:, nc_ * 512:(nc_ + 1) * 512],
                                         t1h[nc_ % 2][:], AF.Gelu))
                    if nc_ >= 2:
                        sc.wait_ge(st.h, stats_st[nc_ - 2])
                    sa.inc(sc.activation(sqbf[nc_ % 2][:], hT[:, nc_ * 512:(nc_ + 1) * 512],
                                         AF.Square))

        if S6_DEBUG_STAGE == "c_h":
            continue
        sv0, sa0, st0 = sv.n, sa.n, st.n
        with nc.Block() as blk:
            @blk.vector
            def _(v):
                sv.inc(v.tensor_scalar_mul(murow[:], pn1[0:1, 0:512], 1.0 / FF))
                v.wait_ge(sv.h, sv0 + 1)
                sv.inc(v.tensor_mul(ssrow[:], murow[:], murow[:]))
                v.wait_ge(sv.h, sv0 + 2)
                sv.inc(v.scalar_tensor_tensor(ssrow[:], pn2[0:1, 0:512], 1.0 / FF,
                                              ssrow[:], ALU.mult, ALU.subtract))
                v.wait_ge(sv.h, sv0 + 3)
                sv.inc(v.tensor_scalar_add(ssrow[:], ssrow[:], 1e-5))
                v.wait_ge(sa.h, sa0 + 1)
                sv.inc(v.reciprocal(r2row[:], ssrow[:]))
                v.wait_ge(sv.h, sv0 + 5)
                sv.inc(v.tensor_mul(murow[:], murow[:], r2row[:]))
                v.wait_ge(sv.h, sv0 + 6)
                sv.inc(v.tensor_scalar_mul(alrow[:], murow[:], -1.0))

            @blk.scalar
            def _(sc):
                sc.wait_ge(sv.h, sv0 + 4)
                sa.inc(sc.activation(ssrow[:], ssrow[:], AF.Sqrt))
        st0, sv0 = st.n, sv.n
        with nc.Block() as blk:
            @blk.tensor
            def _(t_):
                for ti in range(4):
                    if ti >= 2:
                        t_.wait_ge(sv.h, sv0 + ti - 1)
                    st.inc(t_.matmul(py[ti % 2][:, 0:1],
                                     r2row[0:1, ti * 128:(ti + 1) * 128],
                                     ctx.ones32[0:1, 0:1], start=True, stop=True))

            @blk.vector
            def _(v):
                for ti in range(4):
                    v.wait_ge(st.h, st0 + ti + 1)
                    sv.inc(v.tensor_copy(r2col[:, ti:ti + 1], py[ti % 2][:, 0:1]))

        if S6_DEBUG_STAGE == "c_stats":
            continue
        for ti in range(4):
            gt = g * 4 + ti
            st0, sv0 = st.n, sv.n
            sdx = sd.n
            so0 = so.n
            with nc.Block() as blk:
                @blk.sync
                def _(sy):
                    sd.inc(sy.dma_start(xt[:], x_t[gt]), 16)

                @blk.tensor
                def _(t_):
                    for mg in range(2):
                        pj = py[mg]
                        for nc_ in range(32):
                            last = t_.matmul(pj[:], hT[:, nc_ * 512 + ti * 128: nc_ * 512 + ti * 128 + 128],
                                             t2g[:, nc_ * 1024 + mg * 512: nc_ * 1024 + mg * 512 + 512],
                                             start=(nc_ == 0), stop=(nc_ == 31))
                        st.inc(last)
                        t_.matmul(pn1[:, 0:512], alrow[0:1, ti * 128:(ti + 1) * 128],
                                  cp_bf[:, mg * 512:(mg + 1) * 512],
                                  start=True, stop=False)
                        st.inc(t_.matmul(pn1[:, 0:512], ctx.ones_row[:],
                                         ep_bf[:, mg * 512:(mg + 1) * 512],
                                         start=False, stop=True))
                        t_.wait_ge(sv.h, sv0 + 2 * mg + 1)

                @blk.vector
                def _(v):
                    v.wait_ge(sd.h, sdx + 16)
                    v.wait_ge(so.h, so0)
                    for mg in range(2):
                        v.wait_ge(st.h, st0 + 2 * mg + 2)
                        if mg == 1:
                            v.wait_ge(sv.h, sv0 + 2)
                        sv.inc(v.tensor_add(tmp_ps[:], pn1[:, 0:512],
                                            xt[:, mg * 512:(mg + 1) * 512]))
                        v.wait_ge(sv.h, sv0 + 2 * mg + 1)
                        sv.inc(v.scalar_tensor_tensor(xnew[:, mg * 512:(mg + 1) * 512],
                                                      py[mg][:], r2col[:, ti:ti + 1],
                                                      tmp_ps[:], ALU.mult, ALU.add))

                @blk.gpsimd
                def _(gp):
                    gp.wait_ge(sv.h, sv.n)
                    so.inc(gp.dma_start(xo_t[gt], xnew[:]), 16)

    with nc.Block() as blk:
        @blk.sync
        def _(sy):
            sy.wait_ge(so.h, so.n)





S7_DEBUG_DUMP = False


def emit_s7(ctx, es, resid_in, resid_out, W):
    nc = ctx.nc
    sd, so, st, sv, sa = ctx.sd, ctx.so, ctx.st, ctx.sv, ctx.sa
    tag = "s7"
    dbg = {}
    if S7_DEBUG_DUMP:
        dbg["sT"] = nc.dram_tensor("dbg_sT", [128, 4 * 512], BF16, kind="ExternalOutput").ap()
        dbg["den"] = nc.dram_tensor("dbg_den", [1, 512], F32, kind="ExternalOutput").ap()
        dbg["qband"] = nc.dram_tensor("dbg_qband", [128, KC * 512], BF16, kind="ExternalOutput").ap()
        dbg["kblk"] = nc.dram_tensor("dbg_kblk", [128, KC * 128], BF16, kind="ExternalOutput").ap()
        dbg["numT"] = nc.dram_tensor("dbg_numT", [128, KC * 512], BF16, kind="ExternalOutput").ap()
        dbg["rd"] = nc.dram_tensor("dbg_rd", [1, 512], F32, kind="ExternalOutput").ap()
        dbg["v03"] = nc.dram_tensor("dbg_v03", [4 * 128, 1024], BF16, kind="ExternalOutput").ap()

    x_t = resid_in.rearrange("(n p) d -> n p d", p=128)
    xo_t = resid_out.rearrange("(n p) d -> n p d", p=128)

    wqkv = es(nc.sbuf_tensor(f"{tag}_wqkv", [128, KC * 3072], BF16))
    wgate = es(nc.sbuf_tensor(f"{tag}_wgate", [128, KC * 1024], BF16))
    wproj = es(nc.sbuf_tensor(f"{tag}_wproj", [128, KC * 1024], BF16))
    gc = es(nc.sbuf_tensor(f"{tag}_gc", [128, 8], F32))
    bc_bf = es(nc.sbuf_tensor(f"{tag}_bcbf", [128, 8], BF16))
    bq_row = es(nc.sbuf_tensor(f"{tag}_bqrow", [1, 3072], F32))
    bg_row = es(nc.sbuf_tensor(f"{tag}_bgrow", [1, 1024], F32))
    bp_row = es(nc.sbuf_tensor(f"{tag}_bprow", [1, 1024], BF16))
    rqg_row = es(nc.sbuf_tensor(f"{tag}_rqg", [1, 4096], F32))
    rqg_bf = es(nc.sbuf_tensor(f"{tag}_rqgbf", [1, 4096], BF16))
    rcols = es(nc.sbuf_tensor(f"{tag}_rcols", [128, 32], F32))
    bprojB = es(nc.sbuf_tensor(f"{tag}_bprojB", [128, 1024], F32))
    aTg = es(nc.sbuf_tensor(f"{tag}_aTg", [128, KC * 512], BF16))
    qband = es(nc.sbuf_tensor(f"{tag}_qband", [128, KC * 512], BF16))
    sT = es(nc.sbuf_tensor(f"{tag}_sT", [128, 16 * 512], BF16))
    numT = es(nc.sbuf_tensor(f"{tag}_numT", [128, KC * 512], BF16))
    kblk = [es(nc.sbuf_tensor(f"{tag}_kblk{i}", [128, KC * 128], BF16)) for i in range(2)]
    vblk = [es(nc.sbuf_tensor(f"{tag}_vblk{i}", [128, 512], BF16)) for i in range(2)]
    gateT = es(nc.sbuf_tensor(f"{tag}_gateT", [128, 512], F32))
    tmp1 = es(nc.sbuf_tensor(f"{tag}_tmp1", [128, 512], F32))
    tmp2 = es(nc.sbuf_tensor(f"{tag}_tmp2", [128, 512], F32))
    tmp3 = es(nc.sbuf_tensor(f"{tag}_tmp3", [128, 512], F32))
    kpbuf = es(nc.sbuf_tensor(f"{tag}_kpbuf", [128, 512], BF16))
    vbuf = es(nc.sbuf_tensor(f"{tag}_vbuf", [128, 1024], BF16))
    den_row = es(nc.sbuf_tensor(f"{tag}_den", [1, 512], F32))
    rd_row = es(nc.sbuf_tensor(f"{tag}_rd", [1, 512], F32))
    rdcol = es(nc.sbuf_tensor(f"{tag}_rdcol", [128, 1], F32))
    xnew = es(nc.sbuf_tensor(f"{tag}_xnew", [128, 1024], F32))
    ones_colb = es(nc.sbuf_tensor(f"{tag}_onescb", [128, 1], BF16))
    lnb = alloc_ln_bufs(ctx, es, tag, sq_alias=xnew)
    lnb["a_bf2"] = es(nc.sbuf_tensor(f"{tag}_abf2", [128, 1024], F32))
    xt = lnb["xt"]

    pfa = [es(nc.psum_tensor(f"{tag}_pf{i}", [128, 512], F32)) for i in range(6)]
    pt = [es(nc.psum_tensor(f"{tag}_pt{i}", [128, 1024], BF16)) for i in range(2)]

    aT_hbm = nc.dram_tensor(f"{tag}_aThbm", [KC * 128, T], BF16, kind="Internal").ap()
    aT_c = aT_hbm.rearrange("(c p) t -> c p t", p=128)
    kpp_hbm = nc.dram_tensor(f"{tag}_kpph", [TPS * KC * 128, 128], BF16, kind="Internal").ap()
    kpp_b = kpp_hbm.rearrange("(t c p) n -> t c p n", c=KC, p=128)
    qp_hbm = nc.dram_tensor(f"{tag}_qph", [KC * 128, S], BF16, kind="Internal").ap()
    qp_c = qp_hbm.rearrange("(c p) t -> c p t", p=128)
    v_hbm = nc.dram_tensor(f"{tag}_vh", [TPS * 128, 1024], BF16, kind="Internal").ap()
    v_tl = v_hbm.rearrange("(t p) n -> t p n", p=128)

    wq_hbm = W["s7_wqkv"].rearrange("(c p) n -> c p n", p=128)
    wg_hbm = W["s7_wgate"].rearrange("(c p) n -> c p n", p=128)
    wp_hbm = W["s7_wproj"].rearrange("(c p) n -> c p n", p=128)

    with nc.Block() as blk:
        @blk.gpsimd
        def _(g):
            for kc in range(KC):
                sd.inc(g.dma_start(wqkv[:, kc * 3072:(kc + 1) * 3072], wq_hbm[kc]), 16)
                sd.inc(g.dma_start(wgate[:, kc * 1024:(kc + 1) * 1024], wg_hbm[kc]), 16)
                sd.inc(g.dma_start(wproj[:, kc * 1024:(kc + 1) * 1024], wp_hbm[kc]), 16)
            sd.inc(g.dma_start(gc[:], W["s7_gc"]), 16)
            sd.inc(g.dma_start(bc_bf[:], W["s7_bc"]), 16)
            sd.inc(g.dma_start(bq_row[:], W["s7_bqkv"][None, :]), 16)
            sd.inc(g.dma_start(bg_row[:], W["s7_bgate"][None, :]), 16)
            sd.inc(g.dma_start(bp_row[:], W["s7_bproj"][None, :]), 16)
            g.memset(ones_colb[:], 1.0)

        @blk.vector
        def _(v):
            v.wait_ge(sd.h, sd.n)

    st0 = st.n
    with nc.Block() as blk:
        @blk.tensor
        def _(t_):
            for ns in range(6):
                for mc in range(KC):
                    last = t_.matmul(pfa[ns][0:1, 0:512], bc_bf[:, mc:mc + 1],
                                     wqkv[:, mc * 3072 + ns * 512: mc * 3072 + (ns + 1) * 512],
                                     start=(mc == 0), stop=(mc == KC - 1))
                st.inc(last)

        @blk.vector
        def _(v):
            for ns in range(6):
                v.wait_ge(st.h, st0 + ns + 1)
                sv.inc(v.tensor_copy(rqg_row[0:1, ns * 512:(ns + 1) * 512],
                                     pfa[ns][0:1, 0:512]))
    st0 = st.n
    with nc.Block() as blk:
        @blk.tensor
        def _(t_):
            for ns in range(2):
                for mc in range(KC):
                    last = t_.matmul(pfa[ns][0:1, 0:512], bc_bf[:, mc:mc + 1],
                                     wgate[:, mc * 1024 + ns * 512: mc * 1024 + (ns + 1) * 512],
                                     start=(mc == 0), stop=(mc == KC - 1))
                st.inc(last)

        @blk.vector
        def _(v):
            for ns in range(2):
                v.wait_ge(st.h, st0 + ns + 1)
                sv.inc(v.tensor_copy(rqg_row[0:1, 3072 + ns * 512: 3072 + (ns + 1) * 512],
                                     pfa[ns][0:1, 0:512]))
    sv0 = sv.n
    with nc.Block() as blk:
        @blk.vector
        def _(v):
            sv.inc(v.tensor_add(rqg_row[0:1, 0:3072], rqg_row[0:1, 0:3072], bq_row[:]))
            sv.inc(v.tensor_add(rqg_row[0:1, 3072:4096], rqg_row[0:1, 3072:4096], bg_row[:]))
            v.wait_ge(sv.h, sv0 + 2)
            sv.inc(v.tensor_copy(rqg_bf[:], rqg_row[:]))
    st0, sv0 = st.n, sv.n
    with nc.Block() as blk:
        @blk.tensor
        def _(t_):
            for a in range(32):
                if a >= 6:
                    t_.wait_ge(sv.h, sv0 + a - 5)
                st.inc(t_.matmul(pfa[a % 6][:, 0:1], rqg_row[0:1, a * 128:(a + 1) * 128],
                                 ctx.ones32[0:1, 0:1], start=True, stop=True))
            t_.wait_ge(sv.h, sv0 + 32)
            st.inc(t_.matmul(pfa[0][:, 0:512], ctx.ones_row[:], bp_row[:, 0:512],
                             start=True, stop=True))
            st.inc(t_.matmul(pfa[1][:, 0:512], ctx.ones_row[:], bp_row[:, 512:1024],
                             start=True, stop=True))

        @blk.vector
        def _(v):
            for a in range(32):
                v.wait_ge(st.h, st0 + a + 1)
                sv.inc(v.tensor_copy(rcols[:, a:a + 1], pfa[a % 6][:, 0:1]))
            v.wait_ge(st.h, st0 + 34)
            sv.inc(v.tensor_copy(bprojB[:, 0:512], pfa[0][:, 0:512]))
            sv.inc(v.tensor_copy(bprojB[:, 512:1024], pfa[1][:, 0:512]))

    sv0 = sv.n
    with nc.Block() as blk:
        @blk.vector
        def _(v):
            for kc in range(KC):
                sv.inc(v.tensor_single_scalar(wqkv[:, kc * 3072:(kc + 1) * 3072],
                                              wqkv[:, kc * 3072:(kc + 1) * 3072],
                                              gc[:, kc:kc + 1], ALU.mult))
                sv.inc(v.tensor_single_scalar(wgate[:, kc * 1024:(kc + 1) * 1024],
                                              wgate[:, kc * 1024:(kc + 1) * 1024],
                                              gc[:, kc:kc + 1], ALU.mult))

    for sq_ in range(NSEQ):
        tiles = range(sq_ * TPS, (sq_ + 1) * TPS)
        emit_lnaT_phase(ctx, lnb, x_t, aT_c, pt, tiles, double_ln=True)

        for tg in range(4):
            sda = sd.n
            so_w = so.n
            with nc.Block() as blk:
                @blk.sync
                def _(sy):
                    sy.wait_ge(so.h, so_w)
                    for kc in range(KC):
                        sd.inc(sy.dma_start(aTg[:, kc * 512:(kc + 1) * 512],
                                            aT_c[kc][:, sq_ * S + tg * 512: sq_ * S + tg * 512 + 512]), 16)
            for oc in range(KC):
                st0, sv0, sa0 = st.n, sv.n, sa.n
                so0 = so.n
                with nc.Block() as blk:
                    @blk.tensor
                    def _(t_):
                        t_.wait_ge(sd.h, sda + 16 * KC)
                        for kc in range(KC):
                            last = t_.matmul(pfa[0][:], wqkv[:, kc * 3072 + oc * 128: kc * 3072 + oc * 128 + 128],
                                             aTg[:, kc * 512:(kc + 1) * 512],
                                             start=(kc == 0), stop=(kc == KC - 1))
                        st.inc(last)

                    @blk.vector
                    def _(v):
                        v.wait_ge(st.h, st0 + 1)
                        v.wait_ge(so.h, so0)
                        sv.inc(v.tensor_scalar(tmp1[:], pfa[0][:], rcols[:, oc:oc + 1],
                                               0.0, ALU.add, ALU.min))
                        sv.inc(v.tensor_scalar(tmp2[:], pfa[0][:], rcols[:, oc:oc + 1],
                                               0.0, ALU.add, ALU.max))

                    @blk.scalar
                    def _(sc):
                        sc.wait_ge(sv.h, sv0 + 1)
                        sa.inc(sc.activation(tmp3[:], tmp1[:], AF.Exp))

                    @blk.gpsimd
                    def _(gp):
                        pass
                sv0b = sv.n
                so_b = so.n
                with nc.Block() as blk:
                    @blk.vector
                    def _(v):
                        v.wait_ge(so.h, so_b)
                        sv.inc(v.tensor_add(kpbuf[:], tmp2[:], tmp3[:]))

                    @blk.sync
                    def _(sy):
                        sy.wait_ge(sv.h, sv0b + 1)
                        so.inc(sy.dma_start(qp_c[oc][:, tg * 512:(tg + 1) * 512], kpbuf[:]), 16)
            for oc in range(KC):
                st0, sv0, sa0 = st.n, sv.n, sa.n
                so0 = so.n
                with nc.Block() as blk:
                    @blk.tensor
                    def _(t_):
                        for kc in range(KC):
                            last = t_.matmul(pfa[1][:], wgate[:, kc * 1024 + oc * 128: kc * 1024 + oc * 128 + 128],
                                             aTg[:, kc * 512:(kc + 1) * 512],
                                             start=(kc == 0), stop=(kc == KC - 1))
                        st.inc(last)
                        for kc in range(KC):
                            last = t_.matmul(pfa[2][:], wqkv[:, kc * 3072 + 1024 + oc * 128: kc * 3072 + 1024 + oc * 128 + 128],
                                             aTg[:, kc * 512:(kc + 1) * 512],
                                             start=(kc == 0), stop=(kc == KC - 1))
                        st.inc(last)

                    @blk.scalar
                    def _(sc):
                        sc.wait_ge(st.h, st0 + 1)
                        sa.inc(sc.activation(gateT[:], pfa[1][:], AF.Sigmoid,
                                             bias=rcols[:, 24 + oc:25 + oc]))

                    @blk.vector
                    def _(v):
                        v.wait_ge(st.h, st0 + 2)
                        v.wait_ge(so.h, so0)
                        sv.inc(v.tensor_scalar_add(tmp1[:], pfa[2][:], rcols[:, 8 + oc:9 + oc]))
                        v.wait_ge(sa.h, sa0 + 1)
                        v.wait_ge(sv.h, sv0 + 1)
                        sv.inc(v.tensor_mul(tmp1[:], tmp1[:], gateT[:]))
                        v.wait_ge(sv.h, sv0 + 2)
                        sv.inc(v.tensor_scalar_min(tmp2[:], tmp1[:], 0.0))
                        sv.inc(v.tensor_relu(tmp1[:], tmp1[:]))

                    @blk.scalar
                    def _(sc2):
                        sc2.wait_ge(sv.h, sv0 + 3)
                        sa.inc(sc2.activation(tmp3[:], tmp2[:], AF.Exp))
                sv0b = sv.n
                so_b = so.n
                with nc.Block() as blk:
                    @blk.vector
                    def _(v):
                        v.wait_ge(so.h, so_b)
                        sv.inc(v.tensor_add(kpbuf[:], tmp1[:], tmp3[:]))

                    @blk.sync
                    def _(sy):
                        sy.wait_ge(sv.h, sv0b + 1)
                        for j in range(4):
                            so.inc(sy.dma_start(kpp_b[tg * 4 + j, oc],
                                                kpbuf[:, j * 128:(j + 1) * 128]), 16)
            for ti in range(4):
                st0, sv0 = st.n, sv.n
                so0 = so.n
                with nc.Block() as blk:
                    @blk.tensor
                    def _(t_):
                        for vh in range(2):
                            for kc in range(KC):
                                t_.matmul(pfa[3 + vh][:],
                                          aTg[:, kc * 512 + ti * 128: kc * 512 + ti * 128 + 128],
                                          wqkv[:, kc * 3072 + 2048 + vh * 512: kc * 3072 + 2048 + vh * 512 + 512],
                                          start=(kc == 0), stop=False)
                            st.inc(t_.matmul(pfa[3 + vh][:], ctx.ones_row[:],
                                             rqg_bf[:, 2048 + vh * 512: 2048 + (vh + 1) * 512],
                                             start=False, stop=True))

                    @blk.vector
                    def _(v):
                        v.wait_ge(st.h, st0 + 2)
                        v.wait_ge(so.h, so0)
                        for vh in range(2):
                            sv.inc(v.tensor_copy(vbuf[:, vh * 512:(vh + 1) * 512],
                                                 pfa[3 + vh][:]))

                    @blk.sync
                    def _(sy):
                        sy.wait_ge(sv.h, sv.n)
                        so.inc(sy.dma_start(v_tl[tg * 4 + ti], vbuf[:]), 16)

        for tg in range(4):
            tq0 = tg * 512
            nk = tg * 4 + 4
            sdq = sd.n
            so_w = so.n
            with nc.Block() as blk:
                @blk.sync
                def _(sy):
                    sy.wait_ge(so.h, so_w)
                    for kc in range(KC):
                        sd.inc(sy.dma_start(qband[:, kc * 512:(kc + 1) * 512],
                                            qp_c[kc][:, tq0:tq0 + 512]), 16)
            st0, sv0 = st.n, sv.n
            sdk0 = sd.n
            copy_done = {}
            kload_tgt = {}
            with nc.Block() as blk:
                @blk.vector
                def _(v):
                    for tkc in range(nk):
                        v.wait_ge(st.h, st0 + tkc + 1)
                        sv.inc(v.tensor_copy(sT[:, tkc * 512:(tkc + 1) * 512],
                                             pfa[tkc % 2][:]))
                        copy_done[tkc] = sv.n
                        for j in range(4):
                            tqt = tg * 4 + j
                            if tkc == tqt:
                                v.wait_ge(sv.h, sv.n)
                                sv.inc(v.tensor_mul(sT[:, tkc * 512 + j * 128: tkc * 512 + j * 128 + 128],
                                                    sT[:, tkc * 512 + j * 128: tkc * 512 + j * 128 + 128],
                                                    ctx.ltri[:]))
                            elif tkc > tqt:
                                v.wait_ge(sv.h, sv.n)
                                sv.inc(v.memset(sT[:, tkc * 512 + j * 128: tkc * 512 + j * 128 + 128], 0.0))

                @blk.sync
                def _(sy):
                    sy.wait_ge(sd.h, sdq + 16 * KC)
                    for tkc in range(nk):
                        sem = ctx.pa if tkc % 2 == 0 else ctx.pb
                        if tkc >= 2:
                            sy.wait_ge(st.h, st0 + tkc - 1)
                        sem.inc(sy.dma_start(kblk[tkc % 2][:].rearrange("p (c n) -> p c n", n=128),
                                             kpp_b[tkc].rearrange("c p n -> p c n")), 16)
                        kload_tgt[tkc] = sem.n

                @blk.tensor
                def _(t_):
                    for tkc in range(nk):
                        sem = ctx.pa if tkc % 2 == 0 else ctx.pb
                        t_.wait_ge(sem.h, kload_tgt[tkc])
                        if tkc >= 2:
                            t_.wait_ge(sv.h, copy_done[tkc - 2])
                        for kc in range(KC):
                            last = t_.matmul(pfa[tkc % 2][:],
                                             kblk[tkc % 2][:, kc * 128:(kc + 1) * 128],
                                             qband[:, kc * 512:(kc + 1) * 512],
                                             start=(kc == 0), stop=(kc == KC - 1))
                        st.inc(last)

            st0, sv0, sa0 = st.n, sv.n, sa.n
            with nc.Block() as blk:
                @blk.tensor
                def _(t_):
                    for tkc in range(nk):
                        last = t_.matmul(pfa[2][0:1, 0:512], ones_colb[:],
                                         sT[:, tkc * 512:(tkc + 1) * 512],
                                         start=(tkc == 0), stop=(tkc == nk - 1))
                    st.inc(last)

                @blk.vector
                def _(v):
                    v.wait_ge(st.h, st0 + 1)
                    sv.inc(v.tensor_scalar_add(den_row[:], pfa[2][0:1, 0:512], 1e-6))
                    v.wait_ge(sv.h, sv0 + 1)
                    sv.inc(v.reciprocal(rd_row[:], den_row[:]))
            if S7_DEBUG_DUMP and sq_ == 3 and tg == 0:
                with nc.Block() as blk:
                    @blk.sync
                    def _(sy):
                        so.inc(sy.dma_start(dbg["v03"], v_hbm[0:4 * 128, :]), 16)
                        so.inc(sy.dma_start(dbg["sT"], sT[:, 0:4 * 512]), 16)
                        so.inc(sy.dma_start(dbg["den"], den_row[:]), 16)
                        so.inc(sy.dma_start(dbg["qband"], qband[:]), 16)
                        so.inc(sy.dma_start(dbg["kblk"], kblk[(nk - 1) % 2][:]), 16)
                        so.inc(sy.dma_start(dbg["rd"], rd_row[:]), 16)
                        sy.wait_ge(so.h, so.n)
            for half in range(2):
                st0, sv0 = st.n, sv.n
                sdv0 = sd.n
                vload_tgt = {}
                with nc.Block() as blk:
                    @blk.sync
                    def _(sy):
                        for tkc in range(nk):
                            sem = ctx.pa if tkc % 2 == 0 else ctx.pb
                            if tkc >= 2:
                                sy.wait_ge(st.h, st0 + tkc - 1)
                            sem.inc(sy.dma_start(vblk[tkc % 2][:],
                                                 v_tl[tkc][:, half * 512:(half + 1) * 512]), 16)
                            vload_tgt[tkc] = sem.n

                    @blk.tensor
                    def _(t_):
                        for tkc in range(nk):
                            sem = ctx.pa if tkc % 2 == 0 else ctx.pb
                            t_.wait_ge(sem.h, vload_tgt[tkc])
                            for dc in range(4):
                                last = t_.matmul(pfa[dc][:],
                                                 vblk[tkc % 2][:, dc * 128:(dc + 1) * 128],
                                                 sT[:, tkc * 512:(tkc + 1) * 512],
                                                 start=(tkc == 0), stop=(tkc == nk - 1))
                            st.inc(last)

                    @blk.vector
                    def _(v):
                        v.wait_ge(st.h, st0 + nk)
                        for dc in range(4):
                            sv.inc(v.tensor_copy(numT[:, (half * 4 + dc) * 512:(half * 4 + dc) * 512 + 512],
                                                 pfa[dc][:]))
            if S7_DEBUG_DUMP and sq_ == 3 and tg == 0:
                with nc.Block() as blk:
                    @blk.sync
                    def _(sy):
                        so.inc(sy.dma_start(dbg["numT"], numT[:]), 16)
                        sy.wait_ge(so.h, so.n)
            for ti in range(4):
                gt = sq_ * TPS + tg * 4 + ti
                st0, sv0 = st.n, sv.n
                sdx = sd.n
                so0 = so.n
                with nc.Block() as blk:
                    @blk.sync
                    def _(sy):
                        sd.inc(sy.dma_start(xt[:], x_t[gt]), 16)

                    @blk.tensor
                    def _(t_):
                        st.inc(t_.matmul(pfa[2][:, 0:1],
                                         rd_row[0:1, ti * 128:(ti + 1) * 128],
                                         ctx.ones32[0:1, 0:1], start=True, stop=True))
                        for half in range(2):
                            pj = pfa[half]
                            for mc in range(KC):
                                last = t_.matmul(pj[:], numT[:, mc * 512 + ti * 128: mc * 512 + ti * 128 + 128],
                                                 wproj[:, mc * 1024 + half * 512: mc * 1024 + half * 512 + 512],
                                                 start=(mc == 0), stop=(mc == KC - 1))
                            st.inc(last)

                    @blk.vector
                    def _(v):
                        v.wait_ge(st.h, st0 + 1)
                        sv.inc(v.tensor_copy(rdcol[:], pfa[2][:, 0:1]))
                        v.wait_ge(st.h, st0 + 3)
                        v.wait_ge(sd.h, sdx + 16)
                        v.wait_ge(so.h, so0)
                        v.wait_ge(sv.h, sv0 + 1)
                        for half in range(2):
                            sv.inc(v.scalar_tensor_tensor(
                                xnew[:, half * 512:half * 512 + 512],
                                pfa[half][:], rdcol[:],
                                xt[:, half * 512:half * 512 + 512],
                                ALU.mult, ALU.add))
                        v.wait_ge(sv.h, sv0 + 3)
                        sv.inc(v.tensor_add(xnew[:], xnew[:], bprojB[:]))

                    @blk.gpsimd
                    def _(gp):
                        gp.wait_ge(sv.h, sv.n)
                        so.inc(gp.dma_start(xo_t[gt], xnew[:]), 16)

    with nc.Block() as blk:
        @blk.sync
        def _(sy):
            sy.wait_ge(so.h, so.n)



def prep_weights(inputs):
    W = {}
    W["s1_wuv"] = inputs["L0_attn_a_W_uv"]
    W["s1_buv"] = inputs["L0_attn_a_b_uv"]
    W["s1_gbT"] = np.stack([inputs["L0_attn_a_gamma"][0], inputs["L0_attn_a_beta"][0],
                            inputs["L0_attn_a_gamma"][1], inputs["L0_attn_a_beta"][1]],
                           axis=1)
    W["s1_wout"] = inputs["L0_attn_a_W_out"]
    W["s1_bout"] = inputs["L0_attn_a_b_out"]
    W["s1_buvc"] = np.asarray(inputs["L0_attn_a_b_uv"]).reshape(33, 128).T
    W["moe_w1T"] = np.asarray(inputs["L0_ffn_w1"]).transpose(0, 2, 1)
    W["moe_w2T"] = np.asarray(inputs["L0_ffn_w2"]).transpose(0, 2, 1)
    W["moe_b1c"] = np.asarray(inputs["L0_ffn_b1"]).reshape(8, 32, 128).transpose(2, 0, 1).reshape(128, 256)
    W["moe_b2"] = np.asarray(inputs["L0_ffn_b2"])
    W["moe_rwT"] = np.asarray(inputs["L0_ffn_router_w"]).T.reshape(8, 128, 8).transpose(1, 0, 2).reshape(128, 64)
    W["moe_rb"] = np.asarray(inputs["L0_ffn_router_b"])
    W["moe_lnwB"] = np.broadcast_to(np.asarray(inputs["L0_ffn_ln_w"])[None, :], (128, 1024))
    W["moe_lnbB"] = np.broadcast_to(np.asarray(inputs["L0_ffn_ln_b"])[None, :], (128, 1024))
    W["s3_wqkvT"] = np.asarray(inputs["L0_attn_b_w_qkv"]).T
    W["s3_woutT"] = np.asarray(inputs["L0_attn_b_w_out"]).T
    W["s3_bout"] = np.asarray(inputs["L0_attn_b_b_out"])
    W["s3_gb"] = np.concatenate([np.asarray(inputs["L0_attn_b_norm_g"]).reshape(16),
                                 np.asarray(inputs["L0_attn_b_norm_b"]).reshape(16)])
    W["s5_wqkv"] = np.asarray(inputs["L1_attn_a_w_qkv"])
    W["s5_bqkv"] = np.asarray(inputs["L1_attn_a_b_qkv"])
    W["s5_bqc"] = np.asarray(inputs["L1_attn_a_b_qkv"]).reshape(24, 128).T
    W["s5_wout"] = np.asarray(inputs["L1_attn_a_w_out"])
    W["s5_bout"] = np.asarray(inputs["L1_attn_a_b_out"])
    W["s6_w1T"] = np.asarray(inputs["L1_ffn_w1"]).T
    W["s6_w2T"] = np.asarray(inputs["L1_ffn_w2"]).T
    W["s6_g1c"] = np.asarray(inputs["L1_ffn_ln1_g"]).reshape(8, 128).T
    W["s6_b1lc"] = np.asarray(inputs["L1_ffn_ln1_b"]).reshape(8, 128).T
    W["s6_g2c"] = np.asarray(inputs["L1_ffn_ln2_g"]).reshape(32, 128).T
    W["s6_b2lc"] = np.asarray(inputs["L1_ffn_ln2_b"]).reshape(32, 128).T
    W["s6_b1c"] = np.asarray(inputs["L1_ffn_b1"]).reshape(32, 128).T
    W["s6_b2"] = np.asarray(inputs["L1_ffn_b2"])
    W["s7_wqkv"] = np.asarray(inputs["L1_attn_b_w_qkv"])
    W["s7_wgate"] = np.asarray(inputs["L1_attn_b_w_gate"])
    W["s7_wproj"] = np.asarray(inputs["L1_attn_b_w_proj"])
    W["s7_bqkv"] = np.asarray(inputs["L1_attn_b_b_qkv"])
    W["s7_bgate"] = np.asarray(inputs["L1_attn_b_b_gate"])
    W["s7_bproj"] = np.asarray(inputs["L1_attn_b_b_proj"])
    W["s7_gc"] = np.asarray(inputs["L1_attn_b_ln_g"]).reshape(8, 128).T
    W["s7_bc"] = np.asarray(inputs["L1_attn_b_ln_b"]).reshape(8, 128).T
    return {k: np.ascontiguousarray(np.asarray(v, dtype=np.float32)) for k, v in W.items()}


def declare_weight_inputs(nc, Wnp):
    return {k: nc.dram_tensor("in_" + k, list(v.shape), F32, kind="ExternalInput").ap()
            for k, v in Wnp.items()}


PHASES_ALL = ["s1", "moe1", "s3", "moe2", "s5", "s6a", "s7", "s6b"]

EMITTERS = {}


def _register():
    EMITTERS["s1"] = lambda ctx, es, ri, ro, W: emit_s1(ctx, es, ri, ro, W)
    EMITTERS["moe1"] = lambda ctx, es, ri, ro, W: emit_moe(ctx, es, ri, ro, W, "m1")
    EMITTERS["moe2"] = lambda ctx, es, ri, ro, W: emit_moe(ctx, es, ri, ro, W, "m2")
    EMITTERS["s3"] = lambda ctx, es, ri, ro, W: emit_s3(ctx, es, ri, ro, W)
    EMITTERS["s5"] = lambda ctx, es, ri, ro, W: emit_s5(ctx, es, ri, ro, W)
    EMITTERS["s7"] = lambda ctx, es, ri, ro, W: emit_s7(ctx, es, ri, ro, W)


_register()


def build_program(Wnp, phases):
    nc = bass.Bass("TRN2", target_bir_lowering=False)
    x_in = nc.dram_tensor("x_in", [T, D], F32, kind="ExternalInput").ap()
    y_out = nc.dram_tensor("y_out", [T, D], F32, kind="ExternalOutput").ap()
    resid_mid = None
    if len(phases) > 1:
        resid_mid = nc.dram_tensor("resid_mid", [T, D], F32, kind="Internal").ap()
    Waps = declare_weight_inputs(nc, Wnp)
    ctx = Ctx(nc)
    s6_scratch = None
    if any(p in ("s6a", "s6b") for p in phases):
        s6_scratch = {
            "t1g": nc.dram_tensor("s6_t1g", [KC * 128, FF], BF16, kind="Internal").ap(),
            "t2g": nc.dram_tensor("s6_t2g", [32 * 128, 1024], BF16, kind="Internal").ap(),
            "bcomb": nc.dram_tensor("s6_bcombh", [128, 32], F32, kind="Internal").ap(),
            "rbs1": nc.dram_tensor("s6_rbs1h", [128, 1], F32, kind="Internal").ap(),
            "cp": nc.dram_tensor("s6_cph", [1, 1024], BF16, kind="Internal").ap(),
            "ep": nc.dram_tensor("s6_eph", [1, 1024], BF16, kind="Internal").ap(),
        }
    first_s6 = True
    with contextlib.ExitStack() as stack:
        ctx.alloc_consts(lambda cm: stack.enter_context(cm))
        for pi, phname in enumerate(phases):
            rin = x_in if pi == 0 else resid_mid
            rout = y_out if pi == len(phases) - 1 else resid_mid
            with contextlib.ExitStack() as ps:
                pes = lambda cm: ps.enter_context(cm)
                if phname in ("s6a", "s6b"):
                    emit_s6(ctx, pes, rin, rout, Waps, phname, s6_scratch,
                            quant=first_s6)
                    first_s6 = False
                else:
                    EMITTERS[phname](ctx, pes, rin, rout, Waps)
    return nc


def kernel(**inputs):
    Wnp = prep_weights(inputs)
    x = np.ascontiguousarray(np.asarray(inputs["L0_attn_a_inputs"], dtype=np.float32))
    nc = build_program(Wnp, PHASES_ALL)
    in_maps = []
    for c in range(NCORES):
        m = {"x_in": np.ascontiguousarray(x[c * NSEQ:(c + 1) * NSEQ].reshape(T, D))}
        m.update({"in_" + k: v for k, v in Wnp.items()})
        in_maps.append(m)
    res = run_bass_kernel_spmd(nc, in_maps, core_ids=list(range(NCORES)))
    out = np.concatenate([np.asarray(r["y_out"]).reshape(NSEQ, S, D)
                          for r in res.results], axis=0)
    return out


# version 07
# speedup vs baseline: 1.2156x; 1.0784x over previous
import contextlib
import numpy as np
import ml_dtypes
import concourse.bass as bass
import concourse.mybir as mybir
from concourse.bass_utils import run_bass_kernel_spmd

F32 = mybir.dt.float32
BF16 = mybir.dt.bfloat16
AF = mybir.ActivationFunctionType
ALU = mybir.AluOpType
AX = mybir.AxisListType

NCORES = 8
B_TOT = 32
S = 2048
D = 1024
NSEQ = B_TOT // NCORES
T = NSEQ * S
NT = T // 128
TPS = S // 128
KC = D // 128
FF = 4096
NEXP = 8


class Sem:

    def __init__(self, nc, name):
        self._cm = nc.semaphore(name)
        self.h = self._cm.__enter__()
        self.n = 0

    def inc(self, instr, k=1):
        instr.then_inc(self.h, k)
        self.n += k
        return instr


class Ctx:
    def __init__(self, nc):
        self.nc = nc
        self.sd = Sem(nc, "sd")
        self.so = Sem(nc, "so")
        self.st = Sem(nc, "st")
        self.sv = Sem(nc, "sv")
        self.sa = Sem(nc, "sa")
        self.pa = Sem(nc, "pa")
        self.pb = Sem(nc, "pb")
        self.sw1 = Sem(nc, "sw1")
        self.sw2 = Sem(nc, "sw2")

    def alloc_consts(self, es):
        nc = self.nc
        self.ident = es(nc.sbuf_tensor("c_ident", [128, 128], BF16))
        self.ident32 = es(nc.sbuf_tensor("c_ident32", [128, 128], F32))
        self.ones_row = es(nc.sbuf_tensor("c_ones", [1, 128], BF16))
        self.ones32 = es(nc.sbuf_tensor("c_ones32", [1, 128], F32))
        self.ltri = es(nc.sbuf_tensor("c_ltri", [128, 128], BF16))
        with nc.Block() as blk:
            @blk.gpsimd
            def _(g):
                g.memset(self.ident[:], 0.0)
                g.affine_select(out=self.ident[:], in_=self.ident[:],
                                compare_op=ALU.not_equal, fill=1.0, base=0,
                                pattern=[[-1, 128]], channel_multiplier=1)
                g.memset(self.ident32[:], 0.0)
                g.affine_select(out=self.ident32[:], in_=self.ident32[:],
                                compare_op=ALU.not_equal, fill=1.0, base=0,
                                pattern=[[-1, 128]], channel_multiplier=1)
                g.memset(self.ones_row[:], 1.0)
                g.memset(self.ones32[:], 1.0)
                g.memset(self.ltri[:], 1.0)
                g.affine_select(out=self.ltri[:], in_=self.ltri[:],
                                compare_op=ALU.is_ge, fill=0.0, base=0,
                                pattern=[[1, 128]], channel_multiplier=-1)


def emit_ln_tile(ctx, x_tile_ap, out_bf, scr, eps=1e-5, out_f32=None,
                 W=1024, blk=None):
    nc = ctx.nc
    sv, sa = ctx.sv, ctx.sa
    mean, t1, rstd, sq = scr["mean"], scr["t1"], scr["rstd"], scr["sq"]
    inv = 1.0 / W
    sv0, sa0 = sv.n, sa.n
    import contextlib as _c
    own = blk is None
    cmgr = nc.Block() if own else _c.nullcontext(blk)
    with cmgr as blk_:
        @blk_.vector
        def _(v):
            sv.inc(v.reduce_sum(mean[:], x_tile_ap, axis=AX.X))
            v.wait_ge(sv.h, sv0 + 1)
            sv.inc(v.tensor_scalar_mul(mean[:], mean[:], -inv))
            v.wait_ge(sa.h, sa0 + 1)
            sv.inc(v.tensor_scalar(rstd[:], t1[:], inv, eps, ALU.mult, ALU.add))
            v.wait_ge(sa.h, sa0 + 2)
            sv.inc(v.reciprocal(rstd[:], t1[:]))
            v.wait_ge(sv.h, sv0 + 4)
            if out_bf is not None:
                sv.inc(v.tensor_scalar(out_bf, x_tile_ap, mean[:], rstd[:],
                                       ALU.add, ALU.mult))
            if out_f32 is not None:
                sv.inc(v.tensor_scalar(out_f32, x_tile_ap, mean[:], rstd[:],
                                       ALU.add, ALU.mult))

        @blk_.scalar
        def _(sc):
            sc.wait_ge(sv.h, sv0 + 2)
            sa.inc(sc.activation(sq[:, 0:W], x_tile_ap, AF.Square,
                                 bias=mean[:], accum_out=t1[:]))
            sc.wait_ge(sv.h, sv0 + 3)
            sa.inc(sc.activation(t1[:], rstd[:], AF.Sqrt))


def emit_transpose_tile(ctx, src_bf, dst_slices, ptr, blk=None,
                        src_ready_sv=None):
    nc = ctx.nc
    st, sv = ctx.st, ctx.sv
    n = len(dst_slices)
    st0, sv0 = st.n, sv.n
    import contextlib as _c
    own = blk is None
    cmgr = nc.Block() if own else _c.nullcontext(blk)
    with cmgr as blk_:
        @blk_.tensor
        def _(t):
            if src_ready_sv is not None:
                t.wait_ge(sv.h, src_ready_sv)
            for k in range(n):
                if k >= 2:
                    t.wait_ge(sv.h, sv0 + k - 1)
                slot = ptr[k % 2][:, 0:128]
                st.inc(t.transpose(slot, src_bf[:, k * 128:(k + 1) * 128],
                                   ctx.ident[:]))

        @blk_.vector
        def _(v):
            for k in range(n):
                v.wait_ge(st.h, st0 + k + 1)
                slot = ptr[k % 2][:, 0:128]
                sv.inc(v.tensor_copy(dst_slices[k], slot))



def emit_s1(ctx, es, resid_in, resid_out, W):
    nc = ctx.nc
    sd, so, st, sv, sa = ctx.sd, ctx.so, ctx.st, ctx.sv, ctx.sa

    x_t = resid_in.rearrange("(n p) d -> n p d", p=128)
    xo_t = resid_out.rearrange("(n p) d -> n p d", p=128)

    wuv = es(nc.sbuf_tensor("s1_wuv", [128, KC * 4224], BF16))
    wout = es(nc.sbuf_tensor("s1_wout", [128, 16 * 1024], BF16))
    buv_row = es(nc.sbuf_tensor("s1_buvr", [1, 4224], BF16))
    buv_col = es(nc.sbuf_tensor("s1_buvc", [128, 33], F32))
    bout_row = es(nc.sbuf_tensor("s1_boutr", [1, 1024], BF16))
    gb = es(nc.sbuf_tensor("s1_gb", [128, 4], F32))
    gbq = es(nc.sbuf_tensor("s1_gbq", [128, 2], F32))
    aTt = es(nc.sbuf_tensor("s1_aTt", [128, KC * 128], BF16))
    aTg = es(nc.sbuf_tensor("s1_aTg", [128, KC * 512], BF16))
    qT = es(nc.sbuf_tensor("s1_qT", [128, S], BF16))
    kT = es(nc.sbuf_tensor("s1_kT", [128, S], BF16))
    uT = es(nc.sbuf_tensor("s1_uT", [128, 16 * 512], BF16))
    sqT = es(nc.sbuf_tensor("s1_sqT", [128, 16 * 512], BF16))
    vband = es(nc.sbuf_tensor("s1_vband", [128, 16 * 512], BF16))
    xt = es(nc.sbuf_tensor("s1_xt", [128, 1024], F32))
    a_bf = es(nc.sbuf_tensor("s1_abf", [128, 1024], BF16))
    vtile = es(nc.sbuf_tensor("s1_vtile", [128, 2048], BF16))
    base_bf = es(nc.sbuf_tensor("s1_basebf", [128, 128], BF16))
    tmp0 = es(nc.sbuf_tensor("s1_tmp0", [128, 512], F32))
    tmp1 = es(nc.sbuf_tensor("s1_tmp1", [128, 512], F32))
    xnew = es(nc.sbuf_tensor("s1_xnew", [128, 1024], F32))
    scr = {
        "mean": es(nc.sbuf_tensor("s1_mean", [128, 1], F32)),
        "t1": es(nc.sbuf_tensor("s1_t1", [128, 1], F32)),
        "rstd": es(nc.sbuf_tensor("s1_rstd", [128, 1], F32)),
        "sq": es(nc.sbuf_tensor("s1_sq", [128, 1024], F32)),
    }

    ph = [es(nc.psum_tensor(f"s1_ph{i}", [128, 512], F32)) for i in range(2)]
    ptr = [es(nc.psum_tensor(f"s1_ptr{i}", [128, 1024], BF16)) for i in range(2)]
    pav = [es(nc.psum_tensor(f"s1_pav{i}", [128, 512], F32)) for i in range(2)]
    po = [es(nc.psum_tensor(f"s1_po{i}", [128, 512], F32)) for i in range(2)]

    v_hbm = nc.dram_tensor("s1_vhbm", [4 * 16 * 128, 512], BF16,
                           kind="Internal").ap()
    v_blk = v_hbm.rearrange("(e k p) n -> e k p n", k=16, p=128)
    v_band_ap = v_hbm.rearrange("(e q) n -> e q n", q=16 * 128)
    aT_hbm = nc.dram_tensor("s1_aThbm", [KC * 128, S], BF16, kind="Internal").ap()
    aT_c = aT_hbm.rearrange("(c p) t -> c p t", p=128)

    wuv_hbm = W["s1_wuv"].rearrange("(c p) n -> c p n", p=128)
    wout_hbm = W["s1_wout"].rearrange("(c p) n -> c p n", p=128)
    sd0 = sd.n
    with nc.Block() as blk:
        @blk.gpsimd
        def _(g):
            for kc in range(KC):
                sd.inc(g.dma_start(wuv[:, kc * 4224:(kc + 1) * 4224], wuv_hbm[kc]), 16)
            for ec in range(16):
                sd.inc(g.dma_start(wout[:, ec * 1024:(ec + 1) * 1024], wout_hbm[ec]), 16)
            sd.inc(g.dma_start(buv_row[:], W["s1_buv"][None, :]), 16)
            sd.inc(g.dma_start(bout_row[:], W["s1_bout"][None, :]), 16)
            sd.inc(g.dma_start(buv_col[:], W["s1_buvc"]), 16)
            sd.inc(g.dma_start(gb[:], W["s1_gbT"]), 16)

        @blk.vector
        def _(v):
            v.wait_ge(sd.h, sd.n)
            sv.inc(v.tensor_scalar_mul(gbq[:], gb[:, 0:2], 2.0 ** -11))

    for s in range(NSEQ):
        for i in range(TPS):
            g_tile = s * TPS + i
            with nc.Block() as blk:
                @blk.sync
                def _(sy):
                    sd.inc(sy.dma_start(xt[:], x_t[g_tile]), 16)

                @blk.vector
                def _(v):
                    v.wait_ge(sd.h, sd.n)
            emit_ln_tile(ctx, xt[:], a_bf[:], scr)
            dsts = [aTt[:, kc * 128:(kc + 1) * 128] for kc in range(KC)]
            emit_transpose_tile(ctx, a_bf, dsts, ptr)
            with nc.Block() as blk:
                @blk.sync
                def _(sy):
                    for kc in range(KC):
                        so.inc(sy.dma_start(aT_c[kc][:, i * 128:(i + 1) * 128],
                                            aTt[:, kc * 128:(kc + 1) * 128]), 16)

            st0, sa0 = st.n, sa.n
            so_prev = so.n
            with nc.Block() as blk:
                @blk.tensor
                def _(t):
                    for j in range(4):
                        pj = ph[j % 2]
                        if j >= 2:
                            t.wait_ge(sa.h, sa0 + j - 1)
                        for kc in range(KC):
                            t.matmul(pj[:], aTt[:, kc * 128:(kc + 1) * 128],
                                     wuv[:, kc * 4224 + 2048 + j * 512: kc * 4224 + 2560 + j * 512],
                                     start=(kc == 0), stop=False)
                        st.inc(t.matmul(pj[:], ctx.ones_row[:],
                                        buv_row[:, 2048 + j * 512: 2560 + j * 512],
                                        start=False, stop=True))
                    t.wait_ge(sa.h, sa0 + 3)
                    for kc in range(KC):
                        t.matmul(ph[0][:, 0:128], aTt[:, kc * 128:(kc + 1) * 128],
                                 wuv[:, kc * 4224 + 4096: kc * 4224 + 4224],
                                 start=(kc == 0), stop=False)
                    st.inc(t.matmul(ph[0][:, 0:128], ctx.ones_row[:],
                                    buv_row[:, 4096:4224], start=False, stop=True))

                @blk.scalar
                def _(sc):
                    sc.wait_ge(so.h, so_prev)
                    for j in range(4):
                        sc.wait_ge(st.h, st0 + j + 1)
                        sa.inc(sc.activation(vtile[:, j * 512:(j + 1) * 512],
                                             ph[j % 2][:], AF.Silu))
                    sc.wait_ge(st.h, st0 + 5)
                    sa.inc(sc.activation(base_bf[:], ph[0][:, 0:128], AF.Silu))

            st1, sv1 = st.n, sv.n
            with nc.Block() as blk:
                @blk.sync
                def _(sy):
                    for eg in range(4):
                        so.inc(sy.dma_start(v_blk[eg, i], vtile[:, eg * 512:(eg + 1) * 512]), 16)

                @blk.tensor
                def _(t):
                    st.inc(t.transpose(ptr[0][:, 0:128], base_bf[:], ctx.ident[:]))

                @blk.vector
                def _(v):
                    v.wait_ge(st.h, st1 + 1)
                    sv.inc(v.tensor_scalar(qT[:, i * 128:(i + 1) * 128], ptr[0][:, 0:128],
                                           gbq[:, 0:1], gbq[:, 1:2], ALU.mult, ALU.add))
                    sv.inc(v.tensor_scalar(kT[:, i * 128:(i + 1) * 128], ptr[0][:, 0:128],
                                           gb[:, 2:3], gb[:, 3:4], ALU.mult, ALU.add))

        for g in range(4):
            tq0 = g * 512
            sda = sd.n
            so_w = so.n
            with nc.Block() as blk:
                @blk.sync
                def _(sy):
                    sy.wait_ge(so.h, so_w)
                    for kc in range(KC):
                        sd.inc(sy.dma_start(aTg[:, kc * 512:(kc + 1) * 512],
                                            aT_c[kc][:, tq0:tq0 + 512]), 16)
            st0, sa0 = st.n, sa.n
            with nc.Block() as blk:
                @blk.tensor
                def _(t):
                    t.wait_ge(sd.h, sda + 16 * KC)
                    for ec in range(16):
                        pj = ph[ec % 2]
                        if ec >= 2:
                            t.wait_ge(sa.h, sa0 + ec - 1)
                        for kc in range(KC):
                            last = t.matmul(pj[:], wuv[:, kc * 4224 + ec * 128: kc * 4224 + ec * 128 + 128],
                                            aTg[:, kc * 512:(kc + 1) * 512],
                                            start=(kc == 0), stop=(kc == KC - 1))
                        st.inc(last)

                @blk.scalar
                def _(sc):
                    for ec in range(16):
                        sc.wait_ge(st.h, st0 + ec + 1)
                        sa.inc(sc.activation(uT[:, ec * 512:(ec + 1) * 512],
                                             ph[ec % 2][:], AF.Silu,
                                             bias=buv_col[:, ec:ec + 1]))

            st0, sv0, sa0 = st.n, sv.n, sa.n
            with nc.Block() as blk:
                @blk.tensor
                def _(t):
                    for tkc in range(16):
                        pj = pav[tkc % 2]
                        if tkc >= 2:
                            t.wait_ge(sv.h, sv0 + tkc - 1)
                        st.inc(t.matmul(pj[:], kT[:, tkc * 128:(tkc + 1) * 128],
                                        qT[:, tq0:tq0 + 512], start=True, stop=True))

                @blk.vector
                def _(v):
                    for tkc in range(16):
                        v.wait_ge(st.h, st0 + tkc + 1)
                        if tkc >= 2:
                            v.wait_ge(sa.h, sa0 + tkc - 1)
                        tm = tmp0 if tkc % 2 == 0 else tmp1
                        sv.inc(v.tensor_scalar_max(tm[:], pav[tkc % 2][:], 0.0))

                @blk.scalar
                def _(sc):
                    for tkc in range(16):
                        sc.wait_ge(sv.h, sv0 + tkc + 1)
                        tm = tmp0 if tkc % 2 == 0 else tmp1
                        sa.inc(sc.activation(sqT[:, tkc * 512:(tkc + 1) * 512],
                                             tm[:], AF.Square))

            for eg in range(4):
                st0, sv0 = st.n, sv.n
                sdv = sd.n
                so_w2 = so.n
                with nc.Block() as blk:
                    @blk.sync
                    def _(sy):
                        sy.wait_ge(so.h, so_w2)
                        sd.inc(sy.dma_start(vband[:], v_band_ap[eg]), 16)

                    @blk.tensor
                    def _(t):
                        t.wait_ge(sd.h, sdv + 16)
                        for ecl in range(4):
                            pj = pav[ecl % 2]
                            if ecl >= 2:
                                t.wait_ge(sv.h, sv0 + ecl - 1)
                            for tkc in range(16):
                                last = t.matmul(pj[:], vband[:, tkc * 512 + ecl * 128: tkc * 512 + ecl * 128 + 128],
                                                sqT[:, tkc * 512:(tkc + 1) * 512],
                                                start=(tkc == 0), stop=(tkc == 15))
                            st.inc(last)

                    @blk.vector
                    def _(v):
                        for ecl in range(4):
                            ec = eg * 4 + ecl
                            v.wait_ge(st.h, st0 + ecl + 1)
                            sv.inc(v.tensor_mul(uT[:, ec * 512:(ec + 1) * 512],
                                                pav[ecl % 2][:],
                                                uT[:, ec * 512:(ec + 1) * 512]))

            for ti in range(4):
                i = g * 4 + ti
                g_tile = s * TPS + i
                col = ti * 128
                st0, sv0 = st.n, sv.n
                sdx = sd.n
                so0 = so.n
                with nc.Block() as blk:
                    @blk.sync
                    def _(sy):
                        sd.inc(sy.dma_start(xt[:], x_t[g_tile]), 16)

                    @blk.tensor
                    def _(t):
                        for half in range(2):
                            pj = po[half]
                            for ec in range(16):
                                t.matmul(pj[:], uT[:, ec * 512 + col: ec * 512 + col + 128],
                                         wout[:, ec * 1024 + half * 512: ec * 1024 + half * 512 + 512],
                                         start=(ec == 0), stop=False)
                            st.inc(t.matmul(pj[:], ctx.ones_row[:],
                                            bout_row[:, half * 512:half * 512 + 512],
                                            start=False, stop=True))

                    @blk.vector
                    def _(v):
                        v.wait_ge(st.h, st0 + 2)
                        v.wait_ge(sd.h, sdx + 16)
                        v.wait_ge(so.h, so0)
                        for half in range(2):
                            sv.inc(v.tensor_add(xnew[:, half * 512:half * 512 + 512],
                                                po[half][:],
                                                xt[:, half * 512:half * 512 + 512]))

                    @blk.gpsimd
                    def _(gp):
                        gp.wait_ge(sv.h, sv.n)
                        so.inc(gp.dma_start(xo_t[g_tile], xnew[:]), 16)

    with nc.Block() as blk:
        @blk.sync
        def _(sy):
            sy.wait_ge(so.h, so.n)





def emit_moe(ctx, es, resid_in, resid_out, W, tag):
    nc = ctx.nc
    sd, so, st, sv, sa = ctx.sd, ctx.so, ctx.st, ctx.sv, ctx.sa
    NG = NT // 4

    x_t = resid_in.rearrange("(n p) d -> n p d", p=128)
    xo_t = resid_out.rearrange("(n p) d -> n p d", p=128)

    HFF = FF // 2
    w1sb = es(nc.sbuf_tensor(f"{tag}_w1", [128, KC * HFF], BF16))
    w2sb = es(nc.sbuf_tensor(f"{tag}_w2", [128, 16 * 1024], BF16))
    hT = [es(nc.sbuf_tensor(f"{tag}_hT{i}", [128, 16 * 512], BF16)) for i in range(2)]
    aTg = [es(nc.sbuf_tensor(f"{tag}_aTg{i}", [128, KC * 512], BF16)) for i in range(2)]
    acc_sb = [es(nc.sbuf_tensor(f"{tag}_acc{i}", [128, 4 * 1024], F32)) for i in range(2)]
    cw_sb = es(nc.sbuf_tensor(f"{tag}_cw", [128, NT * NEXP], F32))
    cwT_bf = es(nc.sbuf_tensor(f"{tag}_cwTbf", [8, 128], BF16))
    b1c = es(nc.sbuf_tensor(f"{tag}_b1c", [128, NEXP * 32], F32))
    b2sb = es(nc.sbuf_tensor(f"{tag}_b2", [8, 1024], BF16))
    rwT32 = es(nc.sbuf_tensor(f"{tag}_rwT", [128, KC * 8], F32))
    rb_row = es(nc.sbuf_tensor(f"{tag}_rb", [1, 8], F32))
    lnwB = es(nc.sbuf_tensor(f"{tag}_lnwB", [128, 1024], F32))
    lnbB = es(nc.sbuf_tensor(f"{tag}_lnbB", [128, 1024], F32))
    xt = es(nc.sbuf_tensor(f"{tag}_xt", [128, 1024], F32))
    a_bf = es(nc.sbuf_tensor(f"{tag}_abf", [128, 1024], BF16))
    a32 = es(nc.sbuf_tensor(f"{tag}_a32", [128, 1024], F32))
    aTt = es(nc.sbuf_tensor(f"{tag}_aTt", [128, KC * 128], BF16))
    aTt32 = es(nc.sbuf_tensor(f"{tag}_aTt32", [128, KC * 128], F32))
    L = es(nc.sbuf_tensor(f"{tag}_L", [128, 8], F32))
    Lm = es(nc.sbuf_tensor(f"{tag}_Lm", [128, 8], F32))
    c1 = es(nc.sbuf_tensor(f"{tag}_c1", [128, 8], F32))
    cs = es(nc.sbuf_tensor(f"{tag}_cs", [128, 8], F32))
    fsts = es(nc.sbuf_tensor(f"{tag}_fst", [128, 8], F32))
    snds = es(nc.sbuf_tensor(f"{tag}_snd", [128, 8], F32))
    m1 = es(nc.sbuf_tensor(f"{tag}_m1", [128, 1], F32))
    m2 = es(nc.sbuf_tensor(f"{tag}_m2", [128, 1], F32))
    p1 = es(nc.sbuf_tensor(f"{tag}_p1", [128, 1], F32))
    p2 = es(nc.sbuf_tensor(f"{tag}_p2", [128, 1], F32))
    nrm = es(nc.sbuf_tensor(f"{tag}_nrm", [128, 1024], F32))
    xnew = es(nc.sbuf_tensor(f"{tag}_xnew", [128, 1024], F32))
    scr = {
        "mean": es(nc.sbuf_tensor(f"{tag}_mean", [128, 1], F32)),
        "t1": es(nc.sbuf_tensor(f"{tag}_t1", [128, 1], F32)),
        "rstd": es(nc.sbuf_tensor(f"{tag}_rstd", [128, 1], F32)),
        "sq": es(nc.sbuf_tensor(f"{tag}_sq", [128, 1024], F32)),
    }

    ph = [es(nc.psum_tensor(f"{tag}_ph{i}", [128, 512], F32)) for i in range(2)]
    py = [es(nc.psum_tensor(f"{tag}_py{i}", [128, 512], F32)) for i in range(2)]
    pt = [es(nc.psum_tensor(f"{tag}_pt{i}", [128, 1024], BF16)) for i in range(2)]

    aT_hbm = nc.dram_tensor(f"{tag}_aThbm", [KC * 128, T], BF16, kind="Internal").ap()
    aT_c = aT_hbm.rearrange("(c p) t -> c p t", p=128)
    acc_hbm = nc.dram_tensor(f"{tag}_acchbm", [T, 1024], F32, kind="Internal").ap()
    acc_t = acc_hbm.rearrange("(n p) d -> n p d", p=128)

    w1T_hbm = W["moe_w1T"]
    w2T_hbm = W["moe_w2T"]

    with nc.Block() as blk:
        @blk.gpsimd
        def _(g):
            sd.inc(g.dma_start(b1c[:], W["moe_b1c"]), 16)
            sd.inc(g.dma_start(b2sb[:], W["moe_b2"]), 16)
            sd.inc(g.dma_start(rwT32[:], W["moe_rwT"]), 16)
            sd.inc(g.dma_start(rb_row[:], W["moe_rb"][None, :]), 16)
            sd.inc(g.dma_start(lnwB[:], W["moe_lnwB"]), 16)
            sd.inc(g.dma_start(lnbB[:], W["moe_lnbB"]), 16)

        @blk.vector
        def _(v):
            v.wait_ge(sd.h, sd.n)

    for t in range(NT):
        so_prev = so.n
        with nc.Block() as blk:
            @blk.sync
            def _(sy):
                sd.inc(sy.dma_start(xt[:], x_t[t]), 16)

            @blk.vector
            def _(v):
                v.wait_ge(sd.h, sd.n)
                v.wait_ge(so.h, so_prev)
        emit_ln_tile(ctx, xt[:], a_bf[:], scr, out_f32=a32[:])
        with nc.Block() as blk:
            @blk.sync
            def _(sy):
                so.inc(sy.dma_start(acc_t[t], a32[:]), 16)
        dsts = [aTt[:, kc * 128:(kc + 1) * 128] for kc in range(KC)]
        emit_transpose_tile(ctx, a_bf, dsts, pt)
        with nc.Block() as blk:
            @blk.sync
            def _(sy):
                for kc in range(KC):
                    so.inc(sy.dma_start(aT_c[kc][:, t * 128:(t + 1) * 128],
                                        aTt[:, kc * 128:(kc + 1) * 128]), 16)
        st0, sv0 = st.n, sv.n
        with nc.Block() as blk:
            @blk.tensor
            def _(t_):
                for kc in range(KC):
                    if kc >= 2:
                        t_.wait_ge(sv.h, sv0 + kc - 1)
                    st.inc(t_.transpose(py[kc % 2][:, 0:128],
                                        a32[:, kc * 128:(kc + 1) * 128],
                                        ctx.ident32[:]))

            @blk.vector
            def _(v):
                for kc in range(KC):
                    v.wait_ge(st.h, st0 + kc + 1)
                    sv.inc(v.tensor_copy(aTt32[:, kc * 128:(kc + 1) * 128],
                                         py[kc % 2][:, 0:128]))
        st0 = st.n
        with nc.Block() as blk:
            @blk.tensor
            def _(t_):
                for kc in range(KC):
                    t_.matmul(ph[0][:, 0:8], aTt32[:, kc * 128:(kc + 1) * 128],
                              rwT32[:, kc * 8:(kc + 1) * 8],
                              start=(kc == 0), stop=False)
                st.inc(t_.matmul(ph[0][:, 0:8], ctx.ones32[:], rb_row[:],
                                 start=False, stop=True))
        sv0, sa0 = sv.n, sa.n
        with nc.Block() as blk:
            @blk.vector
            def _(v):
                v.wait_ge(st.h, st0 + 1)
                sv.inc(v.tensor_copy(L[:], ph[0][:, 0:8]))
                v.wait_ge(sv.h, sv0 + 1)
                sv.inc(v.reduce_max(m1[:], L[:], axis=AX.X))
                v.wait_ge(sv.h, sv0 + 2)
                sv.inc(v.tensor_single_scalar(c1[:], L[:], m1[:], ALU.is_equal))
                v.wait_ge(sv.h, sv0 + 3)
                sv.inc(v.tensor_tensor_scan(cs[:], c1[:], c1[:], 0.0, ALU.add, ALU.bypass))
                v.wait_ge(sv.h, sv0 + 4)
                sv.inc(v.tensor_single_scalar(cs[:], cs[:], 1.0, ALU.is_equal))
                v.wait_ge(sv.h, sv0 + 5)
                sv.inc(v.tensor_mul(fsts[:], cs[:], c1[:]))
                v.wait_ge(sv.h, sv0 + 6)
                sv.inc(v.scalar_tensor_tensor(Lm[:], fsts[:], -1e30, L[:],
                                              ALU.mult, ALU.add))
                v.wait_ge(sv.h, sv0 + 7)
                sv.inc(v.reduce_max(m2[:], Lm[:], axis=AX.X))
                v.wait_ge(sv.h, sv0 + 8)
                sv.inc(v.tensor_single_scalar(c1[:], Lm[:], m2[:], ALU.is_equal))
                v.wait_ge(sv.h, sv0 + 9)
                sv.inc(v.tensor_tensor_scan(cs[:], c1[:], c1[:], 0.0, ALU.add, ALU.bypass))
                v.wait_ge(sv.h, sv0 + 10)
                sv.inc(v.tensor_single_scalar(cs[:], cs[:], 1.0, ALU.is_equal))
                v.wait_ge(sv.h, sv0 + 11)
                sv.inc(v.tensor_mul(snds[:], cs[:], c1[:]))
                sv.inc(v.tensor_sub(m2[:], m1[:], m2[:]))
                v.wait_ge(sa.h, sa0 + 1)
                sv.inc(v.tensor_scalar(p2[:], p1[:], -1.0, 1.0, ALU.mult, ALU.add))
                v.wait_ge(sv.h, sv0 + 14)
                sv.inc(v.tensor_single_scalar(fsts[:], fsts[:], p1[:], ALU.mult))
                sv.inc(v.tensor_single_scalar(snds[:], snds[:], p2[:], ALU.mult))
                v.wait_ge(sv.h, sv0 + 16)
                sv.inc(v.tensor_add(cw_sb[:, t * 8:(t + 1) * 8], fsts[:], snds[:]))

            @blk.scalar
            def _(sc):
                sc.wait_ge(sv.h, sv0 + 13)
                sa.inc(sc.activation(p1[:], m2[:], AF.Sigmoid))

    pa, pb, sw1, sw2 = ctx.pa, ctx.pb, ctx.sw1, ctx.sw2
    acc_store_cnt = [so.n] * NG
    sweeps = [(e, hf) for e in range(NEXP) for hf in range(2)]
    NSW = len(sweeps)
    h_done = {}
    h_done_nc = {}
    y_done = {}
    y_mm = {}
    y_evac = {}
    gelu_done = {}
    g_store_so = {}
    pa0, pb0 = pa.n, pb.n
    sw10, sw20 = sw1.n, sw2.n

    def psem(gi):
        return pa if gi % 2 == 0 else pb

    def load_tgt(gi):
        if gi % 2 == 0:
            return pa, pa0 + 192 * (gi // 2 + 1)
        return pb, pb0 + 192 * ((gi - 1) // 2 + 1)

    def w1_tgt(si):
        return sw10 + 128 * (si + 1)

    def w2_tgt(si):
        return sw20 + 256 * (si + 1)

    sa_m1 = sa.n

    def gelu_val(gi, nc_):
        return sa_m1 + gi * 16 + nc_ + 1

    def emit_group_load(sy, gi):
        sem = psem(gi)
        g = gi % NG
        buf = aTg[gi % 2]
        for kc in range(KC):
            sem.inc(sy.dma_start(buf[:, kc * 512:(kc + 1) * 512],
                                 aT_c[kc][:, g * 512:(g + 1) * 512]), 16)
        for tt in range(4):
            sem.inc(sy.dma_start(acc_sb[gi % 2][:, tt * 1024:(tt + 1) * 1024],
                                 acc_t[g * 4 + tt]), 16)

    so_m0 = so.n
    with nc.Block() as blk:
        @blk.gpsimd
        def _(g):
            e0, hf0 = sweeps[0]
            w1e = w1T_hbm[e0].rearrange("(c p) n -> c p n", p=128)
            for kc in range(KC):
                sw1.inc(g.dma_start(w1sb[:, kc * HFF:(kc + 1) * HFF],
                                    w1e[kc][:, hf0 * HFF:(hf0 + 1) * HFF]), 16)
            w2e = w2T_hbm[e0].rearrange("(c p) n -> c p n", p=128)
            for nc_ in range(16):
                sw2.inc(g.dma_start(w2sb[:, nc_ * 1024:(nc_ + 1) * 1024],
                                    w2e[hf0 * 16 + nc_]), 16)

        @blk.sync
        def _(sy):
            sy.wait_ge(so.h, so_m0)
            emit_group_load(sy, 0)
            emit_group_load(sy, 1)

    for si, (e, hf) in enumerate(sweeps):
        with nc.Block() as blk:
            @blk.tensor
            def _(t_):
                for g in range(NG):
                    gi = si * NG + g
                    buf = aTg[gi % 2]
                    sem, val = load_tgt(gi)
                    t_.wait_ge(sem.h, val)
                    if g == 0:
                        t_.wait_ge(sw1.h, w1_tgt(si))
                    for nc_ in range(16):
                        prev = (gi, nc_ - 2) if nc_ >= 2 else (gi - 1, 14 + nc_)
                        if prev[0] >= 0:
                            t_.wait_ge(sa.h, gelu_val(*prev))
                        for kc in range(KC):
                            last = t_.matmul(ph[nc_ % 2][:],
                                             w1sb[:, kc * HFF + nc_ * 128: kc * HFF + nc_ * 128 + 128],
                                             buf[:, kc * 512:(kc + 1) * 512],
                                             start=(kc == 0), stop=(kc == KC - 1))
                        st.inc(last)
                        h_done_nc[(gi, nc_)] = st.n
                    h_done[gi] = st.n
                    if g == 0:
                        t_.wait_ge(sw2.h, w2_tgt(si))
                    t_.wait_ge(sa.h, gelu_val(gi, 15))
                    idx = 0
                    for tt in range(4):
                        for mg in range(2):
                            pj = py[idx % 2]
                            prev_evac = y_evac.get((gi, idx - 2))
                            if prev_evac is None and idx < 2:
                                prev_evac = y_evac.get((gi - 1, 6 + idx))
                            if prev_evac is not None:
                                t_.wait_ge(sv.h, prev_evac)
                            for nc_ in range(16):
                                last = t_.matmul(pj[:], hT[gi % 2][:, nc_ * 512 + tt * 128: nc_ * 512 + tt * 128 + 128],
                                                 w2sb[:, nc_ * 1024 + mg * 512: nc_ * 1024 + mg * 512 + 512],
                                                 start=(nc_ == 0), stop=(nc_ == 15))
                            st.inc(last)
                            y_mm[(gi, idx)] = st.n
                            idx += 1
                    y_done[gi] = st.n

            @blk.scalar
            def _(sc):
                for g in range(NG):
                    gi = si * NG + g
                    for nc_ in range(16):
                        sc.wait_ge(st.h, h_done_nc[(gi, nc_)])
                        if nc_ == 0 and (gi - 2) in y_done:
                            sc.wait_ge(st.h, y_done[gi - 2])
                        sa.inc(sc.activation(hT[gi % 2][:, nc_ * 512:(nc_ + 1) * 512],
                                             ph[nc_ % 2][:], AF.Gelu,
                                             bias=b1c[:, e * 32 + hf * 16 + nc_: e * 32 + hf * 16 + nc_ + 1]))
                        gelu_done[(gi, nc_)] = sa.n

            @blk.vector
            def _(v):
                for g in range(NG):
                    gi = si * NG + g
                    sem, val = load_tgt(gi)
                    v.wait_ge(sem.h, val)
                    idx = 0
                    for tt in range(4):
                        gt = g * 4 + tt
                        for mg in range(2):
                            v.wait_ge(st.h, y_mm[(gi, idx)])
                            sv.inc(v.scalar_tensor_tensor(
                                acc_sb[gi % 2][:, tt * 1024 + mg * 512: tt * 1024 + mg * 512 + 512],
                                py[idx % 2][:],
                                cw_sb[:, gt * 8 + e: gt * 8 + e + 1],
                                acc_sb[gi % 2][:, tt * 1024 + mg * 512: tt * 1024 + mg * 512 + 512],
                                ALU.mult, ALU.add))
                            y_evac[(gi, idx)] = sv.n
                            idx += 1

            @blk.gpsimd
            def _(gp):
                for g in range(NG):
                    gi = si * NG + g
                    if g == NG - 1 and si + 1 < NSW:
                        e1, hf1 = sweeps[si + 1]
                        gp.wait_ge(st.h, h_done[si * NG + NG - 1])
                        w1e = w1T_hbm[e1].rearrange("(c p) n -> c p n", p=128)
                        for kc in range(KC):
                            sw1.inc(gp.dma_start(w1sb[:, kc * HFF:(kc + 1) * HFF],
                                                 w1e[kc][:, hf1 * HFF:(hf1 + 1) * HFF]), 16)
                    for tt in range(4):
                        gp.wait_ge(sv.h, y_evac[(gi, tt * 2 + 1)])
                        so.inc(gp.dma_start(acc_t[g * 4 + tt],
                                            acc_sb[gi % 2][:, tt * 1024:(tt + 1) * 1024]), 16)
                    acc_store_cnt[g] = so.n
                    g_store_so[gi] = so.n
                if si + 1 < NSW:
                    e1, hf1 = sweeps[si + 1]
                    gp.wait_ge(st.h, y_done[si * NG + NG - 1])
                    w2e = w2T_hbm[e1].rearrange("(c p) n -> c p n", p=128)
                    for nc_ in range(16):
                        sw2.inc(gp.dma_start(w2sb[:, nc_ * 1024:(nc_ + 1) * 1024],
                                             w2e[hf1 * 16 + nc_]), 16)

            @blk.sync
            def _(sy):
                for g in range(2, NG):
                    gi = si * NG + g
                    sy.wait_ge(st.h, h_done[gi - 2])
                    sy.wait_ge(so.h, g_store_so[gi - 2])
                    emit_group_load(sy, gi)
                if si + 1 < NSW:
                    gi0 = (si + 1) * NG
                    sy.wait_ge(st.h, h_done[gi0 - 2])
                    sy.wait_ge(so.h, g_store_so[gi0 - 2])
                    emit_group_load(sy, gi0)
                    sy.wait_ge(st.h, h_done[gi0 - 1])
                    sy.wait_ge(so.h, g_store_so[gi0 - 1])
                    emit_group_load(sy, gi0 + 1)

    for t in range(NT):
        so_prev = so.n
        sdx = sd.n
        st0 = st.n
        with nc.Block() as blk:
            @blk.sync
            def _(sy):
                sy.wait_ge(so.h, acc_store_cnt[t // 4])
                sd.inc(sy.dma_start(xt[:], x_t[t]), 16)
                sd.inc(sy.dma_start(a32[:], acc_t[t]), 16)

            @blk.vector
            def _(v):
                v.wait_ge(sd.h, sd.n)
                v.wait_ge(so.h, so_prev)
                sv.inc(v.tensor_copy(a_bf[:, 0:8], cw_sb[:, t * 8:(t + 1) * 8]))

            @blk.tensor
            def _(t_):
                t_.wait_ge(sv.h, sv.n)
                st.inc(t_.transpose(pt[0][0:8, 0:128], a_bf[:, 0:8], ctx.ident[:]))

        sv0 = sv.n
        with nc.Block() as blk:
            @blk.vector
            def _(v):
                v.wait_ge(st.h, st0 + 1)
                sv.inc(v.tensor_copy(cwT_bf[:], pt[0][0:8, 0:128]))

            @blk.tensor
            def _(t_):
                t_.wait_ge(sv.h, sv0 + 1)
                for mg in range(2):
                    st.inc(t_.matmul(py[mg][:], cwT_bf[:],
                                     b2sb[:, mg * 512:(mg + 1) * 512],
                                     start=True, stop=True))

        sv0 = sv.n
        with nc.Block() as blk:
            @blk.vector
            def _(v):
                v.wait_ge(st.h, st0 + 3)
                for mg in range(2):
                    sv.inc(v.tensor_add(a32[:, mg * 512:(mg + 1) * 512],
                                        a32[:, mg * 512:(mg + 1) * 512], py[mg][:]))
        emit_ln_tile(ctx, a32[:], None, scr, eps=1e-12, out_f32=nrm[:])
        sv0 = sv.n
        with nc.Block() as blk:
            @blk.vector
            def _(v):
                sv.inc(v.tensor_mul(nrm[:], nrm[:], lnwB[:]))
                v.wait_ge(sv.h, sv0 + 1)
                sv.inc(v.tensor_add(nrm[:], nrm[:], lnbB[:]))
                v.wait_ge(sv.h, sv0 + 2)
                sv.inc(v.tensor_add(xnew[:], nrm[:], xt[:]))

            @blk.gpsimd
            def _(gp):
                gp.wait_ge(sv.h, sv.n)
                so.inc(gp.dma_start(xo_t[t], xnew[:]), 16)

    with nc.Block() as blk:
        @blk.sync
        def _(sy):
            sy.wait_ge(so.h, so.n)




def alloc_ln_bufs(ctx, es, tag, sq_alias=None):
    nc = ctx.nc
    return {
        "xt": es(nc.sbuf_tensor(f"{tag}_xt", [128, 1024], F32)),
        "a_bf": es(nc.sbuf_tensor(f"{tag}_abf", [128, 1024], BF16)),
        "aTt": es(nc.sbuf_tensor(f"{tag}_aTt", [128, KC * 128], BF16)),
        "scr": {
            "mean": es(nc.sbuf_tensor(f"{tag}_mean", [128, 1], F32)),
            "t1": es(nc.sbuf_tensor(f"{tag}_t1", [128, 1], F32)),
            "rstd": es(nc.sbuf_tensor(f"{tag}_rstd", [128, 1], F32)),
            "sq": (sq_alias if sq_alias is not None
                   else es(nc.sbuf_tensor(f"{tag}_sq", [128, 1024], F32))),
        },
    }


def emit_lnaT_phase(ctx, bufs, x_t, aT_c, pt, tiles, eps=1e-5, double_ln=False):
    nc = ctx.nc
    sd, so, sv = ctx.sd, ctx.so, ctx.sv
    xt, a_bf, aTt, scr = bufs["xt"], bufs["a_bf"], bufs["aTt"], bufs["scr"]
    for t in tiles:
        so_prev = so.n
        with nc.Block() as blk:
            @blk.sync
            def _(sy):
                sd.inc(sy.dma_start(xt[:], x_t[t]), 16)

            @blk.vector
            def _(v):
                v.wait_ge(sd.h, sd.n)
                v.wait_ge(so.h, so_prev)
        if double_ln:
            emit_ln_tile(ctx, xt[:], None, scr, eps=eps, out_f32=bufs["a_bf2"][:])
            emit_ln_tile(ctx, bufs["a_bf2"][:], a_bf[:], scr, eps=eps)
        else:
            emit_ln_tile(ctx, xt[:], a_bf[:], scr, eps=eps)
        dsts = [aTt[:, kc * 128:(kc + 1) * 128] for kc in range(KC)]
        emit_transpose_tile(ctx, a_bf, dsts, pt)
        with nc.Block() as blk:
            @blk.sync
            def _(sy):
                for kc in range(KC):
                    so.inc(sy.dma_start(aT_c[kc][:, t * 128:(t + 1) * 128],
                                        aTt[:, kc * 128:(kc + 1) * 128]), 16)



def emit_s3(ctx, es, resid_in, resid_out, W):
    nc = ctx.nc
    sd, so, st, sv, sa = ctx.sd, ctx.so, ctx.st, ctx.sv, ctx.sa
    tag = "s3"

    x_t = resid_in.rearrange("(n p) d -> n p d", p=128)
    xo_t = resid_out.rearrange("(n p) d -> n p d", p=128)

    wqkvT = es(nc.sbuf_tensor(f"{tag}_wqkvT", [128, KC * 3072], BF16))
    woutT = es(nc.sbuf_tensor(f"{tag}_woutT", [128, KC * 1024], BF16))
    bout_row = es(nc.sbuf_tensor(f"{tag}_bout", [1, 1024], BF16))
    gb_row = es(nc.sbuf_tensor(f"{tag}_gb", [1, 32], F32))
    qT_hg = es(nc.sbuf_tensor(f"{tag}_qT", [128, 2 * S], BF16))
    kT_hg = es(nc.sbuf_tensor(f"{tag}_kT", [128, 2 * S], BF16))
    v_hg = es(nc.sbuf_tensor(f"{tag}_v", [128, 16 * 256], BF16))
    dotsT = es(nc.sbuf_tensor(f"{tag}_dotsT", [128, 16 * 512], BF16))
    sq_bf = es(nc.sbuf_tensor(f"{tag}_sqbf", [128, 512], BF16))
    attnvT = es(nc.sbuf_tensor(f"{tag}_attnvT", [128, KC * S], BF16))
    rn_b = es(nc.sbuf_tensor(f"{tag}_rnb", [64, 512], F32))
    aTg = es(nc.sbuf_tensor(f"{tag}_aTg", [128, KC * 512], BF16))
    csum_r = es(nc.sbuf_tensor(f"{tag}_csum", [1, 256], F32))
    bc_row = es(nc.sbuf_tensor(f"{tag}_bcr", [1, 256], F32))
    bc_col = es(nc.sbuf_tensor(f"{tag}_bcc", [64, 4], F32))
    nr_row = es(nc.sbuf_tensor(f"{tag}_nr", [1, 512], F32))
    rn_row = es(nc.sbuf_tensor(f"{tag}_rn", [1, 512], F32))
    xnew = es(nc.sbuf_tensor(f"{tag}_xnew", [128, 1024], F32))
    lnb = alloc_ln_bufs(ctx, es, tag)
    xt = lnb["xt"]

    ps = [es(nc.psum_tensor(f"{tag}_ps{i}", [128, 512], F32)) for i in range(3)]
    pn = es(nc.psum_tensor(f"{tag}_pn", [128, 512], F32))
    pdv = [es(nc.psum_tensor(f"{tag}_pdv{i}", [128, 512], F32)) for i in range(2)]
    pt = [es(nc.psum_tensor(f"{tag}_pt{i}", [128, 1024], BF16)) for i in range(2)]
    ones_col = es(nc.sbuf_tensor(f"{tag}_onesc", [128, 1], BF16))

    aT_hbm = nc.dram_tensor(f"{tag}_aThbm", [KC * 128, T], BF16, kind="Internal").ap()
    aT_c = aT_hbm.rearrange("(c p) t -> c p t", p=128)

    wq_hbm = W["s3_wqkvT"].rearrange("(c p) n -> c p n", p=128)
    wo_hbm = W["s3_woutT"].rearrange("(c p) n -> c p n", p=128)
    with nc.Block() as blk:
        @blk.gpsimd
        def _(g):
            for kc in range(KC):
                sd.inc(g.dma_start(wqkvT[:, kc * 3072:(kc + 1) * 3072], wq_hbm[kc]), 16)
                sd.inc(g.dma_start(woutT[:, kc * 1024:(kc + 1) * 1024], wo_hbm[kc]), 16)
            sd.inc(g.dma_start(bout_row[:], W["s3_bout"][None, :]), 16)
            sd.inc(g.dma_start(gb_row[:], W["s3_gb"][None, :]), 16)
            g.memset(ones_col[:], 1.0)

        @blk.vector
        def _(v):
            v.wait_ge(sd.h, sd.n)

    for sq_ in range(NSEQ):
        tiles = range(sq_ * TPS, (sq_ + 1) * TPS)
        emit_lnaT_phase(ctx, lnb, x_t, aT_c, pt, tiles)

        for hg in range(4):
            for tg in range(4):
                sda = sd.n
                so_w = so.n
                with nc.Block() as blk:
                    @blk.sync
                    def _(sy):
                        sy.wait_ge(so.h, so_w)
                        for kc in range(KC):
                            sd.inc(sy.dma_start(aTg[:, kc * 512:(kc + 1) * 512],
                                                aT_c[kc][:, sq_ * S + tg * 512: sq_ * S + tg * 512 + 512]), 16)
                st0, sv0 = st.n, sv.n
                with nc.Block() as blk:
                    @blk.tensor
                    def _(t_):
                        t_.wait_ge(sd.h, sda + 16 * KC)
                        idx = 0
                        for (dst, oc) in [(qT_hg, hg * 2), (qT_hg, hg * 2 + 1),
                                          (kT_hg, 8 + hg * 2), (kT_hg, 8 + hg * 2 + 1)]:
                            pj = ps[idx % 2]
                            if idx >= 2:
                                t_.wait_ge(sv.h, sv0 + idx - 1)
                            for kc in range(KC):
                                last = t_.matmul(pj[:], wqkvT[:, kc * 3072 + oc * 128: kc * 3072 + oc * 128 + 128],
                                                 aTg[:, kc * 512:(kc + 1) * 512],
                                                 start=(kc == 0), stop=(kc == KC - 1))
                            st.inc(last)
                            idx += 1
                        for ti in range(4):
                            pj = pdv[ti % 2]
                            if ti >= 2:
                                t_.wait_ge(sv.h, sv0 + 4 + ti - 1)
                            for kc in range(KC):
                                last = t_.matmul(pj[:, 0:256],
                                                 aTg[:, kc * 512 + ti * 128: kc * 512 + ti * 128 + 128],
                                                 wqkvT[:, kc * 3072 + 2048 + hg * 256: kc * 3072 + 2048 + hg * 256 + 256],
                                                 start=(kc == 0), stop=(kc == KC - 1))
                            st.inc(last)

                    @blk.vector
                    def _(v):
                        idx = 0
                        for (dst, lc) in [(qT_hg, 0), (qT_hg, 1), (kT_hg, 0), (kT_hg, 1)]:
                            v.wait_ge(st.h, st0 + idx + 1)
                            sv.inc(v.tensor_copy(dst[:, lc * S + tg * 512: lc * S + tg * 512 + 512],
                                                 ps[idx % 2][:]))
                            idx += 1
                        for ti in range(4):
                            v.wait_ge(st.h, st0 + 4 + ti + 1)
                            sv.inc(v.tensor_copy(v_hg[:, (tg * 4 + ti) * 256:(tg * 4 + ti) * 256 + 256],
                                                 pdv[ti % 2][:, 0:256]))

            st0, sv0, sa0 = st.n, sv.n, sa.n
            with nc.Block() as blk:
                @blk.tensor
                def _(t_):
                    for tkc in range(16):
                        st.inc(t_.matmul(pn[0:1, 0:256], ones_col[:],
                                         v_hg[:, tkc * 256:(tkc + 1) * 256],
                                         start=(tkc == 0), stop=(tkc == 15)))

                @blk.vector
                def _(v):
                    v.wait_ge(st.h, st0 + 16)
                    sv.inc(v.tensor_copy(csum_r[:], pn[0:1, 0:256]))
                    for lh in range(4):
                        v.wait_ge(sv.h, sv0 + 1 + lh)
                        sv.inc(v.tensor_single_scalar(bc_row[0:1, lh * 64:(lh + 1) * 64],
                                                      csum_r[0:1, lh * 64:(lh + 1) * 64],
                                                      gb_row[0:1, 16 + hg * 4 + lh: 17 + hg * 4 + lh],
                                                      ALU.mult))
            st0, sv0 = st.n, sv.n
            with nc.Block() as blk:
                @blk.tensor
                def _(t_):
                    for lh in range(4):
                        if lh >= 2:
                            t_.wait_ge(sv.h, sv0 + lh - 1)
                        st.inc(t_.matmul(pdv[lh % 2][0:64, 0:1],
                                         bc_row[0:1, lh * 64:(lh + 1) * 64],
                                         ctx.ones32[0:1, 0:1], start=True, stop=True))

                @blk.vector
                def _(v):
                    for lh in range(4):
                        v.wait_ge(st.h, st0 + lh + 1)
                        sv.inc(v.tensor_copy(bc_col[:, lh:lh + 1], pdv[lh % 2][0:64, 0:1]))

            for lh in range(4):
                h = hg * 4 + lh
                lc = lh // 2
                off = (lh % 2) * 64
                for tg in range(4):
                    tq0 = tg * 512
                    st0, sv0, sa0 = st.n, sv.n, sa.n
                    with nc.Block() as blk:
                        @blk.tensor
                        def _(t_):
                            for tkc in range(16):
                                pj = ps[tkc % 3]
                                if tkc >= 3:
                                    t_.wait_ge(sv.h, sv0 + tkc - 2)
                                st.inc(t_.matmul(pj[:], kT_hg[off:off + 64, lc * S + tkc * 128: lc * S + tkc * 128 + 128],
                                                 qT_hg[off:off + 64, lc * S + tq0: lc * S + tq0 + 512],
                                                 start=True, stop=True))
                            for tkc in range(16):
                                t_.wait_ge(sa.h, sa0 + tkc + 1)
                                st.inc(t_.matmul(pn[0:1, 0:512], ones_col[:], sq_bf[:],
                                                 start=(tkc == 0), stop=(tkc == 15)))
                            t_.wait_ge(sv.h, sv0 + 19)
                            st.inc(t_.matmul(ps[0][0:64, 0:512],
                                             ctx.ones32[0:1, 0:64], rn_row[:],
                                             start=True, stop=True))
                            for tkc in range(16):
                                last = t_.matmul(pdv[0][0:64, 0:512],
                                                 v_hg[:, tkc * 256 + lh * 64: tkc * 256 + lh * 64 + 64],
                                                 dotsT[:, tkc * 512:(tkc + 1) * 512],
                                                 start=(tkc == 0), stop=(tkc == 15))
                            st.inc(last)

                        @blk.vector
                        def _(v):
                            for tkc in range(16):
                                v.wait_ge(st.h, st0 + tkc + 1)
                                sv.inc(v.tensor_copy(dotsT[:, tkc * 512:(tkc + 1) * 512],
                                                     ps[tkc % 3][:]))
                            v.wait_ge(sa.h, sa0 + 17)
                            sv.inc(v.tensor_scalar_max(nr_row[:], nr_row[:], 1e-12))
                            v.wait_ge(sv.h, sv0 + 17)
                            sv.inc(v.reciprocal(rn_row[:], nr_row[:]))
                            v.wait_ge(sv.h, sv0 + 18)
                            sv.inc(v.tensor_single_scalar(rn_row[:], rn_row[:],
                                                          gb_row[0:1, h:h + 1], ALU.mult))
                            v.wait_ge(st.h, st0 + 33)
                            sv.inc(v.tensor_copy(rn_b[:], ps[0][0:64, 0:512]))
                            v.wait_ge(st.h, st0 + 34)
                            sv.inc(v.tensor_mul(rn_b[:], pdv[0][0:64, 0:512], rn_b[:]))
                            v.wait_ge(sv.h, sv0 + 21)
                            sv.inc(v.tensor_scalar_add(
                                attnvT[(h % 2) * 64:(h % 2) * 64 + 64,
                                       (h // 2) * S + tq0:(h // 2) * S + tq0 + 512],
                                rn_b[:], bc_col[:, lh:lh + 1]))

                        @blk.scalar
                        def _(sc):
                            for tkc in range(16):
                                sc.wait_ge(sv.h, sv0 + tkc + 1)
                                if tkc >= 1:
                                    sc.wait_ge(st.h, st0 + 16 + tkc)
                                sa.inc(sc.activation(sq_bf[:], dotsT[:, tkc * 512:(tkc + 1) * 512],
                                                     AF.Square))
                            sc.wait_ge(st.h, st0 + 32)
                            sa.inc(sc.activation(nr_row[:], pn[0:1, 0:512], AF.Sqrt))

        for i in range(TPS):
            g_tile = sq_ * TPS + i
            st0, sv0 = st.n, sv.n
            sdx = sd.n
            so0 = so.n
            with nc.Block() as blk:
                @blk.sync
                def _(sy):
                    sd.inc(sy.dma_start(xt[:], x_t[g_tile]), 16)

                @blk.tensor
                def _(t_):
                    for half in range(2):
                        pj = ps[half]
                        for mc in range(KC):
                            t_.matmul(pj[:], attnvT[:, mc * S + i * 128: mc * S + i * 128 + 128],
                                      woutT[:, mc * 1024 + half * 512: mc * 1024 + half * 512 + 512],
                                      start=(mc == 0), stop=False)
                        st.inc(t_.matmul(pj[:], ctx.ones_row[:],
                                         bout_row[:, half * 512:half * 512 + 512],
                                         start=False, stop=True))

                @blk.vector
                def _(v):
                    v.wait_ge(st.h, st0 + 2)
                    v.wait_ge(sd.h, sdx + 16)
                    v.wait_ge(so.h, so0)
                    for half in range(2):
                        sv.inc(v.tensor_add(xnew[:, half * 512:half * 512 + 512],
                                            ps[half][:],
                                            xt[:, half * 512:half * 512 + 512]))

                @blk.gpsimd
                def _(gp):
                    gp.wait_ge(sv.h, sv.n)
                    so.inc(gp.dma_start(xo_t[g_tile], xnew[:]), 16)

    with nc.Block() as blk:
        @blk.sync
        def _(sy):
            sy.wait_ge(so.h, so.n)





def emit_s5(ctx, es, resid_in, resid_out, W):
    nc = ctx.nc
    sd, so, st, sv, sa = ctx.sd, ctx.so, ctx.st, ctx.sv, ctx.sa
    tag = "s5"

    x_t = resid_in.rearrange("(n p) d -> n p d", p=128)
    xo_t = resid_out.rearrange("(n p) d -> n p d", p=128)

    wqkv = es(nc.sbuf_tensor(f"{tag}_wqkv", [128, KC * 3072], BF16))
    wout = es(nc.sbuf_tensor(f"{tag}_wout", [128, KC * 1024], BF16))
    bqc = es(nc.sbuf_tensor(f"{tag}_bqc", [128, 24], F32))
    bv_row = es(nc.sbuf_tensor(f"{tag}_bvr", [1, 1024], BF16))
    bout_row = es(nc.sbuf_tensor(f"{tag}_bout", [1, 1024], BF16))
    qT_hg = es(nc.sbuf_tensor(f"{tag}_qT", [128, 2 * S], BF16))
    kT_hg = es(nc.sbuf_tensor(f"{tag}_kT", [128, 2 * S], BF16))
    v_hg = es(nc.sbuf_tensor(f"{tag}_v", [128, 16 * 256], BF16))
    PT = es(nc.sbuf_tensor(f"{tag}_PT", [128, 16 * 512], BF16))
    attnoT = es(nc.sbuf_tensor(f"{tag}_attnoT", [128, KC * S], BF16))
    f_b = es(nc.sbuf_tensor(f"{tag}_fb", [64, 512], F32))
    aTg = es(nc.sbuf_tensor(f"{tag}_aTg", [128, KC * 512], BF16))
    mxcols = es(nc.sbuf_tensor(f"{tag}_mxc", [128, 16], F32))
    mxa = es(nc.sbuf_tensor(f"{tag}_mxa", [128, 1], F32))
    mrow = es(nc.sbuf_tensor(f"{tag}_mrow", [1, 1], F32))
    negM = es(nc.sbuf_tensor(f"{tag}_negM", [128, 1], F32))
    f_row = es(nc.sbuf_tensor(f"{tag}_frow", [1, 512], F32))
    xnew = es(nc.sbuf_tensor(f"{tag}_xnew", [128, 1024], F32))
    ones_col = es(nc.sbuf_tensor(f"{tag}_onesc", [128, 1], BF16))
    lnb = alloc_ln_bufs(ctx, es, tag)
    xt = lnb["xt"]

    ps = [es(nc.psum_tensor(f"{tag}_ps{i}", [128, 512], F32)) for i in range(3)]
    pn = es(nc.psum_tensor(f"{tag}_pn", [128, 512], F32))
    pdv = [es(nc.psum_tensor(f"{tag}_pdv{i}", [128, 512], F32)) for i in range(2)]
    pt = [es(nc.psum_tensor(f"{tag}_pt{i}", [128, 1024], BF16)) for i in range(2)]

    aT_hbm = nc.dram_tensor(f"{tag}_aThbm", [KC * 128, T], BF16, kind="Internal").ap()
    aT_c = aT_hbm.rearrange("(c p) t -> c p t", p=128)

    wq_hbm = W["s5_wqkv"].rearrange("(c p) n -> c p n", p=128)
    wo_hbm = W["s5_wout"].rearrange("(c p) n -> c p n", p=128)
    with nc.Block() as blk:
        @blk.gpsimd
        def _(g):
            for kc in range(KC):
                sd.inc(g.dma_start(wqkv[:, kc * 3072:(kc + 1) * 3072], wq_hbm[kc]), 16)
                sd.inc(g.dma_start(wout[:, kc * 1024:(kc + 1) * 1024], wo_hbm[kc]), 16)
            sd.inc(g.dma_start(bqc[:], W["s5_bqc"]), 16)
            sd.inc(g.dma_start(bv_row[:], W["s5_bqkv"][None, 2048:3072]), 16)
            sd.inc(g.dma_start(bout_row[:], W["s5_bout"][None, :]), 16)
            g.memset(ones_col[:], 1.0)

        @blk.vector
        def _(v):
            v.wait_ge(sd.h, sd.n)

    for sq_ in range(NSEQ):
        tiles = range(sq_ * TPS, (sq_ + 1) * TPS)
        emit_lnaT_phase(ctx, lnb, x_t, aT_c, pt, tiles)

        for hg in range(4):
            for tg in range(4):
                sda = sd.n
                so_w = so.n
                with nc.Block() as blk:
                    @blk.sync
                    def _(sy):
                        sy.wait_ge(so.h, so_w)
                        for kc in range(KC):
                            sd.inc(sy.dma_start(aTg[:, kc * 512:(kc + 1) * 512],
                                                aT_c[kc][:, sq_ * S + tg * 512: sq_ * S + tg * 512 + 512]), 16)
                st0, sv0, sa0 = st.n, sv.n, sa.n
                with nc.Block() as blk:
                    @blk.tensor
                    def _(t_):
                        t_.wait_ge(sd.h, sda + 16 * KC)
                        idx = 0
                        for oc in [hg * 2, hg * 2 + 1, 8 + hg * 2, 8 + hg * 2 + 1]:
                            pj = ps[idx % 2]
                            if idx >= 2:
                                t_.wait_ge(sv.h, sv0 + idx - 1)
                            for kc in range(KC):
                                last = t_.matmul(pj[:], wqkv[:, kc * 3072 + oc * 128: kc * 3072 + oc * 128 + 128],
                                                 aTg[:, kc * 512:(kc + 1) * 512],
                                                 start=(kc == 0), stop=(kc == KC - 1))
                            st.inc(last)
                            idx += 1
                        for ti in range(4):
                            pj = pdv[ti % 2]
                            if ti >= 2:
                                t_.wait_ge(sv.h, sv0 + 4 + ti - 1)
                            for kc in range(KC):
                                t_.matmul(pj[:, 0:256],
                                          aTg[:, kc * 512 + ti * 128: kc * 512 + ti * 128 + 128],
                                          wqkv[:, kc * 3072 + 2048 + hg * 256: kc * 3072 + 2048 + hg * 256 + 256],
                                          start=(kc == 0), stop=False)
                            st.inc(t_.matmul(pj[:, 0:256], ctx.ones_row[:],
                                             bv_row[:, hg * 256:(hg + 1) * 256],
                                             start=False, stop=True))


                    @blk.vector
                    def _(v):
                        for idx, oc in enumerate([hg * 2, hg * 2 + 1, 8 + hg * 2, 8 + hg * 2 + 1]):
                            v.wait_ge(st.h, st0 + idx + 1)
                            if idx < 2:
                                sv.inc(v.tensor_scalar(qT_hg[:, idx * S + tg * 512: idx * S + tg * 512 + 512],
                                                       ps[idx % 2][:], bqc[:, oc:oc + 1], 0.125,
                                                       ALU.add, ALU.mult))
                            else:
                                sv.inc(v.tensor_scalar_add(kT_hg[:, (idx - 2) * S + tg * 512: (idx - 2) * S + tg * 512 + 512],
                                                           ps[idx % 2][:], bqc[:, oc:oc + 1]))
                        for ti in range(4):
                            v.wait_ge(st.h, st0 + 4 + ti + 1)
                            sv.inc(v.tensor_copy(v_hg[:, (tg * 4 + ti) * 256:(tg * 4 + ti) * 256 + 256],
                                                 pdv[ti % 2][:, 0:256]))

            for lh in range(4):
                h = hg * 4 + lh
                lc = lh // 2
                off = (lh % 2) * 64
                for tg in range(4):
                    tq0 = tg * 512
                    nk = tg * 4 + 4
                    st0, sv0 = st.n, sv.n
                    with nc.Block() as blk:
                        @blk.tensor
                        def _(t_):
                            for tkc in range(nk):
                                pj = ps[tkc % 3]
                                if tkc >= 3:
                                    t_.wait_ge(sv.h, sv0 + tkc - 2)
                                st.inc(t_.matmul(pj[:], kT_hg[off:off + 64, lc * S + tkc * 128: lc * S + tkc * 128 + 128],
                                                 qT_hg[off:off + 64, lc * S + tq0: lc * S + tq0 + 512],
                                                 start=True, stop=True))

                        @blk.vector
                        def _(v):
                            for tkc in range(nk):
                                v.wait_ge(st.h, st0 + tkc + 1)
                                sv.inc(v.reduce_max(mxcols[:, tkc:tkc + 1],
                                                    ps[tkc % 3][:], axis=AX.X))
                            v.wait_ge(sv.h, sv0 + nk)
                            sv.inc(v.reduce_max(mxa[:], mxcols[:, 0:nk], axis=AX.X))
                    st0, sv0 = st.n, sv.n
                    with nc.Block() as blk:
                        @blk.tensor
                        def _(t_):
                            st.inc(t_.transpose(pn[0:1, 0:128], mxa[:], ctx.ident32[:]))

                        @blk.vector
                        def _(v):
                            v.wait_ge(st.h, st0 + 1)
                            sv.inc(v.reduce_max(mrow[:], pn[0:1, 0:128], axis=AX.X))
                            v.wait_ge(sv.h, sv0 + 1)
                            sv.inc(v.tensor_scalar_mul(mrow[:], mrow[:], -1.0))
                    st0, sv0 = st.n, sv.n
                    with nc.Block() as blk:
                        @blk.tensor
                        def _(t_):
                            st.inc(t_.matmul(pn[:, 0:1], ctx.ones32[:], mrow[:],
                                             start=True, stop=True))

                        @blk.vector
                        def _(v):
                            v.wait_ge(st.h, st0 + 1)
                            sv.inc(v.tensor_copy(negM[:], pn[:, 0:1]))
                    st0, sv0, sa0 = st.n, sv.n, sa.n
                    with nc.Block() as blk:
                        @blk.tensor
                        def _(t_):
                            for tkc in range(nk):
                                pj = ps[tkc % 3]
                                if tkc >= 3:
                                    t_.wait_ge(sa.h, sa0 + tkc - 2)
                                st.inc(t_.matmul(pj[:], kT_hg[off:off + 64, lc * S + tkc * 128: lc * S + tkc * 128 + 128],
                                                 qT_hg[off:off + 64, lc * S + tq0: lc * S + tq0 + 512],
                                                 start=True, stop=True))

                        @blk.scalar
                        def _(sc):
                            for tkc in range(nk):
                                sc.wait_ge(st.h, st0 + tkc + 1)
                                sa.inc(sc.activation(PT[:, tkc * 512:(tkc + 1) * 512],
                                                     ps[tkc % 3][:], AF.Exp,
                                                     bias=negM[:]))

                        @blk.vector
                        def _(v):
                            for tkc in range(nk):
                                v.wait_ge(sa.h, sa0 + tkc + 1)
                                t0i = tq0 // 128
                                for j in range(4):
                                    tqt = t0i + j
                                    if tkc == tqt:
                                        sv.inc(v.tensor_mul(PT[:, tkc * 512 + j * 128: tkc * 512 + j * 128 + 128],
                                                            PT[:, tkc * 512 + j * 128: tkc * 512 + j * 128 + 128],
                                                            ctx.ltri[:]))
                                    elif tkc > tqt:
                                        sv.inc(v.memset(PT[:, tkc * 512 + j * 128: tkc * 512 + j * 128 + 128], 0.0))
                    st0, sv0 = st.n, sv.n
                    with nc.Block() as blk:
                        @blk.tensor
                        def _(t_):
                            for tkc in range(nk):
                                st.inc(t_.matmul(pn[0:1, 0:512], ones_col[:],
                                                 PT[:, tkc * 512:(tkc + 1) * 512],
                                                 start=(tkc == 0), stop=(tkc == nk - 1)))
                            for tkc in range(nk):
                                last = t_.matmul(pdv[0][0:64, 0:512],
                                                 v_hg[:, tkc * 256 + lh * 64: tkc * 256 + lh * 64 + 64],
                                                 PT[:, tkc * 512:(tkc + 1) * 512],
                                                 start=(tkc == 0), stop=(tkc == nk - 1))
                            st.inc(last)

                        @blk.vector
                        def _(v):
                            v.wait_ge(st.h, st0 + nk)
                            sv.inc(v.reciprocal(f_row[:], pn[0:1, 0:512]))
                    st0, sv0 = st.n, sv.n
                    with nc.Block() as blk:
                        @blk.tensor
                        def _(t_):
                            st.inc(t_.matmul(ps[0][0:64, 0:512],
                                             ctx.ones32[0:1, 0:64], f_row[:],
                                             start=True, stop=True))

                        @blk.vector
                        def _(v):
                            v.wait_ge(st.h, st0 + 1)
                            sv.inc(v.tensor_copy(f_b[:], ps[0][0:64, 0:512]))
                            v.wait_ge(sv.h, sv0 + 1)
                            sv.inc(v.tensor_mul(
                                attnoT[(h % 2) * 64:(h % 2) * 64 + 64,
                                       (h // 2) * S + tq0:(h // 2) * S + tq0 + 512],
                                pdv[0][0:64, 0:512], f_b[:]))

        for i in range(TPS):
            g_tile = sq_ * TPS + i
            st0, sv0 = st.n, sv.n
            sdx = sd.n
            so0 = so.n
            with nc.Block() as blk:
                @blk.sync
                def _(sy):
                    sd.inc(sy.dma_start(xt[:], x_t[g_tile]), 16)

                @blk.tensor
                def _(t_):
                    for half in range(2):
                        pj = ps[half]
                        for mc in range(KC):
                            t_.matmul(pj[:], attnoT[:, mc * S + i * 128: mc * S + i * 128 + 128],
                                      wout[:, mc * 1024 + half * 512: mc * 1024 + half * 512 + 512],
                                      start=(mc == 0), stop=False)
                        st.inc(t_.matmul(pj[:], ctx.ones_row[:],
                                         bout_row[:, half * 512:half * 512 + 512],
                                         start=False, stop=True))

                @blk.vector
                def _(v):
                    v.wait_ge(st.h, st0 + 2)
                    v.wait_ge(sd.h, sdx + 16)
                    v.wait_ge(so.h, so0)
                    for half in range(2):
                        sv.inc(v.tensor_add(xnew[:, half * 512:half * 512 + 512],
                                            ps[half][:],
                                            xt[:, half * 512:half * 512 + 512]))

                @blk.gpsimd
                def _(gp):
                    gp.wait_ge(sv.h, sv.n)
                    so.inc(gp.dma_start(xo_t[g_tile], xnew[:]), 16)

    with nc.Block() as blk:
        @blk.sync
        def _(sy):
            sy.wait_ge(so.h, so.n)





S6_DEBUG_STAGE = "all"
S6_DEBUG_DUMP = False


def emit_s6(ctx, es, resid_in, resid_out, W, tag, scratch, quant):
    nc = ctx.nc
    sd, so, st, sv, sa = ctx.sd, ctx.so, ctx.st, ctx.sv, ctx.sa
    NG = NT // 4

    x_t = resid_in.rearrange("(n p) d -> n p d", p=128)
    xo_t = resid_out.rearrange("(n p) d -> n p d", p=128)
    t1g_c = scratch["t1g"].rearrange("(c p) n -> c p n", p=128)
    t2g_c = scratch["t2g"].rearrange("(c p) n -> c p n", p=128)

    if quant and S6_DEBUG_STAGE != "compute":
        import contextlib as _ctl
        with _ctl.ExitStack() as qs:
            qe = lambda cm: qs.enter_context(cm)
            wfull = qe(nc.sbuf_tensor(f"{tag}_wfull", [128, KC * FF], F32))
            tabs = qe(nc.sbuf_tensor(f"{tag}_tabs", [128, FF], F32))
            tsgn = qe(nc.sbuf_tensor(f"{tag}_tsgn", [128, FF], BF16))
            traw = qe(nc.sbuf_tensor(f"{tag}_traw", [128, FF], BF16))
            tg_buf = qe(nc.sbuf_tensor(f"{tag}_tgbuf", [128, FF], BF16))
            sacc = qe(nc.sbuf_tensor(f"{tag}_sacc", [128, 1], F32))
            scol = qe(nc.sbuf_tensor(f"{tag}_scol", [128, 1], F32))
            srow = qe(nc.sbuf_tensor(f"{tag}_srow", [1, 4], F32))
            thrcol = qe(nc.sbuf_tensor(f"{tag}_thrcol", [128, 1], F32))
            invcol = qe(nc.sbuf_tensor(f"{tag}_invcol", [128, 1], F32))
            g1c = qe(nc.sbuf_tensor(f"{tag}_g1c", [128, 8], F32))
            c1cb = qe(nc.sbuf_tensor(f"{tag}_c1cb", [128, 8], BF16))
            g2s = qe(nc.sbuf_tensor(f"{tag}_g2s", [128, 32], F32))
            c2cb = qe(nc.sbuf_tensor(f"{tag}_c2cb", [128, 32], BF16))
            b1c_s = qe(nc.sbuf_tensor(f"{tag}_b1c", [128, 32], F32))
            b2row = qe(nc.sbuf_tensor(f"{tag}_b2row", [1, 1024], F32))
            rrow = qe(nc.sbuf_tensor(f"{tag}_rrow", [1, FF], F32))
            rcol = qe(nc.sbuf_tensor(f"{tag}_rcol", [128, 32], F32))
            onescb = qe(nc.sbuf_tensor(f"{tag}_onescb", [128, 1], BF16))
            prr = [qe(nc.psum_tensor(f"{tag}_prr{i}", [128, 512], F32)) for i in range(8)]

            with nc.Block() as blk:
                @blk.gpsimd
                def _(g):
                    sd.inc(g.dma_start(g1c[:], W["s6_g1c"]), 16)
                    sd.inc(g.dma_start(c1cb[:], W["s6_b1lc"]), 16)
                    sd.inc(g.dma_start(g2s[:], W["s6_g2c"]), 16)
                    sd.inc(g.dma_start(c2cb[:], W["s6_b2lc"]), 16)
                    sd.inc(g.dma_start(b1c_s[:], W["s6_b1c"]), 16)
                    sd.inc(g.dma_start(b2row[:], W["s6_b2"][None, :]), 16)
                    g.memset(onescb[:], 1.0)

                @blk.vector
                def _(v):
                    v.wait_ge(sd.h, sd.n)

            for wi in range(2):
                nchunks = KC if wi == 0 else 32
                width = FF if wi == 0 else 1024
                whbm = (W["s6_w1T"] if wi == 0 else W["s6_w2T"]).rearrange(
                    "(c p) n -> c p n", p=128)
                nelem = float(4096 * 1024)
                sdw = sd.n
                with nc.Block() as blk:
                    @blk.sync
                    def _(sy):
                        for mc in range(nchunks):
                            sd.inc(sy.dma_start(wfull[:, mc * width:(mc + 1) * width],
                                                whbm[mc]), 16)
                sv0, sa0 = sv.n, sa.n
                with nc.Block() as blk:
                    @blk.vector
                    def _(v):
                        v.wait_ge(sd.h, sdw + 16 * nchunks)
                        for mc in range(nchunks):
                            v.wait_ge(sv.h, sv.n)
                            if mc == 0:
                                sv.inc(v.reduce_sum(sacc[:], wfull[:, 0:width], axis=AX.X,
                                                    apply_absolute_value=True))
                            else:
                                sv.inc(v.reduce_sum(scol[:], wfull[:, mc * width:(mc + 1) * width],
                                                    axis=AX.X, apply_absolute_value=True))
                                v.wait_ge(sv.h, sv.n)
                                sv.inc(v.tensor_add(sacc[:], sacc[:], scol[:]))
                st0, sv0 = st.n, sv.n
                with nc.Block() as blk:
                    @blk.tensor
                    def _(t_):
                        st.inc(t_.transpose(prr[0][0:1, 0:128], sacc[:], ctx.ident32[:]))

                    @blk.vector
                    def _(v):
                        v.wait_ge(st.h, st0 + 1)
                        sv.inc(v.reduce_sum(srow[0:1, 0:1], prr[0][0:1, 0:128], axis=AX.X))
                        v.wait_ge(sv.h, sv0 + 1)
                        sv.inc(v.tensor_scalar(srow[0:1, 0:1], srow[0:1, 0:1],
                                               1.0 / nelem, 1e-5, ALU.mult, ALU.max))
                        v.wait_ge(sv.h, sv0 + 2)
                        sv.inc(v.tensor_scalar_mul(srow[0:1, 2:3], srow[0:1, 0:1], 0.5))
                        sv.inc(v.reciprocal(srow[0:1, 1:2], srow[0:1, 0:1]))
                st0, sv0 = st.n, sv.n
                with nc.Block() as blk:
                    @blk.tensor
                    def _(t_):
                        st.inc(t_.matmul(prr[0][:, 0:1], ctx.ones32[:], srow[0:1, 2:3],
                                         start=True, stop=True))
                        st.inc(t_.matmul(prr[1][:, 0:1], ctx.ones32[:], srow[0:1, 0:1],
                                         start=True, stop=True))

                    @blk.vector
                    def _(v):
                        v.wait_ge(st.h, st0 + 2)
                        sv.inc(v.tensor_copy(thrcol[:], prr[0][:, 0:1]))
                        sv.inc(v.tensor_copy(invcol[:], prr[1][:, 0:1]))
                if wi == 0:
                    with nc.Block() as blk:
                        @blk.sync
                        def _(sy):
                            so.inc(sy.dma_start(scratch["rbs1"], invcol[:]), 16)
                else:
                    sv0 = sv.n
                    with nc.Block() as blk:
                        @blk.vector
                        def _(v):
                            sv.inc(v.tensor_single_scalar(g2s[:], g2s[:], invcol[:], ALU.mult))

                for mc in range(nchunks):
                    sv0, sa0, st0 = sv.n, sa.n, st.n
                    so_prev_q = so.n
                    wsl = wfull[:, mc * width:(mc + 1) * width]
                    with nc.Block() as blk:
                        @blk.scalar
                        def _(sc):
                            sa.inc(sc.activation(tabs[:, 0:width], wsl, AF.Abs))
                            sa.inc(sc.activation(tsgn[:, 0:width], wsl, AF.Sign))

                        @blk.vector
                        def _(v):
                            v.wait_ge(so.h, so_prev_q)
                            v.wait_ge(sa.h, sa0 + 1)
                            sv.inc(v.tensor_single_scalar(tabs[:, 0:width], tabs[:, 0:width],
                                                          thrcol[:], ALU.is_ge))
                            v.wait_ge(sa.h, sa0 + 2)
                            v.wait_ge(sv.h, sv0 + 1)
                            sv.inc(v.tensor_mul(traw[:, 0:width], tabs[:, 0:width],
                                                tsgn[:, 0:width]))
                            v.wait_ge(sv.h, sv0 + 2)
                            if wi == 0:
                                sv.inc(v.tensor_single_scalar(tg_buf[:, 0:width],
                                                              traw[:, 0:width],
                                                              g1c[:, mc:mc + 1], ALU.mult))
                            else:
                                sv.inc(v.tensor_single_scalar(tg_buf[:, 0:width],
                                                              traw[:, 0:width],
                                                              g2s[:, mc:mc + 1], ALU.mult))
                    st0 = st.n
                    with nc.Block() as blk:
                        @blk.sync
                        def _(sy):
                            if wi == 0:
                                so.inc(sy.dma_start(t1g_c[mc], tg_buf[:, 0:width]), 16)
                            else:
                                so.inc(sy.dma_start(t2g_c[mc], tg_buf[:, 0:width]), 16)

                        @blk.tensor
                        def _(t_):
                            nsl = width // 512
                            bcol = c1cb if wi == 0 else c2cb
                            for ns in range(nsl):
                                st.inc(t_.matmul(prr[ns][0:1, 0:512],
                                                 bcol[:, mc:mc + 1],
                                                 traw[:, ns * 512:(ns + 1) * 512],
                                                 start=(mc == 0), stop=(mc == nchunks - 1)))
                            if wi == 1:
                                for oh in range(2):
                                    st.inc(t_.matmul(prr[2 + oh][0:1, 0:512],
                                                     onescb[:],
                                                     tg_buf[:, oh * 512:(oh + 1) * 512],
                                                     start=(mc == 0), stop=(mc == nchunks - 1)))
                if wi == 0:
                    st0, sv0 = st.n, sv.n
                    with nc.Block() as blk:
                        @blk.vector
                        def _(v):
                            for ns in range(8):
                                sv.inc(v.tensor_copy(rrow[0:1, ns * 512:(ns + 1) * 512],
                                                     prr[ns][0:1, 0:512]))
                    st0, sv0 = st.n, sv.n
                    with nc.Block() as blk:
                        @blk.tensor
                        def _(t_):
                            for a in range(32):
                                if a >= 8:
                                    t_.wait_ge(sv.h, sv0 + a - 7)
                                st.inc(t_.matmul(prr[a % 8][:, 0:1],
                                                 rrow[0:1, a * 128:(a + 1) * 128],
                                                 ctx.ones32[0:1, 0:1], start=True, stop=True))

                        @blk.vector
                        def _(v):
                            for a in range(32):
                                v.wait_ge(st.h, st0 + a + 1)
                                sv.inc(v.tensor_copy(rcol[:, a:a + 1], prr[a % 8][:, 0:1]))
                    sv0 = sv.n
                    with nc.Block() as blk:
                        @blk.vector
                        def _(v):
                            sv.inc(v.tensor_single_scalar(rcol[:], rcol[:], invcol[:], ALU.mult))
                            v.wait_ge(sv.h, sv0 + 1)
                            sv.inc(v.tensor_add(rcol[:], rcol[:], b1c_s[:]))

                        @blk.gpsimd
                        def _(gp):
                            gp.wait_ge(sv.h, sv.n)
                            so.inc(gp.dma_start(scratch["bcomb"], rcol[:]), 16)
                else:
                    sv0 = sv.n
                    with nc.Block() as blk:
                        @blk.vector
                        def _(v):
                            for oh in range(2):
                                sv.inc(v.tensor_copy(rrow[0:1, oh * 512:(oh + 1) * 512],
                                                     prr[oh][0:1, 0:512]))
                                sv.inc(v.tensor_copy(rrow[0:1, 1024 + oh * 512:1024 + (oh + 1) * 512],
                                                     prr[2 + oh][0:1, 0:512]))
                            v.wait_ge(sv.h, sv0 + 4)
                            sv.inc(v.tensor_single_scalar(rrow[0:1, 0:1024], rrow[0:1, 0:1024],
                                                          srow[0:1, 0:1], ALU.mult))
                            v.wait_ge(sv.h, sv0 + 5)
                            sv.inc(v.tensor_add(rrow[0:1, 0:1024], rrow[0:1, 0:1024],
                                                b2row[:]))

                        @blk.gpsimd
                        def _(gp):
                            gp.wait_ge(sv.h, sv.n)
                            so.inc(gp.dma_start(scratch["ep"], rrow[0:1, 0:1024]), 16)
                            so.inc(gp.dma_start(scratch["cp"], rrow[0:1, 1024:2048]), 16)
            with nc.Block() as blk:
                @blk.sync
                def _(sy):
                    sy.wait_ge(so.h, so.n)

    if S6_DEBUG_DUMP and quant:
        d_t1g = nc.dram_tensor("dbg_t1g", [KC * 128, FF], BF16, kind="ExternalOutput").ap()
        d_t2g = nc.dram_tensor("dbg_t2g", [32 * 128, 1024], BF16, kind="ExternalOutput").ap()
        d_bcomb = nc.dram_tensor("dbg_bcomb", [128, 32], F32, kind="ExternalOutput").ap()
        d_rbs1 = nc.dram_tensor("dbg_rbs1", [128, 1], F32, kind="ExternalOutput").ap()
        d_cp = nc.dram_tensor("dbg_cp", [1, 1024], BF16, kind="ExternalOutput").ap()
        d_ep = nc.dram_tensor("dbg_ep", [1, 1024], BF16, kind="ExternalOutput").ap()
        with nc.Block() as blk:
            @blk.sync
            def _(sy):
                so.inc(sy.dma_start(d_t1g, scratch["t1g"]), 16)
                so.inc(sy.dma_start(d_t2g, scratch["t2g"]), 16)
                so.inc(sy.dma_start(d_bcomb, scratch["bcomb"]), 16)
                so.inc(sy.dma_start(d_rbs1, scratch["rbs1"]), 16)
                so.inc(sy.dma_start(d_cp, scratch["cp"]), 16)
                so.inc(sy.dma_start(d_ep, scratch["ep"]), 16)
                sy.wait_ge(so.h, so.n)

    if S6_DEBUG_STAGE == "quant":
        return
    t1g = es(nc.sbuf_tensor(f"{tag}_t1g", [128, KC * FF], BF16))
    t2g = es(nc.sbuf_tensor(f"{tag}_t2g", [128, 32 * 1024], BF16))
    hT = es(nc.sbuf_tensor(f"{tag}_hT", [128, 32 * 512], BF16))
    aTg = es(nc.sbuf_tensor(f"{tag}_aTg", [128, KC * 512], BF16))
    sqbf = [es(nc.sbuf_tensor(f"{tag}_sqbf{i}", [128, 512], BF16)) for i in range(2)]
    bcomb = es(nc.sbuf_tensor(f"{tag}_bcomb", [128, 32], F32))
    rbs1 = es(nc.sbuf_tensor(f"{tag}_rbs1", [128, 1], F32))
    cp_bf = es(nc.sbuf_tensor(f"{tag}_cpbf", [1, 1024], BF16))
    ep_bf = es(nc.sbuf_tensor(f"{tag}_epbf", [1, 1024], BF16))
    murow = es(nc.sbuf_tensor(f"{tag}_murow", [1, 512], F32))
    ssrow = es(nc.sbuf_tensor(f"{tag}_ssrow", [1, 512], F32))
    r2row = es(nc.sbuf_tensor(f"{tag}_r2row", [1, 512], F32))
    alrow = es(nc.sbuf_tensor(f"{tag}_alrow", [1, 512], BF16))
    r2col = es(nc.sbuf_tensor(f"{tag}_r2col", [128, 4], F32))
    tmp_ps = es(nc.sbuf_tensor(f"{tag}_tmp", [128, 512], F32))
    xnew = es(nc.sbuf_tensor(f"{tag}_xnew", [128, 1024], F32))
    t1h = [es(nc.sbuf_tensor(f"{tag}_t1h{i}", [128, 512], F32)) for i in range(2)]
    onescb2 = es(nc.sbuf_tensor(f"{tag}_onescb2", [128, 1], BF16))
    lnb = alloc_ln_bufs(ctx, es, tag, sq_alias=xnew)
    lnb["a_bf2"] = es(nc.sbuf_tensor(f"{tag}_abf2", [128, 1024], F32))
    xt = lnb["xt"]

    ph = [es(nc.psum_tensor(f"{tag}_ph{i}", [128, 512], F32)) for i in range(2)]
    py = [es(nc.psum_tensor(f"{tag}_py{i}", [128, 512], F32)) for i in range(2)]
    pn1 = es(nc.psum_tensor(f"{tag}_pn1", [128, 512], F32))
    pn2 = es(nc.psum_tensor(f"{tag}_pn2", [128, 512], F32))
    pt = [es(nc.psum_tensor(f"{tag}_pt{i}", [128, 1024], BF16)) for i in range(2)]

    aT_hbm = nc.dram_tensor(f"{tag}_aThbm", [KC * 128, T], BF16, kind="Internal").ap()
    aT_c = aT_hbm.rearrange("(c p) t -> c p t", p=128)

    with nc.Block() as blk:
        @blk.gpsimd
        def _(g):
            for mc in range(KC):
                sd.inc(g.dma_start(t1g[:, mc * FF:(mc + 1) * FF], t1g_c[mc]), 16)
            for nc_ in range(32):
                sd.inc(g.dma_start(t2g[:, nc_ * 1024:(nc_ + 1) * 1024], t2g_c[nc_]), 16)
            sd.inc(g.dma_start(bcomb[:], scratch["bcomb"]), 16)
            sd.inc(g.dma_start(rbs1[:], scratch["rbs1"]), 16)
            sd.inc(g.dma_start(cp_bf[:], scratch["cp"]), 16)
            sd.inc(g.dma_start(ep_bf[:], scratch["ep"]), 16)
            g.memset(onescb2[:], 1.0)

        @blk.vector
        def _(v):
            v.wait_ge(sd.h, sd.n)

    emit_lnaT_phase(ctx, lnb, x_t, aT_c, pt, range(NT), double_ln=True)

    if S6_DEBUG_STAGE == "c_ln":
        with nc.Block() as blk:
            @blk.sync
            def _(sy):
                sy.wait_ge(so.h, so.n)
        return

    for g in range(NG if S6_DEBUG_STAGE not in ("c_h", "c_stats", "c_y") else 2):
        sda = sd.n
        so_w = so.n
        with nc.Block() as blk:
            @blk.sync
            def _(sy):
                sy.wait_ge(so.h, so_w)
                for kc in range(KC):
                    sd.inc(sy.dma_start(aTg[:, kc * 512:(kc + 1) * 512],
                                        aT_c[kc][:, g * 512:(g + 1) * 512]), 16)

        st0, sv0, sa0 = st.n, sv.n, sa.n
        h_st = {}
        stats_st = {}
        with nc.Block() as blk:
            @blk.tensor
            def _(t_):
                def emit_stats(k):
                    t_.wait_ge(sa.h, sa0 + 2 * k + 1)
                    st.inc(t_.matmul(pn1[0:1, 0:512], onescb2[:],
                                     hT[:, k * 512:(k + 1) * 512],
                                     start=(k == 0), stop=(k == 31)))
                    t_.wait_ge(sa.h, sa0 + 2 * k + 2)
                    st.inc(t_.matmul(pn2[0:1, 0:512], onescb2[:], sqbf[k % 2][:],
                                     start=(k == 0), stop=(k == 31)))
                    stats_st[k] = st.n
                t_.wait_ge(sd.h, sda + 16 * KC)
                for nc_ in range(32):
                    pj = ph[nc_ % 2]
                    if nc_ >= 2:
                        t_.wait_ge(sv.h, sv0 + nc_ - 1)
                    for kc in range(KC):
                        last = t_.matmul(pj[:], t1g[:, kc * FF + nc_ * 128: kc * FF + nc_ * 128 + 128],
                                         aTg[:, kc * 512:(kc + 1) * 512],
                                         start=(kc == 0), stop=(kc == KC - 1))
                    st.inc(last)
                    h_st[nc_] = st.n
                    if nc_ >= 2:
                        emit_stats(nc_ - 2)
                emit_stats(30)
                emit_stats(31)

            @blk.vector
            def _(v):
                for nc_ in range(32):
                    v.wait_ge(st.h, h_st[nc_])
                    if nc_ >= 2:
                        v.wait_ge(sa.h, sa0 + 2 * (nc_ - 2) + 1)
                    sv.inc(v.tensor_scalar(t1h[nc_ % 2][:], ph[nc_ % 2][:], rbs1[:],
                                           bcomb[:, nc_:nc_ + 1], ALU.mult, ALU.add))

            @blk.scalar
            def _(sc):
                for nc_ in range(32):
                    sc.wait_ge(sv.h, sv0 + nc_ + 1)
                    sa.inc(sc.activation(hT[:, nc_ * 512:(nc_ + 1) * 512],
                                         t1h[nc_ % 2][:], AF.Gelu))
                    if nc_ >= 2:
                        sc.wait_ge(st.h, stats_st[nc_ - 2])
                    sa.inc(sc.activation(sqbf[nc_ % 2][:], hT[:, nc_ * 512:(nc_ + 1) * 512],
                                         AF.Square))

        if S6_DEBUG_STAGE == "c_h":
            continue
        sv0, sa0, st0 = sv.n, sa.n, st.n
        with nc.Block() as blk:
            @blk.vector
            def _(v):
                sv.inc(v.tensor_scalar_mul(murow[:], pn1[0:1, 0:512], 1.0 / FF))
                v.wait_ge(sv.h, sv0 + 1)
                sv.inc(v.tensor_mul(ssrow[:], murow[:], murow[:]))
                v.wait_ge(sv.h, sv0 + 2)
                sv.inc(v.scalar_tensor_tensor(ssrow[:], pn2[0:1, 0:512], 1.0 / FF,
                                              ssrow[:], ALU.mult, ALU.subtract))
                v.wait_ge(sv.h, sv0 + 3)
                sv.inc(v.tensor_scalar_add(ssrow[:], ssrow[:], 1e-5))
                v.wait_ge(sa.h, sa0 + 1)
                sv.inc(v.reciprocal(r2row[:], ssrow[:]))
                v.wait_ge(sv.h, sv0 + 5)
                sv.inc(v.tensor_mul(murow[:], murow[:], r2row[:]))
                v.wait_ge(sv.h, sv0 + 6)
                sv.inc(v.tensor_scalar_mul(alrow[:], murow[:], -1.0))

            @blk.scalar
            def _(sc):
                sc.wait_ge(sv.h, sv0 + 4)
                sa.inc(sc.activation(ssrow[:], ssrow[:], AF.Sqrt))
        st0, sv0 = st.n, sv.n
        with nc.Block() as blk:
            @blk.tensor
            def _(t_):
                for ti in range(4):
                    if ti >= 2:
                        t_.wait_ge(sv.h, sv0 + ti - 1)
                    st.inc(t_.matmul(py[ti % 2][:, 0:1],
                                     r2row[0:1, ti * 128:(ti + 1) * 128],
                                     ctx.ones32[0:1, 0:1], start=True, stop=True))

            @blk.vector
            def _(v):
                for ti in range(4):
                    v.wait_ge(st.h, st0 + ti + 1)
                    sv.inc(v.tensor_copy(r2col[:, ti:ti + 1], py[ti % 2][:, 0:1]))

        if S6_DEBUG_STAGE == "c_stats":
            continue
        for ti in range(4):
            gt = g * 4 + ti
            st0, sv0 = st.n, sv.n
            sdx = sd.n
            so0 = so.n
            with nc.Block() as blk:
                @blk.sync
                def _(sy):
                    sd.inc(sy.dma_start(xt[:], x_t[gt]), 16)

                @blk.tensor
                def _(t_):
                    for mg in range(2):
                        pj = py[mg]
                        for nc_ in range(32):
                            last = t_.matmul(pj[:], hT[:, nc_ * 512 + ti * 128: nc_ * 512 + ti * 128 + 128],
                                             t2g[:, nc_ * 1024 + mg * 512: nc_ * 1024 + mg * 512 + 512],
                                             start=(nc_ == 0), stop=(nc_ == 31))
                        st.inc(last)
                        t_.matmul(pn1[:, 0:512], alrow[0:1, ti * 128:(ti + 1) * 128],
                                  cp_bf[:, mg * 512:(mg + 1) * 512],
                                  start=True, stop=False)
                        st.inc(t_.matmul(pn1[:, 0:512], ctx.ones_row[:],
                                         ep_bf[:, mg * 512:(mg + 1) * 512],
                                         start=False, stop=True))
                        t_.wait_ge(sv.h, sv0 + 2 * mg + 1)

                @blk.vector
                def _(v):
                    v.wait_ge(sd.h, sdx + 16)
                    v.wait_ge(so.h, so0)
                    for mg in range(2):
                        v.wait_ge(st.h, st0 + 2 * mg + 2)
                        if mg == 1:
                            v.wait_ge(sv.h, sv0 + 2)
                        sv.inc(v.tensor_add(tmp_ps[:], pn1[:, 0:512],
                                            xt[:, mg * 512:(mg + 1) * 512]))
                        v.wait_ge(sv.h, sv0 + 2 * mg + 1)
                        sv.inc(v.scalar_tensor_tensor(xnew[:, mg * 512:(mg + 1) * 512],
                                                      py[mg][:], r2col[:, ti:ti + 1],
                                                      tmp_ps[:], ALU.mult, ALU.add))

                @blk.gpsimd
                def _(gp):
                    gp.wait_ge(sv.h, sv.n)
                    so.inc(gp.dma_start(xo_t[gt], xnew[:]), 16)

    with nc.Block() as blk:
        @blk.sync
        def _(sy):
            sy.wait_ge(so.h, so.n)





S7_DEBUG_DUMP = False


def emit_s7(ctx, es, resid_in, resid_out, W):
    nc = ctx.nc
    sd, so, st, sv, sa = ctx.sd, ctx.so, ctx.st, ctx.sv, ctx.sa
    tag = "s7"
    dbg = {}
    if S7_DEBUG_DUMP:
        dbg["sT"] = nc.dram_tensor("dbg_sT", [128, 4 * 512], BF16, kind="ExternalOutput").ap()
        dbg["den"] = nc.dram_tensor("dbg_den", [1, 512], F32, kind="ExternalOutput").ap()
        dbg["qband"] = nc.dram_tensor("dbg_qband", [128, KC * 512], BF16, kind="ExternalOutput").ap()
        dbg["kblk"] = nc.dram_tensor("dbg_kblk", [128, KC * 128], BF16, kind="ExternalOutput").ap()
        dbg["numT"] = nc.dram_tensor("dbg_numT", [128, KC * 512], BF16, kind="ExternalOutput").ap()
        dbg["rd"] = nc.dram_tensor("dbg_rd", [1, 512], F32, kind="ExternalOutput").ap()
        dbg["v03"] = nc.dram_tensor("dbg_v03", [4 * 128, 1024], BF16, kind="ExternalOutput").ap()

    x_t = resid_in.rearrange("(n p) d -> n p d", p=128)
    xo_t = resid_out.rearrange("(n p) d -> n p d", p=128)

    wqkv = es(nc.sbuf_tensor(f"{tag}_wqkv", [128, KC * 3072], BF16))
    wgate = es(nc.sbuf_tensor(f"{tag}_wgate", [128, KC * 1024], BF16))
    wproj = es(nc.sbuf_tensor(f"{tag}_wproj", [128, KC * 1024], BF16))
    gc = es(nc.sbuf_tensor(f"{tag}_gc", [128, 8], F32))
    bc_bf = es(nc.sbuf_tensor(f"{tag}_bcbf", [128, 8], BF16))
    bq_row = es(nc.sbuf_tensor(f"{tag}_bqrow", [1, 3072], F32))
    bg_row = es(nc.sbuf_tensor(f"{tag}_bgrow", [1, 1024], F32))
    bp_row = es(nc.sbuf_tensor(f"{tag}_bprow", [1, 1024], BF16))
    rqg_row = es(nc.sbuf_tensor(f"{tag}_rqg", [1, 4096], F32))
    rqg_bf = es(nc.sbuf_tensor(f"{tag}_rqgbf", [1, 4096], BF16))
    rcols = es(nc.sbuf_tensor(f"{tag}_rcols", [128, 32], F32))
    bprojB = es(nc.sbuf_tensor(f"{tag}_bprojB", [128, 1024], F32))
    aTg = es(nc.sbuf_tensor(f"{tag}_aTg", [128, KC * 512], BF16))
    qband = es(nc.sbuf_tensor(f"{tag}_qband", [128, KC * 512], BF16))
    sT = es(nc.sbuf_tensor(f"{tag}_sT", [128, 16 * 512], BF16))
    numT = es(nc.sbuf_tensor(f"{tag}_numT", [128, KC * 512], BF16))
    kblk = [es(nc.sbuf_tensor(f"{tag}_kblk{i}", [128, KC * 128], BF16)) for i in range(2)]
    vblk = [es(nc.sbuf_tensor(f"{tag}_vblk{i}", [128, 512], BF16)) for i in range(2)]
    gateT = es(nc.sbuf_tensor(f"{tag}_gateT", [128, 512], F32))
    tmp1 = es(nc.sbuf_tensor(f"{tag}_tmp1", [128, 512], F32))
    tmp2 = es(nc.sbuf_tensor(f"{tag}_tmp2", [128, 512], F32))
    tmp3 = es(nc.sbuf_tensor(f"{tag}_tmp3", [128, 512], F32))
    kpbuf = es(nc.sbuf_tensor(f"{tag}_kpbuf", [128, 512], BF16))
    vbuf = es(nc.sbuf_tensor(f"{tag}_vbuf", [128, 1024], BF16))
    den_row = es(nc.sbuf_tensor(f"{tag}_den", [1, 512], F32))
    rd_row = es(nc.sbuf_tensor(f"{tag}_rd", [1, 512], F32))
    rdcol = es(nc.sbuf_tensor(f"{tag}_rdcol", [128, 1], F32))
    xnew = es(nc.sbuf_tensor(f"{tag}_xnew", [128, 1024], F32))
    ones_colb = es(nc.sbuf_tensor(f"{tag}_onescb", [128, 1], BF16))
    lnb = alloc_ln_bufs(ctx, es, tag, sq_alias=xnew)
    lnb["a_bf2"] = es(nc.sbuf_tensor(f"{tag}_abf2", [128, 1024], F32))
    xt = lnb["xt"]

    pfa = [es(nc.psum_tensor(f"{tag}_pf{i}", [128, 512], F32)) for i in range(6)]
    pt = [es(nc.psum_tensor(f"{tag}_pt{i}", [128, 1024], BF16)) for i in range(2)]

    aT_hbm = nc.dram_tensor(f"{tag}_aThbm", [KC * 128, T], BF16, kind="Internal").ap()
    aT_c = aT_hbm.rearrange("(c p) t -> c p t", p=128)
    kpp_hbm = nc.dram_tensor(f"{tag}_kpph", [TPS * KC * 128, 128], BF16, kind="Internal").ap()
    kpp_b = kpp_hbm.rearrange("(t c p) n -> t c p n", c=KC, p=128)
    qp_hbm = nc.dram_tensor(f"{tag}_qph", [KC * 128, S], BF16, kind="Internal").ap()
    qp_c = qp_hbm.rearrange("(c p) t -> c p t", p=128)
    v_hbm = nc.dram_tensor(f"{tag}_vh", [TPS * 128, 1024], BF16, kind="Internal").ap()
    v_tl = v_hbm.rearrange("(t p) n -> t p n", p=128)

    wq_hbm = W["s7_wqkv"].rearrange("(c p) n -> c p n", p=128)
    wg_hbm = W["s7_wgate"].rearrange("(c p) n -> c p n", p=128)
    wp_hbm = W["s7_wproj"].rearrange("(c p) n -> c p n", p=128)

    with nc.Block() as blk:
        @blk.gpsimd
        def _(g):
            for kc in range(KC):
                sd.inc(g.dma_start(wqkv[:, kc * 3072:(kc + 1) * 3072], wq_hbm[kc]), 16)
                sd.inc(g.dma_start(wgate[:, kc * 1024:(kc + 1) * 1024], wg_hbm[kc]), 16)
                sd.inc(g.dma_start(wproj[:, kc * 1024:(kc + 1) * 1024], wp_hbm[kc]), 16)
            sd.inc(g.dma_start(gc[:], W["s7_gc"]), 16)
            sd.inc(g.dma_start(bc_bf[:], W["s7_bc"]), 16)
            sd.inc(g.dma_start(bq_row[:], W["s7_bqkv"][None, :]), 16)
            sd.inc(g.dma_start(bg_row[:], W["s7_bgate"][None, :]), 16)
            sd.inc(g.dma_start(bp_row[:], W["s7_bproj"][None, :]), 16)
            g.memset(ones_colb[:], 1.0)

        @blk.vector
        def _(v):
            v.wait_ge(sd.h, sd.n)

    st0 = st.n
    with nc.Block() as blk:
        @blk.tensor
        def _(t_):
            for ns in range(6):
                for mc in range(KC):
                    last = t_.matmul(pfa[ns][0:1, 0:512], bc_bf[:, mc:mc + 1],
                                     wqkv[:, mc * 3072 + ns * 512: mc * 3072 + (ns + 1) * 512],
                                     start=(mc == 0), stop=(mc == KC - 1))
                st.inc(last)

        @blk.vector
        def _(v):
            for ns in range(6):
                v.wait_ge(st.h, st0 + ns + 1)
                sv.inc(v.tensor_copy(rqg_row[0:1, ns * 512:(ns + 1) * 512],
                                     pfa[ns][0:1, 0:512]))
    st0 = st.n
    with nc.Block() as blk:
        @blk.tensor
        def _(t_):
            for ns in range(2):
                for mc in range(KC):
                    last = t_.matmul(pfa[ns][0:1, 0:512], bc_bf[:, mc:mc + 1],
                                     wgate[:, mc * 1024 + ns * 512: mc * 1024 + (ns + 1) * 512],
                                     start=(mc == 0), stop=(mc == KC - 1))
                st.inc(last)

        @blk.vector
        def _(v):
            for ns in range(2):
                v.wait_ge(st.h, st0 + ns + 1)
                sv.inc(v.tensor_copy(rqg_row[0:1, 3072 + ns * 512: 3072 + (ns + 1) * 512],
                                     pfa[ns][0:1, 0:512]))
    sv0 = sv.n
    with nc.Block() as blk:
        @blk.vector
        def _(v):
            sv.inc(v.tensor_add(rqg_row[0:1, 0:3072], rqg_row[0:1, 0:3072], bq_row[:]))
            sv.inc(v.tensor_add(rqg_row[0:1, 3072:4096], rqg_row[0:1, 3072:4096], bg_row[:]))
            v.wait_ge(sv.h, sv0 + 2)
            sv.inc(v.tensor_copy(rqg_bf[:], rqg_row[:]))
    st0, sv0 = st.n, sv.n
    with nc.Block() as blk:
        @blk.tensor
        def _(t_):
            for a in range(32):
                if a >= 6:
                    t_.wait_ge(sv.h, sv0 + a - 5)
                st.inc(t_.matmul(pfa[a % 6][:, 0:1], rqg_row[0:1, a * 128:(a + 1) * 128],
                                 ctx.ones32[0:1, 0:1], start=True, stop=True))
            t_.wait_ge(sv.h, sv0 + 32)
            st.inc(t_.matmul(pfa[0][:, 0:512], ctx.ones_row[:], bp_row[:, 0:512],
                             start=True, stop=True))
            st.inc(t_.matmul(pfa[1][:, 0:512], ctx.ones_row[:], bp_row[:, 512:1024],
                             start=True, stop=True))

        @blk.vector
        def _(v):
            for a in range(32):
                v.wait_ge(st.h, st0 + a + 1)
                sv.inc(v.tensor_copy(rcols[:, a:a + 1], pfa[a % 6][:, 0:1]))
            v.wait_ge(st.h, st0 + 34)
            sv.inc(v.tensor_copy(bprojB[:, 0:512], pfa[0][:, 0:512]))
            sv.inc(v.tensor_copy(bprojB[:, 512:1024], pfa[1][:, 0:512]))

    sv0 = sv.n
    with nc.Block() as blk:
        @blk.vector
        def _(v):
            for kc in range(KC):
                sv.inc(v.tensor_single_scalar(wqkv[:, kc * 3072:(kc + 1) * 3072],
                                              wqkv[:, kc * 3072:(kc + 1) * 3072],
                                              gc[:, kc:kc + 1], ALU.mult))
                sv.inc(v.tensor_single_scalar(wgate[:, kc * 1024:(kc + 1) * 1024],
                                              wgate[:, kc * 1024:(kc + 1) * 1024],
                                              gc[:, kc:kc + 1], ALU.mult))

    for sq_ in range(NSEQ):
        tiles = range(sq_ * TPS, (sq_ + 1) * TPS)
        emit_lnaT_phase(ctx, lnb, x_t, aT_c, pt, tiles, double_ln=True)

        for tg in range(4):
            sda = sd.n
            so_w = so.n
            with nc.Block() as blk:
                @blk.sync
                def _(sy):
                    sy.wait_ge(so.h, so_w)
                    for kc in range(KC):
                        sd.inc(sy.dma_start(aTg[:, kc * 512:(kc + 1) * 512],
                                            aT_c[kc][:, sq_ * S + tg * 512: sq_ * S + tg * 512 + 512]), 16)
            for oc in range(KC):
                st0, sv0, sa0 = st.n, sv.n, sa.n
                so0 = so.n
                with nc.Block() as blk:
                    @blk.tensor
                    def _(t_):
                        t_.wait_ge(sd.h, sda + 16 * KC)
                        for kc in range(KC):
                            last = t_.matmul(pfa[0][:], wqkv[:, kc * 3072 + oc * 128: kc * 3072 + oc * 128 + 128],
                                             aTg[:, kc * 512:(kc + 1) * 512],
                                             start=(kc == 0), stop=(kc == KC - 1))
                        st.inc(last)

                    @blk.vector
                    def _(v):
                        v.wait_ge(st.h, st0 + 1)
                        v.wait_ge(so.h, so0)
                        sv.inc(v.tensor_scalar(tmp1[:], pfa[0][:], rcols[:, oc:oc + 1],
                                               0.0, ALU.add, ALU.min))
                        sv.inc(v.tensor_scalar(tmp2[:], pfa[0][:], rcols[:, oc:oc + 1],
                                               0.0, ALU.add, ALU.max))

                    @blk.scalar
                    def _(sc):
                        sc.wait_ge(sv.h, sv0 + 1)
                        sa.inc(sc.activation(tmp3[:], tmp1[:], AF.Exp))

                    @blk.gpsimd
                    def _(gp):
                        pass
                sv0b = sv.n
                so_b = so.n
                with nc.Block() as blk:
                    @blk.vector
                    def _(v):
                        v.wait_ge(so.h, so_b)
                        sv.inc(v.tensor_add(kpbuf[:], tmp2[:], tmp3[:]))

                    @blk.sync
                    def _(sy):
                        sy.wait_ge(sv.h, sv0b + 1)
                        so.inc(sy.dma_start(qp_c[oc][:, tg * 512:(tg + 1) * 512], kpbuf[:]), 16)
            for oc in range(KC):
                st0, sv0, sa0 = st.n, sv.n, sa.n
                so0 = so.n
                with nc.Block() as blk:
                    @blk.tensor
                    def _(t_):
                        for kc in range(KC):
                            last = t_.matmul(pfa[1][:], wgate[:, kc * 1024 + oc * 128: kc * 1024 + oc * 128 + 128],
                                             aTg[:, kc * 512:(kc + 1) * 512],
                                             start=(kc == 0), stop=(kc == KC - 1))
                        st.inc(last)
                        for kc in range(KC):
                            last = t_.matmul(pfa[2][:], wqkv[:, kc * 3072 + 1024 + oc * 128: kc * 3072 + 1024 + oc * 128 + 128],
                                             aTg[:, kc * 512:(kc + 1) * 512],
                                             start=(kc == 0), stop=(kc == KC - 1))
                        st.inc(last)

                    @blk.scalar
                    def _(sc):
                        sc.wait_ge(st.h, st0 + 1)
                        sa.inc(sc.activation(gateT[:], pfa[1][:], AF.Sigmoid,
                                             bias=rcols[:, 24 + oc:25 + oc]))

                    @blk.vector
                    def _(v):
                        v.wait_ge(st.h, st0 + 2)
                        v.wait_ge(so.h, so0)
                        sv.inc(v.tensor_scalar_add(tmp1[:], pfa[2][:], rcols[:, 8 + oc:9 + oc]))
                        v.wait_ge(sa.h, sa0 + 1)
                        v.wait_ge(sv.h, sv0 + 1)
                        sv.inc(v.tensor_mul(tmp1[:], tmp1[:], gateT[:]))
                        v.wait_ge(sv.h, sv0 + 2)
                        sv.inc(v.tensor_scalar_min(tmp2[:], tmp1[:], 0.0))
                        sv.inc(v.tensor_relu(tmp1[:], tmp1[:]))

                    @blk.scalar
                    def _(sc2):
                        sc2.wait_ge(sv.h, sv0 + 3)
                        sa.inc(sc2.activation(tmp3[:], tmp2[:], AF.Exp))
                sv0b = sv.n
                so_b = so.n
                with nc.Block() as blk:
                    @blk.vector
                    def _(v):
                        v.wait_ge(so.h, so_b)
                        sv.inc(v.tensor_add(kpbuf[:], tmp1[:], tmp3[:]))

                    @blk.sync
                    def _(sy):
                        sy.wait_ge(sv.h, sv0b + 1)
                        for j in range(4):
                            so.inc(sy.dma_start(kpp_b[tg * 4 + j, oc],
                                                kpbuf[:, j * 128:(j + 1) * 128]), 16)
            for ti in range(4):
                st0, sv0 = st.n, sv.n
                so0 = so.n
                with nc.Block() as blk:
                    @blk.tensor
                    def _(t_):
                        for vh in range(2):
                            for kc in range(KC):
                                t_.matmul(pfa[3 + vh][:],
                                          aTg[:, kc * 512 + ti * 128: kc * 512 + ti * 128 + 128],
                                          wqkv[:, kc * 3072 + 2048 + vh * 512: kc * 3072 + 2048 + vh * 512 + 512],
                                          start=(kc == 0), stop=False)
                            st.inc(t_.matmul(pfa[3 + vh][:], ctx.ones_row[:],
                                             rqg_bf[:, 2048 + vh * 512: 2048 + (vh + 1) * 512],
                                             start=False, stop=True))

                    @blk.vector
                    def _(v):
                        v.wait_ge(st.h, st0 + 2)
                        v.wait_ge(so.h, so0)
                        for vh in range(2):
                            sv.inc(v.tensor_copy(vbuf[:, vh * 512:(vh + 1) * 512],
                                                 pfa[3 + vh][:]))

                    @blk.sync
                    def _(sy):
                        sy.wait_ge(sv.h, sv.n)
                        so.inc(sy.dma_start(v_tl[tg * 4 + ti], vbuf[:]), 16)

        for tg in range(4):
            tq0 = tg * 512
            nk = tg * 4 + 4
            sdq = sd.n
            so_w = so.n
            with nc.Block() as blk:
                @blk.sync
                def _(sy):
                    sy.wait_ge(so.h, so_w)
                    for kc in range(KC):
                        sd.inc(sy.dma_start(qband[:, kc * 512:(kc + 1) * 512],
                                            qp_c[kc][:, tq0:tq0 + 512]), 16)
            st0, sv0 = st.n, sv.n
            sdk0 = sd.n
            copy_done = {}
            kload_tgt = {}
            with nc.Block() as blk:
                @blk.vector
                def _(v):
                    for tkc in range(nk):
                        v.wait_ge(st.h, st0 + tkc + 1)
                        sv.inc(v.tensor_copy(sT[:, tkc * 512:(tkc + 1) * 512],
                                             pfa[tkc % 2][:]))
                        copy_done[tkc] = sv.n
                        for j in range(4):
                            tqt = tg * 4 + j
                            if tkc == tqt:
                                v.wait_ge(sv.h, sv.n)
                                sv.inc(v.tensor_mul(sT[:, tkc * 512 + j * 128: tkc * 512 + j * 128 + 128],
                                                    sT[:, tkc * 512 + j * 128: tkc * 512 + j * 128 + 128],
                                                    ctx.ltri[:]))
                            elif tkc > tqt:
                                v.wait_ge(sv.h, sv.n)
                                sv.inc(v.memset(sT[:, tkc * 512 + j * 128: tkc * 512 + j * 128 + 128], 0.0))

                @blk.sync
                def _(sy):
                    sy.wait_ge(sd.h, sdq + 16 * KC)
                    for tkc in range(nk):
                        sem = ctx.pa if tkc % 2 == 0 else ctx.pb
                        if tkc >= 2:
                            sy.wait_ge(st.h, st0 + tkc - 1)
                        sem.inc(sy.dma_start(kblk[tkc % 2][:].rearrange("p (c n) -> p c n", n=128),
                                             kpp_b[tkc].rearrange("c p n -> p c n")), 16)
                        kload_tgt[tkc] = sem.n

                @blk.tensor
                def _(t_):
                    for tkc in range(nk):
                        sem = ctx.pa if tkc % 2 == 0 else ctx.pb
                        t_.wait_ge(sem.h, kload_tgt[tkc])
                        if tkc >= 2:
                            t_.wait_ge(sv.h, copy_done[tkc - 2])
                        for kc in range(KC):
                            last = t_.matmul(pfa[tkc % 2][:],
                                             kblk[tkc % 2][:, kc * 128:(kc + 1) * 128],
                                             qband[:, kc * 512:(kc + 1) * 512],
                                             start=(kc == 0), stop=(kc == KC - 1))
                        st.inc(last)

            st0, sv0, sa0 = st.n, sv.n, sa.n
            with nc.Block() as blk:
                @blk.tensor
                def _(t_):
                    for tkc in range(nk):
                        last = t_.matmul(pfa[2][0:1, 0:512], ones_colb[:],
                                         sT[:, tkc * 512:(tkc + 1) * 512],
                                         start=(tkc == 0), stop=(tkc == nk - 1))
                    st.inc(last)

                @blk.vector
                def _(v):
                    v.wait_ge(st.h, st0 + 1)
                    sv.inc(v.tensor_scalar_add(den_row[:], pfa[2][0:1, 0:512], 1e-6))
                    v.wait_ge(sv.h, sv0 + 1)
                    sv.inc(v.reciprocal(rd_row[:], den_row[:]))
            if S7_DEBUG_DUMP and sq_ == 3 and tg == 0:
                with nc.Block() as blk:
                    @blk.sync
                    def _(sy):
                        so.inc(sy.dma_start(dbg["v03"], v_hbm[0:4 * 128, :]), 16)
                        so.inc(sy.dma_start(dbg["sT"], sT[:, 0:4 * 512]), 16)
                        so.inc(sy.dma_start(dbg["den"], den_row[:]), 16)
                        so.inc(sy.dma_start(dbg["qband"], qband[:]), 16)
                        so.inc(sy.dma_start(dbg["kblk"], kblk[(nk - 1) % 2][:]), 16)
                        so.inc(sy.dma_start(dbg["rd"], rd_row[:]), 16)
                        sy.wait_ge(so.h, so.n)
            for half in range(2):
                st0, sv0 = st.n, sv.n
                sdv0 = sd.n
                vload_tgt = {}
                with nc.Block() as blk:
                    @blk.sync
                    def _(sy):
                        for tkc in range(nk):
                            sem = ctx.pa if tkc % 2 == 0 else ctx.pb
                            if tkc >= 2:
                                sy.wait_ge(st.h, st0 + tkc - 1)
                            sem.inc(sy.dma_start(vblk[tkc % 2][:],
                                                 v_tl[tkc][:, half * 512:(half + 1) * 512]), 16)
                            vload_tgt[tkc] = sem.n

                    @blk.tensor
                    def _(t_):
                        for tkc in range(nk):
                            sem = ctx.pa if tkc % 2 == 0 else ctx.pb
                            t_.wait_ge(sem.h, vload_tgt[tkc])
                            for dc in range(4):
                                last = t_.matmul(pfa[dc][:],
                                                 vblk[tkc % 2][:, dc * 128:(dc + 1) * 128],
                                                 sT[:, tkc * 512:(tkc + 1) * 512],
                                                 start=(tkc == 0), stop=(tkc == nk - 1))
                            st.inc(last)

                    @blk.vector
                    def _(v):
                        v.wait_ge(st.h, st0 + nk)
                        for dc in range(4):
                            sv.inc(v.tensor_copy(numT[:, (half * 4 + dc) * 512:(half * 4 + dc) * 512 + 512],
                                                 pfa[dc][:]))
            if S7_DEBUG_DUMP and sq_ == 3 and tg == 0:
                with nc.Block() as blk:
                    @blk.sync
                    def _(sy):
                        so.inc(sy.dma_start(dbg["numT"], numT[:]), 16)
                        sy.wait_ge(so.h, so.n)
            for ti in range(4):
                gt = sq_ * TPS + tg * 4 + ti
                st0, sv0 = st.n, sv.n
                sdx = sd.n
                so0 = so.n
                with nc.Block() as blk:
                    @blk.sync
                    def _(sy):
                        sd.inc(sy.dma_start(xt[:], x_t[gt]), 16)

                    @blk.tensor
                    def _(t_):
                        st.inc(t_.matmul(pfa[2][:, 0:1],
                                         rd_row[0:1, ti * 128:(ti + 1) * 128],
                                         ctx.ones32[0:1, 0:1], start=True, stop=True))
                        for half in range(2):
                            pj = pfa[half]
                            for mc in range(KC):
                                last = t_.matmul(pj[:], numT[:, mc * 512 + ti * 128: mc * 512 + ti * 128 + 128],
                                                 wproj[:, mc * 1024 + half * 512: mc * 1024 + half * 512 + 512],
                                                 start=(mc == 0), stop=(mc == KC - 1))
                            st.inc(last)

                    @blk.vector
                    def _(v):
                        v.wait_ge(st.h, st0 + 1)
                        sv.inc(v.tensor_copy(rdcol[:], pfa[2][:, 0:1]))
                        v.wait_ge(st.h, st0 + 3)
                        v.wait_ge(sd.h, sdx + 16)
                        v.wait_ge(so.h, so0)
                        v.wait_ge(sv.h, sv0 + 1)
                        for half in range(2):
                            sv.inc(v.scalar_tensor_tensor(
                                xnew[:, half * 512:half * 512 + 512],
                                pfa[half][:], rdcol[:],
                                xt[:, half * 512:half * 512 + 512],
                                ALU.mult, ALU.add))
                        v.wait_ge(sv.h, sv0 + 3)
                        sv.inc(v.tensor_add(xnew[:], xnew[:], bprojB[:]))

                    @blk.gpsimd
                    def _(gp):
                        gp.wait_ge(sv.h, sv.n)
                        so.inc(gp.dma_start(xo_t[gt], xnew[:]), 16)

    with nc.Block() as blk:
        @blk.sync
        def _(sy):
            sy.wait_ge(so.h, so.n)



def prep_weights(inputs):
    W = {}
    W["s1_wuv"] = inputs["L0_attn_a_W_uv"]
    W["s1_buv"] = inputs["L0_attn_a_b_uv"]
    W["s1_gbT"] = np.stack([inputs["L0_attn_a_gamma"][0], inputs["L0_attn_a_beta"][0],
                            inputs["L0_attn_a_gamma"][1], inputs["L0_attn_a_beta"][1]],
                           axis=1)
    W["s1_wout"] = inputs["L0_attn_a_W_out"]
    W["s1_bout"] = inputs["L0_attn_a_b_out"]
    W["s1_buvc"] = np.asarray(inputs["L0_attn_a_b_uv"]).reshape(33, 128).T
    W["moe_w1T"] = np.asarray(inputs["L0_ffn_w1"]).transpose(0, 2, 1)
    W["moe_w2T"] = np.asarray(inputs["L0_ffn_w2"]).transpose(0, 2, 1)
    W["moe_b1c"] = np.asarray(inputs["L0_ffn_b1"]).reshape(8, 32, 128).transpose(2, 0, 1).reshape(128, 256)
    W["moe_b2"] = np.asarray(inputs["L0_ffn_b2"])
    W["moe_rwT"] = np.asarray(inputs["L0_ffn_router_w"]).T.reshape(8, 128, 8).transpose(1, 0, 2).reshape(128, 64)
    W["moe_rb"] = np.asarray(inputs["L0_ffn_router_b"])
    W["moe_lnwB"] = np.broadcast_to(np.asarray(inputs["L0_ffn_ln_w"])[None, :], (128, 1024))
    W["moe_lnbB"] = np.broadcast_to(np.asarray(inputs["L0_ffn_ln_b"])[None, :], (128, 1024))
    W["s3_wqkvT"] = np.asarray(inputs["L0_attn_b_w_qkv"]).T
    W["s3_woutT"] = np.asarray(inputs["L0_attn_b_w_out"]).T
    W["s3_bout"] = np.asarray(inputs["L0_attn_b_b_out"])
    W["s3_gb"] = np.concatenate([np.asarray(inputs["L0_attn_b_norm_g"]).reshape(16),
                                 np.asarray(inputs["L0_attn_b_norm_b"]).reshape(16)])
    W["s5_wqkv"] = np.asarray(inputs["L1_attn_a_w_qkv"])
    W["s5_bqkv"] = np.asarray(inputs["L1_attn_a_b_qkv"])
    W["s5_bqc"] = np.asarray(inputs["L1_attn_a_b_qkv"]).reshape(24, 128).T
    W["s5_wout"] = np.asarray(inputs["L1_attn_a_w_out"])
    W["s5_bout"] = np.asarray(inputs["L1_attn_a_b_out"])
    W["s6_w1T"] = np.asarray(inputs["L1_ffn_w1"]).T
    W["s6_w2T"] = np.asarray(inputs["L1_ffn_w2"]).T
    W["s6_g1c"] = np.asarray(inputs["L1_ffn_ln1_g"]).reshape(8, 128).T
    W["s6_b1lc"] = np.asarray(inputs["L1_ffn_ln1_b"]).reshape(8, 128).T
    W["s6_g2c"] = np.asarray(inputs["L1_ffn_ln2_g"]).reshape(32, 128).T
    W["s6_b2lc"] = np.asarray(inputs["L1_ffn_ln2_b"]).reshape(32, 128).T
    W["s6_b1c"] = np.asarray(inputs["L1_ffn_b1"]).reshape(32, 128).T
    W["s6_b2"] = np.asarray(inputs["L1_ffn_b2"])
    W["s7_wqkv"] = np.asarray(inputs["L1_attn_b_w_qkv"])
    W["s7_wgate"] = np.asarray(inputs["L1_attn_b_w_gate"])
    W["s7_wproj"] = np.asarray(inputs["L1_attn_b_w_proj"])
    W["s7_bqkv"] = np.asarray(inputs["L1_attn_b_b_qkv"])
    W["s7_bgate"] = np.asarray(inputs["L1_attn_b_b_gate"])
    W["s7_bproj"] = np.asarray(inputs["L1_attn_b_b_proj"])
    W["s7_gc"] = np.asarray(inputs["L1_attn_b_ln_g"]).reshape(8, 128).T
    W["s7_bc"] = np.asarray(inputs["L1_attn_b_ln_b"]).reshape(8, 128).T
    return {k: np.ascontiguousarray(np.asarray(v, dtype=np.float32)) for k, v in W.items()}


def declare_weight_inputs(nc, Wnp):
    return {k: nc.dram_tensor("in_" + k, list(v.shape), F32, kind="ExternalInput").ap()
            for k, v in Wnp.items()}


PHASES_ALL = ["s1", "moe1", "s3", "moe2", "s5", "s6a", "s7", "s6b"]

EMITTERS = {}


def _register():
    EMITTERS["s1"] = lambda ctx, es, ri, ro, W: emit_s1(ctx, es, ri, ro, W)
    EMITTERS["moe1"] = lambda ctx, es, ri, ro, W: emit_moe(ctx, es, ri, ro, W, "m1")
    EMITTERS["moe2"] = lambda ctx, es, ri, ro, W: emit_moe(ctx, es, ri, ro, W, "m2")
    EMITTERS["s3"] = lambda ctx, es, ri, ro, W: emit_s3(ctx, es, ri, ro, W)
    EMITTERS["s5"] = lambda ctx, es, ri, ro, W: emit_s5(ctx, es, ri, ro, W)
    EMITTERS["s7"] = lambda ctx, es, ri, ro, W: emit_s7(ctx, es, ri, ro, W)


_register()


def build_program(Wnp, phases):
    nc = bass.Bass("TRN2", target_bir_lowering=False)
    x_in = nc.dram_tensor("x_in", [T, D], F32, kind="ExternalInput").ap()
    y_out = nc.dram_tensor("y_out", [T, D], F32, kind="ExternalOutput").ap()
    resid_mid = None
    if len(phases) > 1:
        resid_mid = nc.dram_tensor("resid_mid", [T, D], F32, kind="Internal").ap()
    Waps = declare_weight_inputs(nc, Wnp)
    ctx = Ctx(nc)
    s6_scratch = None
    if any(p in ("s6a", "s6b") for p in phases):
        s6_scratch = {
            "t1g": nc.dram_tensor("s6_t1g", [KC * 128, FF], BF16, kind="Internal").ap(),
            "t2g": nc.dram_tensor("s6_t2g", [32 * 128, 1024], BF16, kind="Internal").ap(),
            "bcomb": nc.dram_tensor("s6_bcombh", [128, 32], F32, kind="Internal").ap(),
            "rbs1": nc.dram_tensor("s6_rbs1h", [128, 1], F32, kind="Internal").ap(),
            "cp": nc.dram_tensor("s6_cph", [1, 1024], BF16, kind="Internal").ap(),
            "ep": nc.dram_tensor("s6_eph", [1, 1024], BF16, kind="Internal").ap(),
        }
    first_s6 = True
    with contextlib.ExitStack() as stack:
        ctx.alloc_consts(lambda cm: stack.enter_context(cm))
        for pi, phname in enumerate(phases):
            rin = x_in if pi == 0 else resid_mid
            rout = y_out if pi == len(phases) - 1 else resid_mid
            with contextlib.ExitStack() as ps:
                pes = lambda cm: ps.enter_context(cm)
                if phname in ("s6a", "s6b"):
                    emit_s6(ctx, pes, rin, rout, Waps, phname, s6_scratch,
                            quant=first_s6)
                    first_s6 = False
                else:
                    EMITTERS[phname](ctx, pes, rin, rout, Waps)
    return nc


def kernel(**inputs):
    Wnp = prep_weights(inputs)
    x = np.ascontiguousarray(np.asarray(inputs["L0_attn_a_inputs"], dtype=np.float32))
    nc = build_program(Wnp, PHASES_ALL)
    in_maps = []
    for c in range(NCORES):
        m = {"x_in": np.ascontiguousarray(x[c * NSEQ:(c + 1) * NSEQ].reshape(T, D))}
        m.update({"in_" + k: v for k, v in Wnp.items()})
        in_maps.append(m)
    res = run_bass_kernel_spmd(nc, in_maps, core_ids=list(range(NCORES)))
    out = np.concatenate([np.asarray(r["y_out"]).reshape(NSEQ, S, D)
                          for r in res.results], axis=0)
    return out
